# version 11
# baseline (speedup 1.0000x reference)
import contextlib
import numpy as np
import ml_dtypes
import concourse.bass as bass
import concourse.mybir as mybir
from concourse.bass_utils import run_bass_kernel_spmd

F32 = mybir.dt.float32
BF16 = mybir.dt.bfloat16
AF = mybir.ActivationFunctionType
ALU = mybir.AluOpType
AX = mybir.AxisListType

T = 4096
Q = 128
NCH = T // Q
NH, HD, NST = 16, 128, 128
DM, DI = 1024, 2048
EPS = 1e-5
NEG = -1e30

ENGINES = ("tensor", "vector", "scalar", "gpsimd", "sync")
FCONST_VALS = [EPS, 1.0 + EPS, EPS * EPS, -0.5, -1.0, 1.0, 0.5, 2.0]


class Stream:
    def __init__(self, k, name):
        self.sem = k.es.enter_context(k.nc.semaphore(name))
        self.count = 0

    def tok(self):
        return (self.sem, self.count)


class Eng:
    def __init__(self, k, name):
        self.name = name
        self.ops = []
        self.sem = k.es.enter_context(k.nc.semaphore("prog_" + name))
        self.count = 0
        self.waited = {}


class K:

    def __init__(self, nc, es):
        self.nc = nc
        self.es = es
        self.eng = {n: Eng(self, n) for n in ENGINES}

    def stream(self, name):
        return Stream(self, name)

    def _filt(self, e, deps, keep_self=False):
        out = []
        for (sem, val) in deps:
            if (sem is e.sem and not keep_self) or val <= 0:
                continue
            if e.waited.get(id(sem), -1) >= val:
                continue
            e.waited[id(sem)] = val
            out.append((sem, val))
        return out

    def op(self, engine, fn, deps=(), inc=True):
        e = self.eng[engine]
        mydeps = self._filt(e, deps)
        if inc:
            e.count += 1
        e.ops.append((fn, mydeps, e.sem if inc else None))
        return (e.sem, e.count)

    def tok(self, engine):
        e = self.eng[engine]
        return (e.sem, e.count)

    def dma(self, queue, out, in_, stream, deps=(), **kw):
        stream.count += 16
        e = self.eng[queue]
        mydeps = self._filt(e, deps, keep_self=True)

        def fn(eng):
            return eng.dma_start(out, in_, **kw)

        e.ops.append((fn, mydeps, ("dma", stream.sem)))
        return (stream.sem, stream.count)

    def raw16(self, queue, fn, stream, deps=()):
        stream.count += 16
        e = self.eng[queue]
        mydeps = self._filt(e, deps, keep_self=True)
        e.ops.append((fn, mydeps, ("dma", stream.sem)))
        return (stream.sem, stream.count)

    def fence(self, streams=()):
        toks = [(e.sem, e.count) for e in self.eng.values()]
        toks += [s.tok() for s in streams]
        for name in ENGINES:
            self.op(name, lambda e: e.nop(), deps=toks, inc=True)

    def emit(self):
        block = self.es.enter_context(self.nc.Block())
        for name in ENGINES:
            e = self.eng[name]
            selfsync = name in ("vector", "scalar", "gpsimd")

            def make(e, selfsync):
                def body(eng):
                    n = 0
                    for fn, deps, inc in e.ops:
                        if selfsync and n > 0 and inc is not None and not isinstance(inc, tuple):
                            eng.wait_ge(e.sem, n)
                        for (sem, val) in deps:
                            eng.wait_ge(sem, val)
                        ins = fn(eng)
                        if inc is None:
                            continue
                        if isinstance(ins, (list, tuple)):
                            ins = ins[-1]
                        if isinstance(inc, tuple):
                            ins.then_inc(inc[1], 16)
                        else:
                            ins.then_inc(inc, 1)
                            n += 1
                return body

            getattr(block, name)(make(e, selfsync))


class Ctx:
    pass


def build_nc(mode="fused", stage="full", dbg=False):
    nc = bass.Bass("TRN2", target_bir_lowering=False, detect_race_conditions=False)
    es = contextlib.ExitStack()
    k = K(nc, es)
    c = Ctx()
    c.nc, c.es, c.k, c.mode, c.stage = nc, es, k, mode, stage

    def din(name, shape, dt=F32):
        setattr(c, name, nc.dram_tensor(name, shape, dt, kind="ExternalInput").ap())

    def dout(name, shape, dt=F32):
        setattr(c, name, nc.dram_tensor(name, shape, dt, kind="ExternalOutput").ap())

    def dint(name, shape, dt):
        kind = "ExternalOutput" if dbg else "Internal"
        setattr(c, name, nc.dram_tensor(name, shape, dt, kind=kind).ap())

    din("x_main", [T, DM]); din("x_halo", [16, DM])
    din("mask_col", [128, 1]); din("smask", [128, 8])
    din("w_in", [DM, 4368]); din("w_out", [DI, DM])
    din("lnw_col", [128, 8]); din("lnb_col", [128, 8])
    din("conv_diag", [4, 18, 128, 128]); din("convb_col", [128, 18])
    din("dtb_col", [16, 1]); din("alog_col", [16, 1]); din("dhead_row", [1, 16])
    din("rmsw_col", [128, 16])
    din("router_wT", [DM, 16]); din("corrb_row", [1, 16])
    din("gate_w", [16, DM, 256]); din("up_w", [16, DM, 256]); din("down_w", [16, 256, DM])
    din("sgT", [DM, DI]); din("suT", [DM, DI]); din("sdT", [DI, DM])
    din("i128", [128, 128]); din("ones_row", [1, 512]); din("fconst", [128, 8])
    din("triu", [128, 128]); din("tricap", [128, 512]); din("negd", [16, 2048])
    dout("out_t", [T, DM])
    if dbg:
        dout("dbg0", [128, 160])
        dout("dbg1", [128, 512])
    if mode == "twoA":
        dout("s_half_out", [DI, NST])
    if mode == "twoB":
        din("s_init_in", [DI, NST])
    dint("xn_tm_d", [T, DM], BF16)
    dint("xn2_tm_d", [T, DM], BF16)
    dint("xn2h_tm_d", [16, DM], BF16)
    dint("xbc_fm_d", [2304, T], BF16)
    dint("z_fm_d", [DI, T], BF16)
    dint("h_fm_d", [16 * 256 + DI, T], BF16)
    dint("y_tm_d", [T, DI], BF16)
    dint("g_tm_d", [T, DM], F32)
    dint("sflat_d", [NCH, 2048], F32)
    dint("sfm_d", [NCH, 17, 128], F32)
    dint("sq_d", [NCH, 16], F32)
    dint("comb_fm_d", [16, T], F32)
    if mode == "fused":
        dint("coll_in_d", [DI, NST], F32)
        dint("coll_out_d", [8 * DI, NST], F32)

    SB = {}

    def st(name, shape, dt):
        SB[name] = es.enter_context(nc.sbuf_tensor(name, shape, dt))
        setattr(c, name, SB[name])
        return SB[name]

    st("S_t", [128, DI], F32)
    st("stm_all", [128, NCH * 16], F32)
    st("dttm_all", [128, NCH * 16], F32)
    st("dectm_all", [128, NCH * 16], F32)
    st("comb", [128, NCH * 16], F32)
    st("msq_all", [128, NCH], F32)
    st("rho", [128, NCH], F32); st("nmr", [128, NCH], F32)
    st("rho2", [128, NCH], F32); st("nmr2", [128, NCH], F32)
    st("rstd_all", [128, NCH], F32)
    st("hstat", [16, 8], F32)
    st("i128_t", [128, 128], F32); st("i128b_t", [128, 128], BF16)
    st("ones1", [1, 128], F32)
    st("triu_t", [128, 128], F32); st("tricap_t", [128, 512], F32)
    st("dhi", [128, NH * 128], BF16)
    st("sfm", [17, 128], F32)
    st("lnw_t", [128, 8], F32); st("lnb_t", [128, 8], F32)
    st("maskc", [128, 1], F32); st("smask_t", [128, 8], F32)
    st("dtb_t", [16, 1], F32); st("alog_t", [16, 1], F32); st("acol", [16, 1], F32)
    st("dhead_t", [1, 16], F32); st("corrb_t", [1, 16], F32)
    st("rmsw_t", [128, 16], F32)
    st("convb_t", [128, 18], F32)
    st("fconst_t", [128, 8], F32)
    st("tiny", [128, 64], F32)
    st("tiny16", [16, 128], F32)
    st("AR1", [128, 32768], BF16)
    st("AR2", [128, 32768], BF16)
    st("AR3", [128, 26944], BF16)

    c.ps = [es.enter_context(nc.psum_tensor(f"ps{i}", [128, 512], F32)) for i in range(8)]

    c.s_in = k.stream("s_in")
    c.s_w = k.stream("s_w")
    c.s_w2 = k.stream("s_w2")
    c.s_w3 = k.stream("s_w3")
    c.s_sp = k.stream("s_sp")
    c.s_tr = k.stream("s_tr")
    c.s_g2 = k.stream("s_g2")
    c.s_out = k.stream("s_out")
    c.s_bnc = k.stream("s_bnc")
    c.s_coll = k.stream("s_coll")
    c.s_const = k.stream("s_const")
    c.s_x = [k.stream("s_x0"), k.stream("s_x1")]
    c.s_xh = [k.stream("s_xh0"), k.stream("s_xh1")]
    c.s_pa = [k.stream("s_pa0"), k.stream("s_pa1")]
    c.s_pb = [k.stream("s_pb0"), k.stream("s_pb1")]
    c.s_cr = [k.stream("s_cr0"), k.stream("s_cr1")]
    c.s_we = [k.stream("s_we0"), k.stream("s_we1")]
    c.s_hl = [k.stream("s_hl0"), k.stream("s_hl1")]
    c.s_xl = [k.stream("s_xl0"), k.stream("s_xl1")]
    c.s_gl = [k.stream("s_gl0"), k.stream("s_gl1")]

    def bf(arena, off, n, parts=128):
        return SB[arena][0:parts, off:off + n]

    def f32(arena, off, n, parts=128):
        return SB[arena][0:parts, off:off + 2 * n].bitcast(F32)

    c.bf, c.f32 = bf, f32

    k.dma("sync", SB["ones1"][:], c.ones_row[0:1, 0:128], c.s_const)
    for dst, src in [("i128_t", "i128"), ("triu_t", "triu"),
                     ("tricap_t", "tricap"), ("lnw_t", "lnw_col"), ("lnb_t", "lnb_col"),
                     ("maskc", "mask_col"), ("smask_t", "smask"), ("dtb_t", "dtb_col"),
                     ("alog_t", "alog_col"), ("dhead_t", "dhead_row"), ("corrb_t", "corrb_row"),
                     ("rmsw_t", "rmsw_col"), ("convb_t", "convb_col")]:
        k.dma("sync", SB[dst][:], getattr(c, src), c.s_const)
    k.dma("sync", SB["sfm"][16:17, :], c.ones_row[0:1, 0:128], c.s_in)
    k.dma("sync", SB["fconst_t"][:], c.fconst, c.s_const)
    k.dma("gpsimd", SB["i128b_t"][:], c.i128, c.s_w)
    c.consts = c.s_const.tok()
    for i, v in enumerate(FCONST_VALS):
        nc.const_aps.register(F32, float(v), SB["fconst_t"][:, i:i + 1])
        if float(np.float32(v)) != float(v):
            nc.const_aps.register(F32, float(np.float32(v)), SB["fconst_t"][:, i:i + 1])
    k.fence([c.s_const])
    c.constsw = c.s_w.tok()
    k.op("scalar", lambda e: e.activation(c.acol[:], c.alog_t[:], AF.Exp), deps=[c.consts])
    k.op("vector", lambda e: e.tensor_scalar(c.acol[:], c.acol[:], -1.0, None, ALU.mult),
         deps=[k.tok("scalar")])
    c.acol_tok = k.tok("vector")
    t0 = k.op("tensor", lambda e: e.matmul(c.ps[0][:, 0:16], c.ones1[:],
                                           c.dhead_t[:], start=True, stop=True), deps=[c.consts])
    t1 = k.op("scalar", lambda e: e.activation(c.tiny[:, 0:16], c.ps[0][:, 0:16], AF.Identity),
              deps=[t0])
    for h in range(NH):
        k.op("vector", lambda e, h=h: e.tensor_scalar(
            c.dhi[:, h * 128:(h + 1) * 128], c.i128b_t[:],
            c.tiny[:, h:h + 1], None, ALU.mult), deps=[t1, c.constsw])
    c.dhi_tok = k.tok("vector")

    phase_p0(c)
    if stage != "p0":
        phase_p1a(c)
    if stage not in ("p0", "p1a"):
        phase_p1b(c)
    if mode != "twoA" and stage not in ("p0", "p1a", "p1b"):
        phase_p3z(c)
        if stage != "p3z":
            phase_p3(c)
        if stage not in ("p3z", "p3sh", "p3e", "p3dn"):
            phase_p2a(c)
        if stage == "full":
            phase_p2b(c)
    finish(c)
    k.emit()
    return nc, es, c


def phase_p0(c):
    k = c.k
    XT = [c.f32("AR1", b * 2048, DM) for b in range(4)]
    SQ = c.f32("AR2", 0, DM)
    NT = [c.bf("AR3", b * 1024, DM) for b in range(2)]
    NT2 = [c.bf("AR3", 2048 + b * 1024, DM) for b in range(2)]
    SUMS = c.f32("AR2", 2048, NCH)
    SSQ = c.f32("AR2", 2176, NCH)
    MU = c.f32("AR2", 2304, NCH)
    ld_toks = [None] * NCH
    red_toks = [None] * NCH
    sq_toks = [None] * NCH
    zz = k.op("vector", lambda e: e.memset(SSQ[:], 0.0))
    for i in range(NCH):
        b = i % 4
        dep = [red_toks[i - 4], sq_toks[i - 4]] if i >= 4 else []
        ld = k.dma("sync", XT[b][:], c.x_main[i * Q:(i + 1) * Q, :],
                   (c.s_x + c.s_pa)[b], deps=dep)
        ld_toks[i] = ld
        red_toks[i] = k.op("vector", lambda e, i=i, b=b: e.tensor_reduce(
            SUMS[:, i:i + 1], XT[b][:], AX.X, ALU.add), deps=[ld])
        sq_toks[i] = k.op("scalar", lambda e, i=i, b=b: e.activation(
            SQ[:], XT[b][:], AF.Square, accum_out=SSQ[:, i:i + 1]), deps=[ld, zz])
    tA = k.op("vector", lambda e: e.tensor_scalar(MU[:], SUMS[:], 1.0 / DM, None, ALU.mult),
              deps=[red_toks[-1]])
    tB = k.op("vector", lambda e: e.tensor_scalar(SSQ[:], SSQ[:], 1.0 / DM, None, ALU.mult),
              deps=[sq_toks[-1]])
    tC = k.op("vector", lambda e: e.tensor_tensor(c.rho[:], MU[:], MU[:], ALU.mult))
    tD = k.op("vector", lambda e: e.tensor_tensor(SSQ[:], SSQ[:], c.rho[:], ALU.subtract))
    tE = k.op("vector", lambda e: e.tensor_scalar(c.rho[:], SSQ[:], EPS, None, ALU.add))
    tF = k.op("vector", lambda e: e.tensor_scalar(c.rho2[:], SSQ[:], 1.0 + EPS, EPS * EPS,
                                                  ALU.mult, ALU.add))
    l1 = k.op("scalar", lambda e: e.activation(c.rho[:], c.rho[:], AF.Ln), deps=[tE])
    l2 = k.op("scalar", lambda e: e.activation(c.rho2[:], c.rho2[:], AF.Ln), deps=[tF])
    h1 = k.op("vector", lambda e: e.tensor_scalar(c.rho[:], c.rho[:], -0.5, None, ALU.mult),
              deps=[l1])
    h2 = k.op("vector", lambda e: e.tensor_scalar(c.rho2[:], c.rho2[:], -0.5, None, ALU.mult),
              deps=[l2])
    e1 = k.op("scalar", lambda e: e.activation(c.rho[:], c.rho[:], AF.Exp), deps=[h1])
    e2 = k.op("scalar", lambda e: e.activation(c.rho2[:], c.rho2[:], AF.Exp), deps=[h2])
    n1 = k.op("vector", lambda e: e.tensor_tensor(c.nmr[:], MU[:], c.rho[:], ALU.mult),
              deps=[e1])
    n1b = k.op("vector", lambda e: e.tensor_scalar(c.nmr[:], c.nmr[:], -1.0, None, ALU.mult))
    n2 = k.op("vector", lambda e: e.tensor_tensor(c.nmr2[:], MU[:], c.rho2[:], ALU.mult),
              deps=[e2])
    n2b = k.op("vector", lambda e: e.tensor_scalar(c.nmr2[:], c.nmr2[:], -1.0, None, ALU.mult))
    sp_toks = {}
    for i in range(NCH):
        b = i % 4
        nb = i % 2
        dep = [sp_toks.get(i - 4, None)] if i >= 4 else []
        dep = [t for t in dep if t]
        ld = k.dma("sync", XT[b][:], c.x_main[i * Q:(i + 1) * Q, :],
                   (c.s_x + c.s_pa)[b],
                   deps=dep + ([sq_toks[-1], red_toks[-1]] if i < 4 else []))
        wdep = [c.s_sp.tok()] if i >= 2 else []
        x1 = k.op("scalar", lambda e, i=i, b=b, nb=nb: e.activation(
            NT[nb][:], XT[b][:], AF.Identity, scale=c.rho[:, i:i + 1], bias=c.nmr[:, i:i + 1]),
            deps=[ld, n1b] + wdep)
        k.dma("scalar", c.xn_tm_d[i * Q:(i + 1) * Q, :], NT[nb][:], c.s_sp, deps=[x1])
        x2 = k.op("scalar", lambda e, i=i, b=b, nb=nb: e.activation(
            NT2[nb][:], XT[b][:], AF.Identity, scale=c.rho2[:, i:i + 1], bias=c.nmr2[:, i:i + 1]),
            deps=[n2b])
        sp_toks[i] = k.dma("scalar", c.xn2_tm_d[i * Q:(i + 1) * Q, :], NT2[nb][:], c.s_sp,
                           deps=[x2])
    HX = c.f32("AR2", 4096, DM, 16)
    HS = c.f32("AR2", 8192, DM, 16)
    HB = c.bf("AR2", 12288, DM, 16)
    hld = k.dma("sync", HX[0:16, :], c.x_halo, c.s_in)
    h_z = k.op("vector", lambda e: e.memset(c.hstat[:, 1:2], 0.0))
    h_r = k.op("vector", lambda e: e.tensor_reduce(c.hstat[:, 0:1], HX[0:16, :], AX.X, ALU.add),
               deps=[hld])
    h_q = k.op("scalar", lambda e: e.activation(HS[0:16, :], HX[0:16, :], AF.Square,
                                                accum_out=c.hstat[:, 1:2]), deps=[hld, h_z])
    k.op("vector", lambda e: e.tensor_scalar(c.hstat[:, 2:3], c.hstat[:, 0:1], 1.0 / DM, None,
                                             ALU.mult), deps=[h_r])
    k.op("vector", lambda e: e.tensor_tensor(c.hstat[:, 3:4], c.hstat[:, 2:3], c.hstat[:, 2:3],
                                             ALU.mult))
    k.op("vector", lambda e: e.tensor_scalar(c.hstat[:, 4:5], c.hstat[:, 1:2], 1.0 / DM, None,
                                             ALU.mult), deps=[h_q])
    hv = k.op("vector", lambda e: e.tensor_tensor(c.hstat[:, 4:5], c.hstat[:, 4:5],
                                                  c.hstat[:, 3:4], ALU.subtract))
    he = k.op("vector", lambda e: e.tensor_scalar(c.hstat[:, 5:6], c.hstat[:, 4:5],
                                                  1.0 + EPS, EPS * EPS, ALU.mult, ALU.add),
              deps=[hv])
    hl = k.op("scalar", lambda e: e.activation(c.hstat[:, 5:6], c.hstat[:, 5:6], AF.Ln),
              deps=[he])
    hh = k.op("vector", lambda e: e.tensor_scalar(c.hstat[:, 5:6], c.hstat[:, 5:6],
                                                  -0.5, None, ALU.mult), deps=[hl])
    hr = k.op("scalar", lambda e: e.activation(c.hstat[:, 5:6], c.hstat[:, 5:6], AF.Exp),
              deps=[hh])
    hm = k.op("vector", lambda e: e.tensor_tensor(c.hstat[:, 6:7], c.hstat[:, 5:6],
                                                  c.hstat[:, 2:3], ALU.mult), deps=[hr])
    hm2 = k.op("vector", lambda e: e.tensor_scalar(c.hstat[:, 6:7], c.hstat[:, 6:7], -1.0, None,
                                                   ALU.mult))
    hx2 = k.op("scalar", lambda e: e.activation(HB[0:16, :], HX[0:16, :], AF.Identity,
                                                scale=c.hstat[:, 5:6], bias=c.hstat[:, 6:7]),
               deps=[hm2])
    k.dma("sync", c.xn2h_tm_d, HB[0:16, :], c.s_bnc, deps=[hx2])
    if c.stage == "p0" and hasattr(c, "dbg0"):
        k.dma("sync", c.dbg0[:, 0:32], c.rho[:], c.s_bnc, deps=[k.tok("vector"), k.tok("scalar")])
        k.dma("sync", c.dbg0[:, 32:64], c.nmr[:], c.s_bnc)
        k.dma("sync", c.dbg0[:, 64:96], c.rho2[:], c.s_bnc)
        k.dma("sync", c.dbg0[:, 96:128], c.nmr2[:], c.s_bnc)
        k.dma("sync", c.dbg0[:, 128:160], c.nmr[:], c.s_bnc)


def phase_p1a(c):
    k = c.k
    k.fence([c.s_sp, c.s_in, c.s_w])
    XF = [c.bf("AR1", j * 4096, 4096) for j in range(8)]
    WIN = [c.bf("AR2", j * 2320, 2320) for j in range(8)]
    DIAG = [c.bf("AR2", 18560 + t * 2304, 2304) for t in range(4)]
    XH = [c.bf("AR2", 27776 + j * 16, 16) for j in range(8)]
    XBCR = c.bf("AR3", 0, 18 * 516)
    CSTG = c.bf("AR3", 9288, 18 * 512)
    DTR = c.f32("AR3", 18504, T, 16)
    xbv = c.AR3[:, 0:9288].rearrange("p (j w) -> p j w", w=516)

    xf_tok = []
    for j in range(8):
        xf_tok.append(k.dma("sync", XF[j][:], c.xn2_tm_d[0:T, j * 128:(j + 1) * 128],
                            c.s_tr, transpose=True))
        k.dma("gpsimd", WIN[j][:], c.w_in[j * 128:(j + 1) * 128, 2048:4368], c.s_w)
        k.dma("sync", XH[j][:], c.xn2h_tm_d[0:16, j * 128:(j + 1) * 128], c.s_tr, transpose=True)
    for t in range(4):
        k.dma("gpsimd", DIAG[t][:], c.conv_diag[t].rearrange("j p w -> p j w"), c.s_w)
    wtok = c.s_w.tok()
    trtok = c.s_tr.tok()
    aff = []
    for j in range(8):
        aff.append(k.op("scalar", lambda e, j=j: e.activation(
            XF[j][:], XF[j][:], AF.Identity, scale=c.lnw_t[:, j:j + 1], bias=c.lnb_t[:, j:j + 1]),
            deps=[trtok]))
    aff_tok = k.tok("scalar")
    for j in range(8):
        k.op("scalar", lambda e, j=j: e.activation(
            XH[j][:], XH[j][:], AF.Identity, scale=c.lnw_t[:, j:j + 1], bias=c.lnb_t[:, j:j + 1]))
    hf_tok = k.tok("scalar")
    hev = None
    for j in range(18):
        mm = None
        for kc in range(8):
            mm = k.op("tensor", lambda e, j=j, kc=kc: e.matmul(
                c.ps[4][:, 0:16], WIN[kc][:, j * 128:(j + 1) * 128], XH[kc][:],
                start=(kc == 0), stop=(kc == 7)),
                deps=[hf_tok, wtok] + ([hev] if hev else []), inc=(kc == 7))
        hev = k.op("scalar", lambda e, j=j: e.activation(
            xbv[:, j, 0:3],
            c.ps[4][:, 0:3], AF.Identity, scale=c.maskc[:, 0:1]), deps=[mm, c.consts])
    halo_tok = k.tok("scalar")

    NG = 8
    GW = 512
    jw = [(j * 128, 128) for j in range(18)] + [(2304, 16)]
    conv_done = [None] * NG
    evac_done = [None] * NG
    halo_cp = [None] * NG
    spl_prev = None
    for g in range(NG):
        ev = []
        for jj, (joff, jwid) in enumerate(jw):
            bank = c.ps[jj % 2]
            bdep = [evac_done[g - 1]] if (g >= 1 and jj < 2) else []
            mm = None
            for kc in range(8):
                mm = k.op("tensor", lambda e, jj=jj, joff=joff, jwid=jwid, kc=kc, g=g, bank=bank:
                          e.matmul(bank[0:jwid if jwid < 128 else 128, 0:GW],
                                   WIN[kc][:, joff:joff + jwid],
                                   XF[kc][:, g * GW:(g + 1) * GW],
                                   start=(kc == 0), stop=(kc == 7)),
                          deps=([aff_tok, wtok, halo_tok] + bdep +
                                ([ev[jj - 2]] if jj >= 2 else [])),
                          inc=(kc == 7))
            if jj < 18:
                edep = [mm] + ([halo_cp[g - 1]] if g >= 1 else [])
                e1 = k.op("scalar", lambda e, jj=jj, g=g, bank=bank: e.activation(
                    xbv[:, jj, 3:3 + GW], bank[:, 0:GW], AF.Identity), deps=edep)
            else:
                e1 = k.op("scalar", lambda e, g=g, bank=bank: e.activation(
                    DTR[0:16, g * GW:(g + 1) * GW], bank[0:16, 0:GW], AF.Identity), deps=[mm])
            ev.append(e1)
        evac_done[g] = ev[-1]
        wv_tok = k.tok("scalar")
        cv = []
        for j in range(18):
            bank = c.ps[2 + (j % 2)]
            bdep = [cv[j - 2]] if j >= 2 else ([conv_done[g - 1]] if g >= 1 else [])
            mm = None
            for t in range(4):
                mm = k.op("tensor", lambda e, j=j, t=t, g=g, bank=bank: e.matmul(
                    bank[:, 0:GW], DIAG[t][:, j * 128:(j + 1) * 128],
                    xbv[:, j, t:t + GW], start=(t == 0), stop=(t == 3)),
                    deps=[ev[j], wtok] + bdep, inc=(t == 3))
            cv.append(k.op("scalar", lambda e, j=j, g=g, bank=bank: e.activation(
                CSTG[:, j * 512:j * 512 + GW], bank[:, 0:GW], AF.Silu,
                bias=c.convb_t[:, j:j + 1]),
                deps=[mm] + ([spl_prev] if (j == 0 and spl_prev) else [])))
        conv_done[g] = cv[-1]
        halo_cp[g] = k.op("vector", lambda e: e.tensor_copy(
            xbv[:, :, 0:3], xbv[:, :, 512:515]), deps=[k.tok("tensor")])
        spl_prev = k.dma("scalar",
                         c.xbc_fm_d.rearrange("(j p) t -> p j t", p=128)[:, :, g * GW:(g + 1) * GW],
                         c.AR3[:, 9288:9288 + 9216].rearrange("p (j w) -> p j w", w=512),
                         c.s_sp, deps=[conv_done[g]])


def phase_p1b(c):
    k = c.k
    k.fence([c.s_sp, c.s_w, c.s_tr])
    DTR = c.f32("AR3", 18504, T, 16)
    DTF = c.f32("AR1", 0, T, 16)
    SPT = c.f32("AR1", 8192, T, 16)
    AFM = c.f32("AR1", 8192, T, 16)
    XHT = [c.bf("AR1", 16384 + b * 2048, 2048) for b in range(2)]
    XDC = [c.bf("AR1", 20480 + b * 2048, 2048) for b in range(2)]
    BTM = [c.bf("AR1", 24576 + b * 128, 128) for b in range(2)]
    ATM = c.f32("AR1", 24832, 512)
    SFM = [c.f32("AR2", 0, 128, 17), c.f32("AR2", 256, 128, 17)]
    EQROW = c.f32("AR2", 512, 16, 1)
    EQR = c.f32("AR2", 576, 16)
    DECF = [c.f32("AR2", 640 + b * 256, 128, 16) for b in range(2)]
    STMP = c.f32("AR1", 26112, 2048)

    t1 = k.op("vector", lambda e: e.tensor_scalar(DTF[0:16, :], DTR[0:16, :],
                                                  c.dtb_t[0:16, 0:1], None, ALU.add),
              deps=[c.consts])
    t2 = k.op("scalar", lambda e: e.activation(SPT[0:16, :], DTF[0:16, :], AF.Abs), deps=[t1])
    t2b = k.op("vector", lambda e: e.tensor_scalar(SPT[0:16, :], SPT[0:16, :], -1.0, None,
                                                   ALU.mult), deps=[t2])
    t3 = k.op("scalar", lambda e: e.activation(SPT[0:16, :], SPT[0:16, :], AF.Exp), deps=[t2b])
    t3b = k.op("vector", lambda e: e.tensor_scalar(SPT[0:16, :], SPT[0:16, :], 1.0, None,
                                                   ALU.add), deps=[t3])
    t4 = k.op("scalar", lambda e: e.activation(SPT[0:16, :], SPT[0:16, :], AF.Ln), deps=[t3b])
    t5 = k.op("scalar", lambda e: e.activation(DTF[0:16, :], DTF[0:16, :], AF.Relu), deps=[t2])
    t6 = k.op("vector", lambda e: e.tensor_tensor(DTF[0:16, :], DTF[0:16, :], SPT[0:16, :],
                                                  ALU.add), deps=[t4, t5])
    t7 = k.op("vector", lambda e: e.tensor_scalar(AFM[0:16, :], DTF[0:16, :],
                                                  c.acol[0:16, 0:1], None, ALU.mult),
              deps=[t6, c.acol_tok])
    ev_hist = {}
    for cc in range(NCH):
        bk = c.ps[cc % 2]
        pdep = [ev_hist.get(cc - 2)] if cc >= 2 else []
        pdep = [t for t in pdep if t]
        m1 = k.op("tensor", lambda e, cc=cc, bk=bk: e.matmul(
            bk[:, 0:16], DTF[0:16, cc * Q:(cc + 1) * Q], c.i128_t[0:16, 0:16],
            start=True, stop=True), deps=[t6, c.consts] + pdep)
        s1 = k.op("scalar", lambda e, cc=cc, bk=bk: e.activation(
            c.dttm_all[:, cc * 16:(cc + 1) * 16], bk[:, 0:16], AF.Identity), deps=[m1])
        m2 = k.op("tensor", lambda e, cc=cc, bk=bk: e.matmul(
            bk[:, 16:32], AFM[0:16, cc * Q:(cc + 1) * Q], c.i128_t[0:16, 0:16],
            start=True, stop=True), deps=[t7] + pdep)
        s2 = k.op("scalar", lambda e, cc=cc, bk=bk: e.activation(
            ATM[:, cc * 16:(cc + 1) * 16], bk[:, 16:32], AF.Identity), deps=[m2])
        m3 = k.op("tensor", lambda e, cc=cc, bk=bk: e.matmul(
            bk[:, 32:48], c.triu_t[:], ATM[:, cc * 16:(cc + 1) * 16],
            start=True, stop=True), deps=[s2])
        s3 = k.op("scalar", lambda e, cc=cc, bk=bk: e.activation(
            c.stm_all[:, cc * 16:(cc + 1) * 16], bk[:, 32:48], AF.Identity), deps=[m3])
        ev_hist[cc] = s3
    for b_ in range(2):
        k.dma("sync", SFM[b_][16:17, :], c.ones_row[0:1, 0:128], c.s_const)
    ones_tok = c.s_const.tok()
    sp_hist = {}
    dec_hist = {}
    for cc in range(NCH):
        b = cc % 2
        bk = c.ps[2 + (cc % 2)]
        sfm = SFM[b]
        pdep = [c.s_sp.tok()] if cc >= 2 else []
        m4 = k.op("tensor", lambda e, cc=cc, bk=bk: e.matmul(
            bk[:, 0:128][0:16, :], c.stm_all[:, cc * 16:(cc + 1) * 16], c.i128_t[:],
            start=True, stop=True), deps=[ev_hist[cc]] + ([dec_hist.get(cc - 2)] if cc >= 2 else []))
        s4 = k.op("scalar", lambda e, cc=cc, bk=bk, sfm=sfm: e.activation(
            sfm[0:16, :], bk[:, 0:128][0:16, :], AF.Identity),
            deps=[m4, c.consts] + pdep)
        sp1 = k.dma("scalar", c.sfm_d[cc], sfm[0:17, :], c.s_sp, deps=[s4, ones_tok])
        sp2 = k.dma("scalar", c.sflat_d[cc:cc + 1, :].rearrange("a (h t) -> (a h) t", h=16),
                    sfm[0:16, :], c.s_sp, deps=[s4])
        sp_hist[cc] = k.dma("scalar", c.sq_d[cc:cc + 1, :].rearrange("a h -> h a"),
                            sfm[0:16, 127:128], c.s_sp, deps=[s4])
        d0 = k.op("vector", lambda e, b=b, sfm=sfm: e.tensor_scalar(
            DECF[b][0:16, :], sfm[0:16, :], -1.0, None, ALU.mult),
            deps=[s4] + ([dec_hist.get(cc - 2)] if cc >= 2 else []))
        d0b = k.op("vector", lambda e, b=b, sfm=sfm: e.tensor_scalar(
            DECF[b][0:16, :], DECF[b][0:16, :], sfm[0:16, 127:128], None, ALU.add))
        d1 = k.op("scalar", lambda e, b=b: e.activation(DECF[b][0:16, :], DECF[b][0:16, :],
                                                        AF.Exp), deps=[d0b])
        m6 = k.op("tensor", lambda e, b=b, bk=bk: e.matmul(
            bk[:, 128:144], DECF[b][0:16, :], c.i128_t[0:16, 0:16], start=True, stop=True),
            deps=[d1])
        d2 = k.op("scalar", lambda e, cc=cc, bk=bk: e.activation(
            c.dectm_all[:, cc * 16:(cc + 1) * 16], bk[:, 128:144], AF.Identity), deps=[m6])
        dec_hist[cc] = k.op("vector", lambda e, cc=cc: e.tensor_tensor(
            c.dectm_all[:, cc * 16:(cc + 1) * 16], c.dectm_all[:, cc * 16:(cc + 1) * 16],
            c.dttm_all[:, cc * 16:(cc + 1) * 16], ALU.mult), deps=[d2])
    dec_all = k.tok("vector")
    sq_all = c.s_sp.tok()
    sz = k.op("vector", lambda e: e.memset(c.S_t[:], 0.0))
    upd_tok = sz
    add_hist = {}
    xd_hist = {}
    pe_hist = {}
    q2_hist = {}
    xput = [None, None]
    for cc in range(NCH):
        b = cc % 2
        ldep = [t for t in (pe_hist.get(cc - 2), xd_hist.get(cc - 2)) if t]
        k.dma("sync", XHT[b][:], c.xbc_fm_d[0:2048, cc * Q:(cc + 1) * Q], c.s_xh[b],
              transpose=True, deps=ldep)
        xput[b] = k.dma("sync", BTM[b][:], c.xbc_fm_d[2048:2176, cc * Q:(cc + 1) * Q],
                        c.s_xh[b], transpose=True)
        xd = k.op("gpsimd", lambda e, cc=cc, b=b: e.tensor_tensor(
            XDC[b][:].rearrange("p (h w) -> p h w", w=128),
            XHT[b][:].rearrange("p (h w) -> p h w", w=128),
            c.dectm_all[:, cc * 16:(cc + 1) * 16].unsqueeze(2).broadcast_to((128, 16, 128)),
            ALU.mult), deps=[dec_all, xput[b]] + ([pe_hist.get(cc - 2)] if cc >= 2 else []))
        xd_hist[cc] = xd
        lq = k.dma("sync", EQROW[0:1, :], c.sq_d[cc:cc + 1, :], c.s_bnc,
                   deps=[sq_all] + [t for t in (pe_hist.get(cc - 1),) if t])
        q1 = k.op("scalar", lambda e: e.activation(EQROW[0:1, :], EQROW[0:1, :], AF.Exp),
                  deps=[lq] + [t for t in (q2_hist.get(cc - 1),) if t])
        m5 = k.op("tensor", lambda e: e.matmul(c.ps[6][:, 0:16], c.ones1[:],
                                               EQROW[0:1, :], start=True, stop=True),
                  deps=[q1] + [t for t in (q2_hist.get(cc - 1),) if t])
        q2 = k.op("scalar", lambda e: e.activation(EQR[:, 0:16], c.ps[6][:, 0:16],
                                                   AF.Identity), deps=[m5])
        q2_hist[cc] = q2
        mm = None
        for h in range(16):
            pt = c.ps[2 + h // 4]
            mm = k.op("tensor", lambda e, h=h, pt=pt, b=b: e.matmul(
                pt[:, (h % 4) * 128:(h % 4 + 1) * 128], BTM[b][:],
                XDC[b][:, h * 128:(h + 1) * 128], start=True, stop=True),
                deps=[xd, xput[b]] + ([add_hist.get(cc - 1)] if cc >= 1 else []),
                inc=(h % 4 == 3))
        mu0 = k.op("vector", lambda e, cc=cc: e.tensor_tensor(
            STMP[:, 0:1024].rearrange("p (h w) -> p h w", w=128),
            c.S_t[:, 0:1024].rearrange("p (h w) -> p h w", w=128),
            EQR[:, 0:8].unsqueeze(2).broadcast_to((128, 8, 128)), ALU.mult),
            deps=[q2, upd_tok])
        mu1 = k.op("gpsimd", lambda e, cc=cc: e.tensor_tensor(
            STMP[:, 1024:2048].rearrange("p (h w) -> p h w", w=128),
            c.S_t[:, 1024:2048].rearrange("p (h w) -> p h w", w=128),
            EQR[:, 8:16].unsqueeze(2).broadcast_to((128, 8, 128)), ALU.mult),
            deps=[q2, upd_tok])
        aa = None
        for j in range(4):
            aa = k.op("vector", lambda e, j=j: e.tensor_tensor(
                c.S_t[:, j * 512:(j + 1) * 512], STMP[:, j * 512:(j + 1) * 512],
                c.ps[2 + j][:], ALU.add), deps=[mu0 if j < 2 else mu1, mm])
        add_hist[cc] = aa
        upd_tok = aa
        pe_hist[cc] = k.tok("tensor")
    if c.mode == "twoA":
        k.dma("sync", c.s_half_out.rearrange("(h n) p -> n h p", n=128),
              c.S_t[:].rearrange("n (h p) -> n h p", p=128), c.s_out, deps=[upd_tok])
    elif c.mode == "fused":
        sp = k.dma("sync", c.coll_in_d.rearrange("(h n) p -> n h p", n=128),
                   c.S_t[:].rearrange("n (h p) -> n h p", p=128), c.s_bnc, deps=[upd_tok])

        def coll(e):
            return e.collective_compute(
                "AllGather", ALU.bypass, [[0, 1, 2, 3, 4, 5, 6, 7]],
                [c.coll_in_d.opt()], [c.coll_out_d.opt()]).then_inc(c.s_coll.sem, 1)
        c.s_coll.count += 1
        k.op("gpsimd", coll, deps=[sp], inc=False)


def phase_p3z(c):
    k = c.k
    k.fence([c.s_sp, c.s_tr, c.s_w, c.s_bnc])
    XF = [c.bf("AR1", j * 4096, 4096) for j in range(8)]
    WZ = [c.bf("AR2", j * 2048, 2048) for j in range(8)]
    ZST = c.bf("AR3", 0, 16 * 512)
    for j in range(8):
        k.dma("sync", XF[j][:], c.xn2_tm_d[0:T, j * 128:(j + 1) * 128], c.s_tr, transpose=True)
        k.dma("gpsimd", WZ[j][:], c.w_in[j * 128:(j + 1) * 128, 0:2048], c.s_w)
    trtok, wtok = c.s_tr.tok(), c.s_w.tok()
    for j in range(8):
        k.op("scalar", lambda e, j=j: e.activation(
            XF[j][:], XF[j][:], AF.Identity, scale=c.lnw_t[:, j:j + 1], bias=c.lnb_t[:, j:j + 1]),
            deps=[trtok])
    aff_tok = k.tok("scalar")
    ev = []
    sp_prev = None
    for g in range(8):
        for j in range(16):
            bank = c.ps[j % 2]
            mm = None
            for kc in range(8):
                mm = k.op("tensor", lambda e, j=j, kc=kc, g=g, bank=bank: e.matmul(
                    bank[:, 0:512], WZ[kc][:, j * 128:(j + 1) * 128],
                    XF[kc][:, g * 512:(g + 1) * 512], start=(kc == 0), stop=(kc == 7)),
                    deps=[aff_tok, wtok] + ([ev[-2]] if len(ev) >= 2 else []), inc=(kc == 7))
            ev.append(k.op("scalar", lambda e, j=j, bank=bank: e.activation(
                ZST[:, j * 512:(j + 1) * 512], bank[:, 0:512], AF.Silu),
                deps=[mm] + ([sp_prev] if (j < 2 and sp_prev) else [])))
        sp_prev = k.dma("scalar", c.z_fm_d.rearrange("(j p) t -> p j t", p=128)[:, :, g * 512:(g + 1) * 512],
                        ZST[:].rearrange("p (j w) -> p j w", w=512), c.s_sp, deps=[ev[-1]])


def phase_p3(c):
    k = c.k
    k.fence([c.s_sp, c.s_tr, c.s_w])
    XF = [c.bf("AR1", j * 4096, 4096) for j in range(8)]
    for j in range(8):
        k.dma("sync", XF[j][:], c.xn_tm_d[0:T, j * 128:(j + 1) * 128], c.s_tr, transpose=True)
    xn_tok = c.s_tr.tok()
    RW = [c.bf("AR3", 26816 + j * 16, 16) for j in range(8)]
    for j in range(8):
        k.dma("gpsimd", RW[j][:], c.router_wT[j * 128:(j + 1) * 128, :], c.s_w)
    rw_tok = c.s_w.tok()
    lg_ev = [None, None]
    for i in range(NCH):
        bank = c.ps[i % 2]
        mm = None
        for kc in range(8):
            mm = k.op("tensor", lambda e, i=i, kc=kc, bank=bank: e.matmul(
                bank[:, 0:16], XF[kc][:, i * 128:(i + 1) * 128], RW[kc][:],
                start=(kc == 0), stop=(kc == 7)),
                deps=[xn_tok, rw_tok] + ([lg_ev[i % 2]] if lg_ev[i % 2] else []),
                inc=(kc == 7))
        lg_ev[i % 2] = k.op("scalar", lambda e, i=i, bank=bank: e.activation(
            c.comb[:, i * 16:(i + 1) * 16], bank[:, 0:16], AF.Identity), deps=[mm])
    lg_tok = k.tok("scalar")
    _routing_elementwise(c, lg_tok)
    cb_ev = None
    cb_sp = None
    for i in range(NCH):
        m = k.op("tensor", lambda e, i=i: e.matmul(
            c.ps[2][:, 0:128][0:16, :], c.comb[:, i * 16:(i + 1) * 16], c.i128_t[:],
            start=True, stop=True), deps=[k.tok("vector")] + ([cb_ev] if cb_ev else []))
        cb_ev = k.op("scalar", lambda e, i=i: e.activation(c.tiny16[0:16, :],
                                                           c.ps[2][:, 0:128][0:16, :],
                                                           AF.Identity),
                     deps=[m] + ([cb_sp] if cb_sp else []))
        cb_sp = k.dma("scalar", c.comb_fm_d[0:16, i * 128:(i + 1) * 128], c.tiny16[0:16, :],
                      c.s_sp, deps=[cb_ev])
    comb_sp_tok = cb_sp
    route_free = k.tok("vector")
    SGT = [c.bf("AR2", j * 2048, 2048) for j in range(8)]
    SUT = [c.bf("AR2", 16384 + j * 2048, 2048) for j in range(8)]
    for j in range(8):
        k.dma("gpsimd", SGT[j][:], c.sgT[j * 128:(j + 1) * 128, :], c.s_w)
        k.dma("gpsimd", SUT[j][:], c.suT[j * 128:(j + 1) * 128, :], c.s_w2)
    sw_tok = c.s_w.tok()
    su_tok = c.s_w2.tok()
    HST = c.bf("AR3", 0, 16 * 512)
    TMP = [c.f32("AR3", 8192 + t * 1024, 512) for t in range(2)]
    sp_prev = None
    ev_hist = {}
    nj = 0
    for g in range(8):
        ev_last = None
        for j in range(16):
            pg, pu = c.ps[4 + 2 * (j % 2)], c.ps[5 + 2 * (j % 2)]
            mm_g = mm_u = None
            pdep = [ev_hist.get(nj - 2)] if ev_hist.get(nj - 2) else []
            for kc in range(8):
                mm_g = k.op("tensor", lambda e, j=j, kc=kc, g=g, pg=pg: e.matmul(
                    pg[:, 0:512], SGT[kc][:, j * 128:(j + 1) * 128],
                    XF[kc][:, g * 512:(g + 1) * 512], start=(kc == 0), stop=(kc == 7)),
                    deps=[sw_tok, xn_tok] + pdep, inc=(kc == 7))
            for kc in range(8):
                mm_u = k.op("tensor", lambda e, j=j, kc=kc, g=g, pu=pu: e.matmul(
                    pu[:, 0:512], SUT[kc][:, j * 128:(j + 1) * 128],
                    XF[kc][:, g * 512:(g + 1) * 512], start=(kc == 0), stop=(kc == 7)),
                    deps=[su_tok] + pdep, inc=(kc == 7))
            tmp = TMP[j % 2]
            s1 = k.op("scalar", lambda e, tmp=tmp, pg=pg: e.activation(tmp[:], pg[:], AF.Silu),
                      deps=[mm_g] + pdep)
            ev_last = k.op("vector", lambda e, j=j, tmp=tmp, pu=pu: e.tensor_tensor(
                HST[:, j * 512:(j + 1) * 512], pu[:], tmp[:], ALU.mult),
                deps=[s1, mm_u] + ([sp_prev] if (j < 1 and sp_prev) else []))
            ev_hist[nj] = ev_last
            nj += 1
        sp_prev = k.dma("scalar",
                        c.h_fm_d.rearrange("(j p) t -> p j t", p=128)[:, 32:48, g * 512:(g + 1) * 512],
                        HST[:].rearrange("p (j w) -> p j w", w=512), c.s_sp, deps=[ev_last])
    GWB = [[c.bf("AR2", b * 8192 + j * 256, 256) for j in range(8)] for b in range(2)]
    UWB = [[c.bf("AR2", 16384 + b * 8192 + j * 256, 256) for j in range(2 * 4)] for b in range(2)]
    CRP = [c.f32("AR3", 10240 + b * 1024, 512) for b in range(2)]
    TMPE = [c.f32("AR3", 12288 + t * 1024, 512) for t in range(2)]
    HS2 = c.bf("AR3", 0, 2 * 512)
    sh_sp_tok = sp_prev
    sh_pe_tok = k.tok("tensor")
    pair_free = [ev_hist.get(nj - 2), ev_hist.get(nj - 1)]
    wtoks = [None, None]
    ctoks = [None, None]
    sp_toks = {}
    pe_ev = None
    for e_i in range(16):
        b = e_i % 2
        wdep = [c.s_sp.tok()] if e_i >= 2 else [sh_pe_tok]
        k.dma("gpsimd", c.AR2[:, b * 8192:b * 8192 + 2048].rearrange(
            "p (kc j) -> p kc j", j=256),
            c.gate_w[e_i].rearrange("(kc p) j -> p kc j", p=128), c.s_we[b], deps=wdep)
        wtoks[b] = k.dma("gpsimd", c.AR2[:, 16384 + b * 8192:16384 + b * 8192 + 2048].rearrange(
            "p (kc j) -> p kc j", j=256),
            c.up_w[e_i].rearrange("(kc p) j -> p kc j", p=128), c.s_we[b])
        for g in range(8):
            bb = g % 2
            cdep = [c.s_sp.tok()]
            ctoks[bb] = k.dma("sync", CRP[bb][:],
                              c.comb_fm_d[e_i:e_i + 1, g * 512:(g + 1) * 512].broadcast_to((128, 512)),
                              c.s_cr[bb], deps=cdep + [comb_sp_tok, route_free])
            v2 = None
            for jj in range(2):
                pg, pu = c.ps[4 + 2 * jj], c.ps[5 + 2 * jj]
                mm_g = mm_u = None
                pbank = [t for t in (pair_free[jj],) if t]
                for kc in range(8):
                    mm_g = k.op("tensor", lambda e, jj=jj, kc=kc, g=g, b=b, pg=pg: e.matmul(
                        pg[:, 0:512],
                        c.AR2[:, b * 8192 + kc * 256 + jj * 128: b * 8192 + kc * 256 + jj * 128 + 128],
                        XF[kc][:, g * 512:(g + 1) * 512], start=(kc == 0), stop=(kc == 7)),
                        deps=[wtoks[b], xn_tok] + pbank, inc=(kc == 7))
                for kc in range(8):
                    mm_u = k.op("tensor", lambda e, jj=jj, kc=kc, g=g, b=b, pu=pu: e.matmul(
                        pu[:, 0:512],
                        c.AR2[:, 16384 + b * 8192 + kc * 256 + jj * 128: 16384 + b * 8192 + kc * 256 + jj * 128 + 128],
                        XF[kc][:, g * 512:(g + 1) * 512], start=(kc == 0), stop=(kc == 7)),
                        deps=pbank, inc=(kc == 7))
                tmp = TMPE[jj]
                s1 = k.op("scalar", lambda e, tmp=tmp, pg=pg: e.activation(tmp[:], pg[:], AF.Silu),
                          deps=[mm_g] +
                          ([sh_sp_tok] if (e_i == 0 and g == 0 and jj == 0) else []))
                v1 = k.op("vector", lambda e, tmp=tmp, pu=pu: e.tensor_tensor(
                    tmp[:], pu[:], tmp[:], ALU.mult), deps=[s1, mm_u])
                v2 = k.op("vector", lambda e, jj=jj, bb=bb, tmp=tmp: e.tensor_tensor(
                    HS2[:, jj * 512:(jj + 1) * 512], tmp[:], CRP[bb][:], ALU.mult),
                    deps=[v1, ctoks[bb]] +
                    ([sh_sp_tok] if (e_i == 0 and g == 0 and jj == 0) else []))
                pair_free[jj] = v2
            pe_ev = k.tok("vector")
            sp_toks[(e_i, g)] = k.dma(
                "scalar",
                c.h_fm_d.rearrange("(j p) t -> p j t", p=128)[:, 2 * e_i:2 * e_i + 2,
                                                              g * 512:(g + 1) * 512],
                HS2[:].rearrange("p (j w) -> p j w", w=512), c.s_sp,
                deps=[pe_ev])
    dwt = {}
    wqs = [c.s_w, c.s_w2, c.s_w3]
    qi = 0
    for e_i in range(16):
        if e_i % 4 == 0:
            continue
        dwt[e_i] = k.dma("gpsimd",
                         c.AR2[:, e_i * 2048:(e_i + 1) * 2048].rearrange("p (kc d) -> p kc d", d=1024),
                         c.down_w[e_i].rearrange("(kc p) d -> p kc d", p=128),
                         wqs[qi % 3], deps=[sh_pe_tok])
        qi += 1
    k.fence([c.s_sp, c.s_g2])
    DW = [c.bf("AR2", i * 1024, 1024) for i in range(32)]
    SD = [c.bf("AR1", i * 1024, 1024) for i in range(16)]
    for e_i in (0, 4, 8, 12):
        dwt[e_i] = k.dma("gpsimd",
                         c.AR2[:, e_i * 2048:(e_i + 1) * 2048].rearrange("p (kc d) -> p kc d", d=1024),
                         c.down_w[e_i].rearrange("(kc p) d -> p kc d", p=128), wqs[e_i // 4 % 3])
    sdt = {}
    for j in range(16):
        sdt[j] = k.dma("gpsimd", SD[j][:], c.sdT[j * 128:(j + 1) * 128, :], wqs[j % 3])
    HLD = [c.bf("AR3", b * 6144, 6144) for b in range(2)]
    GT = c.f32("AR3", 12288, 1024)
    htoks = [None, None]
    gsp = None
    gev = [None] * NCH
    pe_hist = [None] * NCH
    for g in range(NCH):
        b = g % 2
        hdep = [pe_hist[g - 2]] if g >= 2 and pe_hist[g - 2] else []
        htoks[b] = k.dma("sync", HLD[b][:].rearrange("p (j w) -> p j w", w=128),
                         c.h_fm_d.rearrange("(j p) t -> p j t", p=128)[:, :, g * Q:(g + 1) * Q],
                         c.s_hl[b], deps=hdep)
        pd = [c.ps[0 + 2 * (g % 2)], c.ps[1 + 2 * (g % 2)]]
        mm = None
        for half in range(2):
            first = True
            for j in range(48):
                src = (c.AR2[:, (j // 2) * 2048 + (j % 2) * 1024 + half * 512:
                             (j // 2) * 2048 + (j % 2) * 1024 + half * 512 + 512]
                       if j < 32 else SD[j - 32][:, half * 512:half * 512 + 512])
                bankdep = [gev[g - 2]] if (first and g >= 2 and gev[g - 2]) else []
                wtok = dwt[j // 2] if j < 32 else sdt[j - 32]
                mm = k.op("tensor", lambda e, j=j, half=half, b=b, src=src, pd=pd, first=first:
                          e.matmul(pd[half][:, 0:512],
                                   HLD[b][:, j * 128:(j + 1) * 128], src,
                                   start=first, stop=(j == 47)),
                          deps=[htoks[b], wtok] + bankdep, inc=(j == 47))
                first = False
        pe_hist[g] = k.tok("tensor")
        gdep = [gsp] if gsp else []
        e1 = k.op("scalar", lambda e, pd=pd: e.activation(GT[:, 0:512], pd[0][:], AF.Silu),
                  deps=[mm] + gdep)
        e2 = k.op("scalar", lambda e, pd=pd: e.activation(GT[:, 512:1024], pd[1][:], AF.Silu))
        gev[g] = e2
        gsp = k.dma("scalar", c.g_tm_d[g * Q:(g + 1) * Q, :], GT[:], c.s_sp, deps=[e2])


def _routing_elementwise(c, lg_tok):
    k = c.k
    SC = c.f32("AR3", 0, 512)
    SBv = c.f32("AR3", 1024, 512)
    GS = c.f32("AR3", 2048, 128)
    GM = c.f32("AR3", 2304, 128)
    E1 = c.f32("AR3", 2560, 128)
    E2 = c.f32("AR3", 2816, 128)
    MK = c.f32("AR3", 3072, 512)
    SEL = c.f32("AR3", 4096, 512)
    EQ = c.f32("AR3", 5120, 512)
    CU = c.f32("AR3", 6144, 512)
    CV = c.f32("AR3", 7168, 512)
    MX = c.f32("AR3", 8192, 32)
    CB = c.f32("AR3", 8256, 16)
    WS = c.f32("AR3", 8288, 32)
    t0 = k.op("vector", lambda e: e.tensor_scalar(SC[:], c.comb[:], 0.5, None, ALU.mult),
              deps=[lg_tok])
    t1 = k.op("scalar", lambda e: e.activation(SC[:], SC[:], AF.Tanh), deps=[t0])
    t2 = k.op("vector", lambda e: e.tensor_scalar(SC[:], SC[:], 0.5, 0.5, ALU.mult, ALU.add),
              deps=[t1])
    m0 = k.op("tensor", lambda e: e.matmul(c.ps[3][:, 0:16], c.ones1[:], c.corrb_t[:],
                                           start=True, stop=True), deps=[c.consts])
    c0 = k.op("scalar", lambda e: e.activation(CB[:, 0:16], c.ps[3][:, 0:16], AF.Identity),
              deps=[m0])
    t3 = k.op("vector", lambda e: e.tensor_tensor(
        SBv[:].rearrange("p (a b) -> p a b", b=16), SC[:].rearrange("p (a b) -> p a b", b=16),
        CB[:, 0:16].unsqueeze(1).broadcast_to((128, 32, 16)), ALU.add), deps=[t2, c0])
    V = SBv.rearrange("p (a b) -> p a b", b=4)
    first = True
    for i in range(4):
        for j in range(i + 1, 4):
            if first:
                k.op("vector", lambda e, i=i, j=j: e.tensor_tensor(GS[:], V[:, :, i], V[:, :, j],
                                                                   ALU.add), deps=[t3])
                first = False
            else:
                k.op("vector", lambda e, i=i, j=j: e.tensor_tensor(CU[:, 0:128], V[:, :, i],
                                                                   V[:, :, j], ALU.add))
                k.op("vector", lambda e: e.tensor_tensor(GS[:], GS[:], CU[:, 0:128], ALU.max))
    k.op("vector", lambda e: e.memset(GM[:], 0.0))
    GSv = GS.rearrange("p (a b) -> p a b", b=4)
    for it in range(2):
        k.op("vector", lambda e: e.tensor_reduce(MX[:].rearrange("p (a b) -> p a b", b=1),
                                                 GSv, AX.X, ALU.max))
        k.op("vector", lambda e: e.tensor_tensor(
            E1[:].rearrange("p (a b) -> p a b", b=4), GSv,
            MX[:].unsqueeze(2).broadcast_to((128, 32, 4)), ALU.is_equal))
        k.op("vector", lambda e: e.tensor_copy(E2[:], E1[:]))
        k.op("vector", lambda e: e.tensor_tensor(
            E2[:].rearrange("p (a b) -> p a b", b=4)[:, :, 1:4],
            E1[:].rearrange("p (a b) -> p a b", b=4)[:, :, 1:4],
            E1[:].rearrange("p (a b) -> p a b", b=4)[:, :, 0:3], ALU.add))
        k.op("vector", lambda e: e.tensor_copy(CU[:, 0:128], E2[:]))
        k.op("vector", lambda e: e.tensor_tensor(
            CU[:, 0:128].rearrange("p (a b) -> p a b", b=4)[:, :, 2:4],
            E2[:].rearrange("p (a b) -> p a b", b=4)[:, :, 2:4],
            E2[:].rearrange("p (a b) -> p a b", b=4)[:, :, 0:2], ALU.add))
        k.op("vector", lambda e: e.tensor_tensor(CU[:, 0:128], CU[:, 0:128], E1[:],
                                                 ALU.subtract))
        k.op("vector", lambda e: e.tensor_scalar(CU[:, 0:128], CU[:, 0:128], 0.0, None,
                                                 ALU.is_equal))
        k.op("vector", lambda e: e.tensor_tensor(E1[:], E1[:], CU[:, 0:128], ALU.mult))
        k.op("vector", lambda e: e.tensor_tensor(GM[:], GM[:], E1[:], ALU.add))
        if it == 0:
            k.op("vector", lambda e: e.scalar_tensor_tensor(GS[:], E1[:], NEG, GS[:],
                                                            ALU.mult, ALU.add))
    k.op("vector", lambda e: e.tensor_scalar(GM[:], GM[:], -1.0, NEG, ALU.add, ALU.mult))
    k.op("vector", lambda e: e.tensor_scalar(GM[:], GM[:], -1.0, None, ALU.mult))
    k.op("vector", lambda e: e.tensor_tensor(
        MK[:].rearrange("p (a b) -> p a b", b=4), SBv.rearrange("p (a b) -> p a b", b=4),
        GM[:].unsqueeze(2).broadcast_to((128, 128, 4)), ALU.add))
    k.op("vector", lambda e: e.memset(SEL[:], 0.0))
    MKv = MK.rearrange("p (a b) -> p a b", b=16)
    for it in range(4):
        k.op("vector", lambda e: e.tensor_reduce(MX[:].rearrange("p (a b) -> p a b", b=1),
                                                 MKv, AX.X, ALU.max))
        k.op("vector", lambda e: e.tensor_tensor(
            EQ[:].rearrange("p (a b) -> p a b", b=16), MKv,
            MX[:].unsqueeze(2).broadcast_to((128, 32, 16)), ALU.is_equal))
        k.op("vector", lambda e: e.tensor_copy(CU[:], EQ[:]))
        src, dst = CU, CV
        for sh in (1, 2, 4, 8):
            k.op("vector", lambda e, sh=sh, src=src, dst=dst: e.tensor_copy(
                dst[:].rearrange("p (a b) -> p a b", b=16)[:, :, 0:sh],
                src[:].rearrange("p (a b) -> p a b", b=16)[:, :, 0:sh]))
            k.op("vector", lambda e, sh=sh, src=src, dst=dst: e.tensor_tensor(
                dst[:].rearrange("p (a b) -> p a b", b=16)[:, :, sh:16],
                src[:].rearrange("p (a b) -> p a b", b=16)[:, :, sh:16],
                src[:].rearrange("p (a b) -> p a b", b=16)[:, :, 0:16 - sh], ALU.add))
            src, dst = dst, src
        k.op("vector", lambda e, src=src: e.tensor_tensor(CV[:] if src is CU else CU[:],
                                                          src[:], EQ[:], ALU.subtract))
        exv = CV if src is CU else CU
        k.op("vector", lambda e, exv=exv: e.tensor_scalar(exv[:], exv[:], 0.0, None,
                                                          ALU.is_equal))
        k.op("vector", lambda e, exv=exv: e.tensor_tensor(EQ[:], EQ[:], exv[:], ALU.mult))
        k.op("vector", lambda e: e.tensor_tensor(SEL[:], SEL[:], EQ[:], ALU.add))
        if it < 3:
            k.op("vector", lambda e: e.scalar_tensor_tensor(MK[:], EQ[:], NEG, MK[:],
                                                            ALU.mult, ALU.add))
    k.op("vector", lambda e: e.tensor_tensor(SEL[:], SEL[:], SC[:], ALU.mult))
    k.op("vector", lambda e: e.tensor_reduce(WS[:].rearrange("p (a b) -> p a b", b=1),
                                             SEL[:].rearrange("p (a b) -> p a b", b=16),
                                             AX.X, ALU.add))
    k.op("vector", lambda e: e.tensor_scalar(WS[:], WS[:], 1e-20, None, ALU.add))
    k.op("vector", lambda e: e.reciprocal(WS[:], WS[:]))
    k.op("vector", lambda e: e.tensor_tensor(
        c.comb[:].rearrange("p (a b) -> p a b", b=16),
        SEL[:].rearrange("p (a b) -> p a b", b=16),
        WS[:].unsqueeze(2).broadcast_to((128, 32, 16)), ALU.mult))


import os as _os
DBG_NODEFER = bool(int(_os.environ.get("DBG_NODEFER", "0")))


def phase_p2a(c):
    k = c.k
    if c.mode != "twoB":
        SG2 = [c.f32("AR3", 14336 + b_ * 2048, 1024) for b_ in range(2)]
        init_tok = k.op("vector", lambda e: e.memset(c.S_t[:], 0.0))
        sg_toks = {}
        for idx in range(16):
            r, half = idx // 2, idx % 2
            b2 = idx % 2
            ld = k.dma("sync", SG2[b2][:].rearrange("n (h p) -> n h p", p=128),
                       c.coll_out_d[r * DI + half * 1024:r * DI + (half + 1) * 1024]
                       .rearrange("(h n) p -> n h p", n=128),
                       c.s_in, deps=[(c.s_coll.sem, c.s_coll.count), init_tok] +
                       ([sg_toks[idx - 2]] if idx >= 2 else []))
            sg_toks[idx] = k.op("vector", lambda e, r=r, half=half, b2=b2:
                                e.scalar_tensor_tensor(
                c.S_t[:, half * 1024:(half + 1) * 1024], SG2[b2][:],
                c.smask_t[:, r:r + 1], c.S_t[:, half * 1024:(half + 1) * 1024],
                ALU.mult, ALU.add), deps=[ld, c.consts])
        init_tok = sg_toks[15]
    k.fence([c.s_sp, c.s_tr, c.s_w, c.s_g2, c.s_bnc])
    XHT = [c.bf("AR1", b * 2048, 2048) for b in range(2)]
    SZT = [c.bf("AR1", 4096 + b * 2048, 2048) for b in range(2)]
    LL = [c.bf("AR1", 8192 + b * 2048, 2048) for b in range(2)]
    MT = [c.bf("AR1", 12288 + b * 2048, 2048) for b in range(2)]
    XTI = [c.bf("AR1", 16384 + b * 2048, 2048) for b in range(2)]
    XDC = [c.bf("AR1", 20480 + b * 2048, 2048) for b in range(2)]
    SBF = c.bf("AR1", 24576, 2048)
    YTM = c.f32("AR1", 26624, 2048)
    YBF = [c.bf("AR2", b * 2048, 2048) for b in range(2)]
    BTM = [c.bf("AR2", 4096 + b * 128, 128) for b in range(2)]
    BFM = [c.bf("AR2", 4352 + b * 128, 128) for b in range(2)]
    CFM = [c.bf("AR2", 4608 + b * 128, 128) for b in range(2)]
    GSB = [c.f32("AR2", 4864 + b * 256, 128) for b in range(2)]
    S17 = [c.f32("AR2", 5376 + b * 256, 128, 17) for b in range(2)]
    ECOL = [c.f32("AR2", 5888 + b * 32, 16) for b in range(2)]
    EQR = [c.f32("AR2", 5952 + b * 32, 16) for b in range(2)]
    EQROW = [c.f32("AR2", 6016 + b * 32, 16, 1) for b in range(2)]
    T2E = [c.bf("AR2", 6080, 512), c.bf("AR2", 6592, 512)]
    DMS = [c.f32("AR2", 7104 + b * 4096, 2048) for b in range(2)]
    Y2 = c.f32("AR2", 15296, 2048)
    STMP = c.f32("AR2", 19392, 2048)
    DRHS = [c.f32("AR2", 23488 + b * 4096, 2048, 17) for b in range(2)]
    msqz = k.op("vector", lambda e: e.memset(c.msq_all[:], 0.0))
    nd_tok = [k.dma("sync", DRHS[b_][0:16, :], c.negd, c.s_const) for b_ in range(2)][-1]
    if c.mode == "twoB":
        si = k.dma("sync", c.S_t[:].rearrange("n (h p) -> n h p", p=128),
                   c.s_init_in.rearrange("(h n) p -> n h p", n=128), c.s_in)
        init_tok = k.op("vector", lambda e: e.engine_nop(), deps=[si])
    st = dict(upd=init_tok, yb=None, m5=[None, None], dbuild=[None, None],
              lexp=[None, None], dv_e=[None, None], ge=[None, None], q2=[None, None],
              scst=None, prev_yt=None, prev_yt2=None, dvo_tok=None,
              xput=[None, None], bput=[None, None], szput=[None, None],
              bfput=[None, None], cfput=[None, None], s17put=[None, None],
              mt={}, xt={}, xd={}, ectok={}, eq={}, lastmm1={}, gt={}, ds={},
              gm={}, mm2l={}, mul={}, e2l={}, ms={}, getok={}, q1=[None, None])

    def eqg(cc):
        b = cc % 2
        m5 = k.op("tensor", lambda e, b=b: e.matmul(c.ps[0][:, 128 + b * 16:144 + b * 16],
                                                    c.ones1[:], EQROW[b][0:1, :],
                                                    start=True, stop=True),
                  deps=[st["q1"][b]] + [t for t in (st["q2"][b],) if t])
        q2 = k.op("scalar", lambda e, b=b: e.activation(EQR[b][:, 0:16],
                                                        c.ps[0][:, 128 + b * 16:144 + b * 16],
                                                        AF.Identity),
                  deps=[m5] + [t for t in (st["mul"].get(cc - 2),) if t])
        st["m5"][b] = m5
        st["q2"][b] = q2
        st["eq"][cc] = q2
        gm = k.op("tensor", lambda e, b=b: e.matmul(c.ps[0][:, 0:128], BFM[b][:], CFM[b][:],
                                                    start=True, stop=True),
                  deps=[st["bfput"][b], q2] + [t for t in (st["ge"][1 - b],) if t])
        st["gm"][cc] = gm
        ge = k.op("scalar", lambda e, b=b: e.activation(GSB[b][:], c.ps[0][:, 0:128],
                                                        AF.Identity),
                  deps=[gm] + [t for t in (st["mt"].get(cc - 2),) if t])
        st["ge"][b] = ge
        st["getok"][cc] = ge

    def mtxt(cc):
        b = cc % 2
        mt = k.op("gpsimd", lambda e, b=b: e.tensor_tensor(
            MT[b][:].rearrange("p (h w) -> p h w", w=128),
            LL[b][:].rearrange("p (h w) -> p h w", w=128),
            GSB[b][:].unsqueeze(1).broadcast_to((128, 16, 128)), ALU.mult),
            deps=[st["lexp"][b], st["getok"][cc]] +
            [t for t in (st["lastmm1"].get(cc - 2),) if t])
        xt = k.op("gpsimd", lambda e, cc=cc, b=b: e.tensor_tensor(
            XTI[b][:].rearrange("p (h w) -> p h w", w=128),
            XHT[b][:].rearrange("p (h w) -> p h w", w=128),
            c.dttm_all[:, cc * 16:(cc + 1) * 16].unsqueeze(2).broadcast_to((128, 16, 128)),
            ALU.mult), deps=[st["xput"][b]] + [t for t in (st["lastmm1"].get(cc - 2),) if t])
        st["mt"][cc] = mt
        st["xt"][cc] = xt

    def xdo(cc):
        b = cc % 2
        st["xd"][cc] = k.op("vector", lambda e, cc=cc, b=b: e.tensor_tensor(
            XDC[b][:].rearrange("p (h w) -> p h w", w=128),
            XHT[b][:].rearrange("p (h w) -> p h w", w=128),
            c.dectm_all[:, cc * 16:(cc + 1) * 16].unsqueeze(2).broadcast_to((128, 16, 128)),
            ALU.mult), deps=[st["xput"][b]] + [t for t in (st["ds"].get(cc - 2),) if t])

    def s1(cc):
        b = cc % 2
        eqput = k.dma("sync", EQROW[b][0:1, :], c.sq_d[cc:cc + 1, :], c.s_pb[b],
              deps=[t for t in (st["m5"][b],) if t])
        s17dep = [t for t in (st["dbuild"][b],) if t]
        st["s17put"][b] = k.dma("sync", S17[b][:], c.sfm_d[cc], c.s_pb[b], deps=s17dep)
        drput = k.dma("sync", DRHS[b][16:17, :], c.sflat_d[cc:cc + 1, :], c.s_pb[b])
        gpb = [t for t in (st["gm"].get(cc - 2), st["mm2l"].get(cc - 2)) if t]
        k.dma("sync", BFM[b][:], c.xbc_fm_d[2048:2176, cc * Q:(cc + 1) * Q],
              c.s_pb[b], deps=gpb)
        st["bfput"][b] = st["cfput"][b] = k.dma(
            "sync", CFM[b][:], c.xbc_fm_d[2176:2304, cc * Q:(cc + 1) * Q], c.s_pb[b])
        gpa = [t for t in (st["lastmm1"].get(cc - 2), st["gt"].get(cc - 2),
                           st["ds"].get(cc - 2)) if t]
        k.dma("sync", XHT[b][:], c.xbc_fm_d[0:2048, cc * Q:(cc + 1) * Q], c.s_pa[b],
              transpose=True, deps=gpa)
        k.dma("sync", BTM[b][:], c.xbc_fm_d[2048:2176, cc * Q:(cc + 1) * Q], c.s_pa[b],
              transpose=True)
        st["xput"][b] = st["bput"][b] = st["szput"][b] = k.dma(
            "sync", SZT[b][:], c.z_fm_d[0:2048, cc * Q:(cc + 1) * Q], c.s_pa[b],
            transpose=True)
        ec = k.op("scalar", lambda e, cc=cc, b=b: e.activation(
            ECOL[b][:, 0:16], c.stm_all[:, cc * 16:(cc + 1) * 16], AF.Exp),
            deps=[t for t in (st["e2l"].get(cc - 2),) if t])
        st["ectok"][cc] = ec
        dl = [None, None, None, None]
        for gi in range(4):
            pD = c.ps[1] if gi % 2 == 0 else c.ps[6]
            if gi < 2:
                ddep = [st["s17put"][b], drput, nd_tok]
                ddep += ([t for t in (st["dv_e"][1 - b],) if t] if gi == 0 else
                         [t for t in (st["prev_yt"],) if t])
            else:
                ddep = [dl[gi - 2]]
            dm = k.op("tensor", lambda e, gi=gi, b=b, pD=pD: e.matmul(
                pD[:, 0:512], S17[b][0:17, :], DRHS[b][0:17, gi * 512:(gi + 1) * 512],
                start=True, stop=True), deps=ddep)
            st["dbuild"][b] = dm
            dv = k.op("vector", lambda e, gi=gi, b=b, pD=pD: e.tensor_tensor(
                DMS[b][:, gi * 512:(gi + 1) * 512], pD[:, 0:512], c.tricap_t[:],
                ALU.min), deps=[dm, c.consts] +
                [t for t in (st["lexp"][b],) if t and gi == 0])
            dl[gi] = dv
            k.op("scalar", lambda e, gi=gi, b=b: e.activation(
                LL[b][:, gi * 512:(gi + 1) * 512], DMS[b][:, gi * 512:(gi + 1) * 512],
                AF.Exp), deps=[dv] + [t for t in (st["mt"].get(cc - 2),) if t and gi == 0])
        st["dv_e"][b] = dl[2]
        st["dvo_tok"] = dl[3]
        st["lexp"][b] = k.tok("scalar")
        st["q1"][b] = k.op("scalar", lambda e, b=b: e.activation(
            EQROW[b][0:1, :], EQROW[b][0:1, :], AF.Exp), deps=[eqput])
        if cc == 0 or DBG_NODEFER:
            eqg(cc)
            mtxt(cc)
            xdo(cc)

    def s23(cc):
        b = cc % 2
        sca = k.op("scalar", lambda e: e.activation(SBF[:, 0:1024], c.S_t[:, 0:1024],
                                                    AF.Identity),
                   deps=[st["upd"]] + [t for t in (st["mm2l"].get(cc - 1),) if t])
        scb = k.op("gpsimd", lambda e: e.tensor_copy(SBF[:, 1024:2048], c.S_t[:, 1024:2048]),
                   deps=[st["upd"]] + [t for t in (st["mm2l"].get(cc - 1),) if t])
        ytoks = []
        e2_prev = {0: st["prev_yt2"], 1: st["prev_yt"]}
        mm2 = None
        e2 = None
        for hg in range(4):
            p1 = c.ps[4 + 2 * (hg % 2)]
            p2 = c.ps[5 + 2 * (hg % 2)]
            if hg >= 2:
                bank_free = [ytoks[hg - 2]]
            else:
                bank_free = [t for t in ((st["prev_yt2"],) if hg == 0 else
                                         (st["prev_yt"],)) if t]
            if hg % 2 == 1:
                bank_free = bank_free + [t for t in (st["dvo_tok"],) if t]
            mm1 = None
            for hh in range(4):
                h = hg * 4 + hh
                mm1 = k.op("tensor", lambda e, h=h, hh=hh, p1=p1, b=b: e.matmul(
                    p1[:, hh * 128:(hh + 1) * 128], MT[b][:, h * 128:(h + 1) * 128],
                    XTI[b][:, h * 128:(h + 1) * 128], start=True, stop=False),
                    deps=[st["mt"][cc], st["xt"][cc]] + bank_free, inc=False)
                mm1 = k.op("tensor", lambda e, h=h, hh=hh, p1=p1, b=b: e.matmul(
                    p1[:, hh * 128:(hh + 1) * 128], c.dhi[:, h * 128:(h + 1) * 128],
                    XHT[b][:, h * 128:(h + 1) * 128], start=False, stop=True),
                    deps=[c.dhi_tok], inc=(hh == 3))
            scd = sca if hg < 2 else scb
            for hh in range(4):
                h = hg * 4 + hh
                mm2 = k.op("tensor", lambda e, h=h, hh=hh, p2=p2, b=b: e.matmul(
                    p2[:, hh * 128:(hh + 1) * 128], CFM[b][:], SBF[:, h * 128:(h + 1) * 128],
                    start=True, stop=True), deps=[scd, st["cfput"][b]] + bank_free,
                    inc=(hh == 3))
            for hh in range(4):
                h = hg * 4 + hh
                e2 = k.op("scalar", lambda e, h=h, hh=hh, p2=p2, hg=hg, b=b: e.activation(
                    T2E[hg % 2][:, hh * 128:(hh + 1) * 128], p2[:, hh * 128:(hh + 1) * 128],
                    AF.Identity, scale=ECOL[b][:, h:h + 1]),
                    deps=[mm2, st["ectok"][cc]] +
                    ([e2_prev[hg % 2]] if (hg < 2 and e2_prev[hg % 2]) else
                     [ytoks[hg - 2]] if hg >= 2 else []))
            yt = k.op("vector", lambda e, hg=hg, p1=p1: e.tensor_tensor(
                YTM[:, hg * 512:(hg + 1) * 512], p1[:, 0:512], T2E[hg % 2][:], ALU.add),
                deps=[e2, mm1] + [t for t in (st["ms"].get(cc - 1),) if t])
            ytoks.append(yt)
        st["mm2l"][cc] = mm2
        st["e2l"][cc] = e2
        st["prev_yt"] = ytoks[3]
        st["prev_yt2"] = ytoks[2]
        st["lastmm1"][cc] = ytoks[3]
        if cc + 1 < NCH and not DBG_NODEFER:
            eqg(cc + 1)
        mm_ = None
        for half in range(2):
            pa, pb2 = c.ps[2], c.ps[3]
            for hh in range(8):
                h = half * 8 + hh
                pt = pa if hh < 4 else pb2
                mm_ = k.op("tensor", lambda e, h=h, hh=hh, pt=pt, b=b: e.matmul(
                    pt[:, (hh % 4) * 128:(hh % 4 + 1) * 128], BTM[b][:],
                    XDC[b][:, h * 128:(h + 1) * 128], start=True, stop=True),
                    deps=[st["xd"][cc], st["bput"][b]] +
                    ([st["upd"], sca, scb] if hh == 0 else []), inc=(hh == 7))
            mu = k.op("gpsimd", lambda e, half=half, b=b: e.tensor_tensor(
                STMP[:, half * 1024:(half + 1) * 1024].rearrange("p (h w) -> p h w", w=128),
                c.S_t[:, half * 1024:(half + 1) * 1024].rearrange("p (h w) -> p h w", w=128),
                EQR[b][:, half * 8:half * 8 + 8].unsqueeze(2).broadcast_to((128, 8, 128)),
                ALU.mult), deps=[st["eq"][cc], st["upd"], sca, scb])
            st["mul"][cc] = mu
            a1 = k.op("vector", lambda e, half=half, pa=pa: e.tensor_tensor(
                c.S_t[:, half * 1024:half * 1024 + 512], STMP[:, half * 1024:half * 1024 + 512],
                pa[:], ALU.add), deps=[mu, mm_])
            st["upd"] = k.op("vector", lambda e, half=half, pb2=pb2: e.tensor_tensor(
                c.S_t[:, half * 1024 + 512:half * 1024 + 1024],
                STMP[:, half * 1024 + 512:half * 1024 + 1024], pb2[:], ALU.add))
        st["ds"][cc] = mm_
        if cc + 1 < NCH and not DBG_NODEFER:
            xdo(cc + 1)
            mtxt(cc + 1)
        gt = k.op("vector", lambda e, b=b: e.tensor_tensor(YTM[:], YTM[:], SZT[b][:], ALU.mult),
                  deps=[ytoks[-1], st["szput"][b]])
        st["gt"][cc] = gt
        ms = k.op("scalar", lambda e, cc=cc: e.activation(Y2[:], YTM[:], AF.Square,
                                                          accum_out=c.msq_all[:, cc:cc + 1]),
                  deps=[gt, msqz])
        st["ms"][cc] = ms
        r1 = k.op("vector", lambda e, cc=cc: e.tensor_scalar(
            c.tiny[:, 32:33], c.msq_all[:, cc:cc + 1], 1.0 / DI, EPS, ALU.mult, ALU.add),
            deps=[ms])
        r2 = k.op("scalar", lambda e: e.activation(c.tiny[:, 33:34], c.tiny[:, 32:33], AF.Ln),
                  deps=[r1])
        r2b = k.op("vector", lambda e: e.tensor_scalar(c.tiny[:, 33:34], c.tiny[:, 33:34],
                                                       -0.5, None, ALU.mult), deps=[r2])
        r3 = k.op("scalar", lambda e, cc=cc: e.activation(c.rstd_all[:, cc:cc + 1],
                                                          c.tiny[:, 33:34], AF.Exp), deps=[r2b])
        yb = k.op("vector", lambda e, cc=cc, b=b: e.tensor_scalar(
            YBF[b][:], YTM[:], c.rstd_all[:, cc:cc + 1], None, ALU.mult),
            deps=[r3] + ([c.s_sp.tok()] if cc >= 2 else []))
        st["yb"] = yb
        k.dma("scalar", c.y_tm_d[cc * Q:(cc + 1) * Q, :], YBF[b][:], c.s_sp, deps=[yb])

    s1(0)
    for cc in range(1, NCH):
        s1(cc)
        s23(cc - 1)
    s23(NCH - 1)


def phase_p2b(c):
    k = c.k
    k.fence([c.s_sp, c.s_tr, c.s_w, c.s_g2, c.s_in])
    WO = [c.bf("AR2", j * 1024, 1024) for j in range(16)]
    YF2 = [[c.bf("AR1", p_ * 16384 + j * 1024, 1024) for j in range(16)] for p_ in range(2)]
    XT = [c.f32("AR3", b * 2048, DM) for b in range(2)]
    GTL = [c.f32("AR3", 4096 + b * 2048, DM) for b in range(2)]
    XN = c.f32("AR3", 8192, DM)
    SS = c.f32("AR3", 10240, DM)
    OT = [c.f32("AR3", 12288 + b * 2048, DM) for b in range(2)]
    for j in range(16):
        k.dma("gpsimd", WO[j][:], c.w_out[j * 128:(j + 1) * 128, :], c.s_w2)
    wld = c.s_w2.tok()
    for j in range(16):
        k.op("scalar", lambda e, j=j: e.activation(WO[j][:], WO[j][:], AF.Identity,
                                                   scale=c.rmsw_t[:, j:j + 1]),
             deps=[wld, c.consts])
    wo_tok = k.tok("scalar")
    otok = [None, None]
    vfree = [None, None]
    vprev = None
    ptoks = {}
    yput = {}
    for sg in range(4):
        p_ = sg % 2
        for j in range(16):
            dep = [ptoks[sg - 2]] if sg >= 2 else []
            yput[(sg, j)] = k.dma("sync", YF2[p_][j][:],
                                  c.y_tm_d[sg * 1024:(sg + 1) * 1024, j * 128:(j + 1) * 128],
                                  c.s_tr, transpose=True, deps=dep)
        for tc in range(8):
            g = sg * 8 + tc
            b = tc % 2
            pd = [c.ps[0 + 2 * b], c.ps[1 + 2 * b]]
            xl = k.dma("sync", XT[b][:], c.x_main[g * Q:(g + 1) * Q, :], c.s_xl[b],
                       deps=[otok[b]] if otok[b] else [])
            gl = k.dma("gpsimd", GTL[b][:], c.g_tm_d[g * Q:(g + 1) * Q, :], c.s_gl[b],
                       deps=[otok[b]] if otok[b] else [])
            mm = [None, None]
            for half in range(2):
                first = True
                for j in range(16):
                    mm[half] = k.op("tensor", lambda e, j=j, half=half, tc=tc, pd=pd,
                                    first=first, p_=p_:
                                    e.matmul(pd[half][:, 0:512],
                                             YF2[p_][j][:, tc * 128:(tc + 1) * 128],
                                             WO[j][:, half * 512:half * 512 + 512],
                                             start=first, stop=(j == 15)),
                                    deps=[yput[(sg, j)], wo_tok] +
                                    ([vfree[b]] if (first and vfree[b]) else []),
                                    inc=(j == 15))
                    first = False
            ptoks[sg] = k.tok("tensor")
            xn = k.op("scalar", lambda e, g=g, b=b: e.activation(
                XN[:], XT[b][:], AF.Identity, scale=c.rho[:, g:g + 1], bias=c.nmr[:, g:g + 1]),
                deps=[xl] + ([vprev] if vprev else []))
            v1 = k.op("vector", lambda e, pd=pd: e.tensor_tensor(SS[:, 0:512], pd[0][:], XN[:, 0:512],
                                                                 ALU.add), deps=[mm[0], xn])
            v2 = k.op("vector", lambda e, pd=pd: e.tensor_tensor(SS[:, 512:1024], pd[1][:],
                                                                 XN[:, 512:1024], ALU.add),
                      deps=[mm[1]])
            v3 = k.op("vector", lambda e, b=b: e.tensor_tensor(SS[:], SS[:], GTL[b][:], ALU.mult),
                      deps=[v2, gl])
            v4 = k.op("vector", lambda e, b=b: e.tensor_tensor(OT[b][:], SS[:], XT[b][:], ALU.add),
                      deps=[v3])
            vprev = v4
            vfree[b] = v2
            otok[b] = k.dma("sync", c.out_t[g * Q:(g + 1) * Q, :], OT[b][:], c.s_out, deps=[v4])


def finish(c):
    k = c.k
    if c.stage == "full" or c.mode == "twoA":
        k.op("sync", lambda e: e.nop(), deps=[c.s_out.tok()])
    k.fence([c.s_sp, c.s_out])


def _layout_inputs(inputs, n_cores=8):
    f = np.float32
    asf = lambda a: np.ascontiguousarray(np.asarray(a, dtype=f))
    x = asf(inputs["L0_attn_x"])
    w_in = asf(inputs["L0_attn_W_in"])
    conv_w = asf(inputs["L0_attn_conv_w"])
    conv_diag = np.zeros((4, 18, 128, 128), f)
    for t in range(4):
        for j in range(18):
            np.fill_diagonal(conv_diag[t, j], conv_w[j * 128:(j + 1) * 128, t])
    tri = np.triu(np.ones((128, 128), f))
    cap = np.where(tri > 0, 0.0, -1e9).astype(f)
    tricap = np.tile(cap, (1, 4))
    negd = np.zeros((16, 2048), f)
    for h in range(16):
        negd[h, h * 128:(h + 1) * 128] = -1.0
    com = dict(
        w_in=w_in,
        w_out=asf(inputs["L0_attn_W_out"]),
        lnw_col=asf(inputs["L0_attn_ln_w"]).reshape(8, 128).T.copy(),
        lnb_col=asf(inputs["L0_attn_ln_b"]).reshape(8, 128).T.copy(),
        conv_diag=conv_diag,
        convb_col=asf(inputs["L0_attn_conv_b"]).reshape(18, 128).T.copy(),
        dtb_col=asf(inputs["L0_attn_dt_bias"]).reshape(16, 1),
        alog_col=asf(inputs["L0_attn_A_log"]).reshape(16, 1),
        dhead_row=asf(inputs["L0_attn_D"]).reshape(1, 16),
        rmsw_col=asf(inputs["L0_attn_rms_w"]).reshape(16, 128).T.copy(),
        router_wT=asf(inputs["L0_ffn_router_w"]).T.copy(),
        corrb_row=asf(inputs["L0_ffn_correction_bias"]).reshape(1, 16),
        gate_w=asf(inputs["L0_ffn_gate_w"]),
        up_w=asf(inputs["L0_ffn_up_w"]),
        down_w=asf(inputs["L0_ffn_down_w"]),
        sgT=asf(inputs["L0_ffn_shared_gate_w"]).T.copy(),
        suT=asf(inputs["L0_ffn_shared_up_w"]).T.copy(),
        sdT=asf(inputs["L0_ffn_shared_down_w"]).T.copy(),
        i128=np.eye(128, dtype=f),
        fconst=np.tile(np.array([[1e-5, 1.0 + 1e-5, 1e-10, -0.5, -1.0, 1.0, 0.5, 2.0]], f),
                       (128, 1)),
        ones_row=np.ones((1, 512), f),
        triu=tri, tricap=tricap, negd=negd,
    )
    maps = []
    for core in range(n_cores):
        b, h = core // 2, core % 2
        m = dict(com)
        m["x_main"] = x[b, h * T:(h + 1) * T]
        halo = np.zeros((16, DM), f)
        if h == 1:
            halo[0:3] = x[b, T - 3:T]
        m["x_halo"] = halo
        m["mask_col"] = np.full((128, 1), 1.0 if h == 1 else 0.0, f)
        sm = np.zeros((128, 8), f)
        if h == 1:
            sm[:, core - 1] = 1.0
        m["smask"] = sm
        maps.append(m)
    return maps


def kernel(**inputs):
    maps = _layout_inputs(inputs)
    import os
    fused = os.environ.get("KMODE", "fused") == "fused"
    if fused:
        nc, es, c = build_nc(mode="fused", stage="full")
        res = run_bass_kernel_spmd(nc, maps, core_ids=list(range(8)))
        results = res.results
    else:
        ncA, esA, cA = build_nc(mode="twoA", stage="p1b")
        resA = run_bass_kernel_spmd(ncA, maps, core_ids=list(range(8)))
        mapsB = []
        for core in range(8):
            m = dict(maps[core])
            if core % 2 == 1:
                m["s_init_in"] = np.ascontiguousarray(resA.results[core - 1]["s_half_out"])
            else:
                m["s_init_in"] = np.zeros((DI, NST), np.float32)
            mapsB.append(m)
        ncB, esB, cB = build_nc(mode="twoB", stage="full")
        resB = run_bass_kernel_spmd(ncB, mapsB, core_ids=list(range(8)))
        results = resB.results
    out = np.zeros((4, 8192, 1024), np.float32)
    for core in range(8):
        b, h = core // 2, core % 2
        out[b, h * T:(h + 1) * T] = results[core]["out_t"]
    return out


# version 12
# speedup vs baseline: 1.0174x; 1.0174x over previous
import contextlib
import numpy as np
import ml_dtypes
import concourse.bass as bass
import concourse.mybir as mybir
from concourse.bass_utils import run_bass_kernel_spmd

F32 = mybir.dt.float32
BF16 = mybir.dt.bfloat16
AF = mybir.ActivationFunctionType
ALU = mybir.AluOpType
AX = mybir.AxisListType

T = 4096
Q = 128
NCH = T // Q
NH, HD, NST = 16, 128, 128
DM, DI = 1024, 2048
EPS = 1e-5
NEG = -1e30

ENGINES = ("tensor", "vector", "scalar", "gpsimd", "sync")
FCONST_VALS = [EPS, 1.0 + EPS, EPS * EPS, -0.5, -1.0, 1.0, 0.5, 2.0]


class Stream:
    def __init__(self, k, name):
        self.sem = k.es.enter_context(k.nc.semaphore(name))
        self.count = 0

    def tok(self):
        return (self.sem, self.count)


class Eng:
    def __init__(self, k, name):
        self.name = name
        self.ops = []
        self.sem = k.es.enter_context(k.nc.semaphore("prog_" + name))
        self.count = 0
        self.waited = {}


class K:

    def __init__(self, nc, es):
        self.nc = nc
        self.es = es
        self.eng = {n: Eng(self, n) for n in ENGINES}

    def stream(self, name):
        return Stream(self, name)

    def _filt(self, e, deps, keep_self=False):
        out = []
        for (sem, val) in deps:
            if (sem is e.sem and not keep_self) or val <= 0:
                continue
            if e.waited.get(id(sem), -1) >= val:
                continue
            e.waited[id(sem)] = val
            out.append((sem, val))
        return out

    def op(self, engine, fn, deps=(), inc=True):
        e = self.eng[engine]
        mydeps = self._filt(e, deps)
        if inc:
            e.count += 1
        e.ops.append((fn, mydeps, e.sem if inc else None))
        return (e.sem, e.count)

    def tok(self, engine):
        e = self.eng[engine]
        return (e.sem, e.count)

    def dma(self, queue, out, in_, stream, deps=(), **kw):
        stream.count += 16
        e = self.eng[queue]
        mydeps = self._filt(e, deps, keep_self=True)

        def fn(eng):
            return eng.dma_start(out, in_, **kw)

        e.ops.append((fn, mydeps, ("dma", stream.sem)))
        return (stream.sem, stream.count)

    def raw16(self, queue, fn, stream, deps=()):
        stream.count += 16
        e = self.eng[queue]
        mydeps = self._filt(e, deps, keep_self=True)
        e.ops.append((fn, mydeps, ("dma", stream.sem)))
        return (stream.sem, stream.count)

    def fence(self, streams=()):
        toks = [(e.sem, e.count) for e in self.eng.values()]
        toks += [s.tok() for s in streams]
        for name in ENGINES:
            self.op(name, lambda e: e.nop(), deps=toks, inc=True)

    def emit(self):
        block = self.es.enter_context(self.nc.Block())
        for name in ENGINES:
            e = self.eng[name]
            selfsync = name in ("vector", "scalar", "gpsimd")

            def make(e, selfsync):
                def body(eng):
                    n = 0
                    for fn, deps, inc in e.ops:
                        if selfsync and n > 0 and inc is not None and not isinstance(inc, tuple):
                            eng.wait_ge(e.sem, n)
                        for (sem, val) in deps:
                            eng.wait_ge(sem, val)
                        ins = fn(eng)
                        if inc is None:
                            continue
                        if isinstance(ins, (list, tuple)):
                            ins = ins[-1]
                        if isinstance(inc, tuple):
                            ins.then_inc(inc[1], 16)
                        else:
                            ins.then_inc(inc, 1)
                            n += 1
                return body

            getattr(block, name)(make(e, selfsync))


class Ctx:
    pass


def build_nc(mode="fused", stage="full", dbg=False):
    nc = bass.Bass("TRN2", target_bir_lowering=False, detect_race_conditions=False)
    es = contextlib.ExitStack()
    k = K(nc, es)
    c = Ctx()
    c.nc, c.es, c.k, c.mode, c.stage = nc, es, k, mode, stage

    def din(name, shape, dt=F32):
        setattr(c, name, nc.dram_tensor(name, shape, dt, kind="ExternalInput").ap())

    def dout(name, shape, dt=F32):
        setattr(c, name, nc.dram_tensor(name, shape, dt, kind="ExternalOutput").ap())

    def dint(name, shape, dt):
        kind = "ExternalOutput" if dbg else "Internal"
        setattr(c, name, nc.dram_tensor(name, shape, dt, kind=kind).ap())

    din("x_main", [T, DM]); din("x_halo", [16, DM])
    din("mask_col", [128, 1]); din("smask", [128, 8])
    din("w_in", [DM, 4368]); din("w_out", [DI, DM])
    din("lnw_col", [128, 8]); din("lnb_col", [128, 8])
    din("conv_diag", [4, 18, 128, 128]); din("convb_col", [128, 18])
    din("dtb_col", [16, 1]); din("alog_col", [16, 1]); din("dhead_row", [1, 16])
    din("rmsw_col", [128, 16])
    din("router_wT", [DM, 16]); din("corrb_row", [1, 16])
    din("gate_w", [16, DM, 256]); din("up_w", [16, DM, 256]); din("down_w", [16, 256, DM])
    din("sgT", [DM, DI]); din("suT", [DM, DI]); din("sdT", [DI, DM])
    din("i128", [128, 128]); din("ones_row", [1, 512]); din("fconst", [128, 8])
    din("triu", [128, 128]); din("tricap", [128, 512]); din("negd", [16, 2048])
    dout("out_t", [T, DM])
    if dbg:
        dout("dbg0", [128, 160])
        dout("dbg1", [128, 512])
    if mode == "twoA":
        dout("s_half_out", [DI, NST])
    if mode == "twoB":
        din("s_init_in", [DI, NST])
    dint("xn_tm_d", [T, DM], BF16)
    dint("xn2_tm_d", [T, DM], BF16)
    dint("xn2h_tm_d", [16, DM], BF16)
    dint("xbc_fm_d", [2304, T], BF16)
    dint("z_fm_d", [DI, T], BF16)
    dint("h_fm_d", [16 * 256 + DI, T], BF16)
    dint("y_tm_d", [T, DI], BF16)
    dint("g_tm_d", [T, DM], F32)
    dint("sflat_d", [NCH, 2048], F32)
    dint("sfm_d", [NCH, 17, 128], F32)
    dint("sq_d", [NCH, 16], F32)
    dint("comb_fm_d", [16, T], F32)
    if mode == "fused":
        dint("coll_in_d", [DI, NST], F32)
        dint("coll_out_d", [8 * DI, NST], F32)

    SB = {}

    def st(name, shape, dt):
        SB[name] = es.enter_context(nc.sbuf_tensor(name, shape, dt))
        setattr(c, name, SB[name])
        return SB[name]

    st("S_t", [128, DI], F32)
    st("stm_all", [128, NCH * 16], F32)
    st("dttm_all", [128, NCH * 16], F32)
    st("dectm_all", [128, NCH * 16], F32)
    st("comb", [128, NCH * 16], F32)
    st("msq_all", [128, NCH], F32)
    st("rho", [128, NCH], F32); st("nmr", [128, NCH], F32)
    st("rho2", [128, NCH], F32); st("nmr2", [128, NCH], F32)
    st("rstd_all", [128, NCH], F32)
    st("hstat", [16, 8], F32)
    st("i128_t", [128, 128], F32); st("i128b_t", [128, 128], BF16)
    st("ones1", [1, 128], F32)
    st("triu_t", [128, 128], F32); st("tricap_t", [128, 512], F32)
    st("dhi", [128, NH * 128], BF16)
    st("sfm", [17, 128], F32)
    st("lnw_t", [128, 8], F32); st("lnb_t", [128, 8], F32)
    st("maskc", [128, 1], F32); st("smask_t", [128, 8], F32)
    st("dtb_t", [16, 1], F32); st("alog_t", [16, 1], F32); st("acol", [16, 1], F32)
    st("dhead_t", [1, 16], F32); st("corrb_t", [1, 16], F32)
    st("rmsw_t", [128, 16], F32)
    st("convb_t", [128, 18], F32)
    st("fconst_t", [128, 8], F32)
    st("tiny", [128, 64], F32)
    st("tiny16", [16, 128], F32)
    st("AR1", [128, 32768], BF16)
    st("AR2", [128, 32768], BF16)
    st("AR3", [128, 26944], BF16)

    c.ps = [es.enter_context(nc.psum_tensor(f"ps{i}", [128, 512], F32)) for i in range(8)]

    c.s_in = k.stream("s_in")
    c.s_w = k.stream("s_w")
    c.s_w2 = k.stream("s_w2")
    c.s_w3 = k.stream("s_w3")
    c.s_sp = k.stream("s_sp")
    c.s_tr = k.stream("s_tr")
    c.s_g2 = k.stream("s_g2")
    c.s_out = k.stream("s_out")
    c.s_bnc = k.stream("s_bnc")
    c.s_coll = k.stream("s_coll")
    c.s_const = k.stream("s_const")
    c.s_x = [k.stream("s_x0"), k.stream("s_x1")]
    c.s_xh = [k.stream("s_xh0"), k.stream("s_xh1")]
    c.s_pa = [k.stream("s_pa0"), k.stream("s_pa1")]
    c.s_pb = [k.stream("s_pb0"), k.stream("s_pb1")]
    c.s_cr = [k.stream("s_cr0"), k.stream("s_cr1")]
    c.s_we = [k.stream("s_we0"), k.stream("s_we1")]
    c.s_hl = [k.stream("s_hl0"), k.stream("s_hl1")]
    c.s_xl = [k.stream("s_xl0"), k.stream("s_xl1")]
    c.s_gl = [k.stream("s_gl0"), k.stream("s_gl1")]

    def bf(arena, off, n, parts=128):
        return SB[arena][0:parts, off:off + n]

    def f32(arena, off, n, parts=128):
        return SB[arena][0:parts, off:off + 2 * n].bitcast(F32)

    c.bf, c.f32 = bf, f32

    k.dma("sync", SB["ones1"][:], c.ones_row[0:1, 0:128], c.s_const)
    for dst, src in [("i128_t", "i128"), ("triu_t", "triu"),
                     ("tricap_t", "tricap"), ("lnw_t", "lnw_col"), ("lnb_t", "lnb_col"),
                     ("maskc", "mask_col"), ("smask_t", "smask"), ("dtb_t", "dtb_col"),
                     ("alog_t", "alog_col"), ("dhead_t", "dhead_row"), ("corrb_t", "corrb_row"),
                     ("rmsw_t", "rmsw_col"), ("convb_t", "convb_col")]:
        k.dma("sync", SB[dst][:], getattr(c, src), c.s_const)
    k.dma("sync", SB["sfm"][16:17, :], c.ones_row[0:1, 0:128], c.s_in)
    k.dma("sync", SB["fconst_t"][:], c.fconst, c.s_const)
    k.dma("gpsimd", SB["i128b_t"][:], c.i128, c.s_w)
    c.consts = c.s_const.tok()
    for i, v in enumerate(FCONST_VALS):
        nc.const_aps.register(F32, float(v), SB["fconst_t"][:, i:i + 1])
        if float(np.float32(v)) != float(v):
            nc.const_aps.register(F32, float(np.float32(v)), SB["fconst_t"][:, i:i + 1])
    k.fence([c.s_const])
    c.constsw = c.s_w.tok()
    k.op("scalar", lambda e: e.activation(c.acol[:], c.alog_t[:], AF.Exp), deps=[c.consts])
    k.op("vector", lambda e: e.tensor_scalar(c.acol[:], c.acol[:], -1.0, None, ALU.mult),
         deps=[k.tok("scalar")])
    c.acol_tok = k.tok("vector")
    t0 = k.op("tensor", lambda e: e.matmul(c.ps[0][:, 0:16], c.ones1[:],
                                           c.dhead_t[:], start=True, stop=True), deps=[c.consts])
    t1 = k.op("scalar", lambda e: e.activation(c.tiny[:, 0:16], c.ps[0][:, 0:16], AF.Identity),
              deps=[t0])
    for h in range(NH):
        k.op("vector", lambda e, h=h: e.tensor_scalar(
            c.dhi[:, h * 128:(h + 1) * 128], c.i128b_t[:],
            c.tiny[:, h:h + 1], None, ALU.mult), deps=[t1, c.constsw])
    c.dhi_tok = k.tok("vector")

    phase_p0(c)
    if stage != "p0":
        phase_p1a(c)
    if stage not in ("p0", "p1a"):
        phase_p1b(c)
    if mode != "twoA" and stage not in ("p0", "p1a", "p1b"):
        phase_p3z(c)
        if stage != "p3z":
            phase_p3(c)
        if stage not in ("p3z", "p3sh", "p3e", "p3dn"):
            phase_p2a(c)
        if stage == "full":
            phase_p2b(c)
    finish(c)
    k.emit()
    return nc, es, c


def phase_p0(c):
    k = c.k
    XT = [c.f32("AR1", b * 2048, DM) for b in range(4)]
    SQ = c.f32("AR2", 0, DM)
    NT = [c.bf("AR3", b * 1024, DM) for b in range(2)]
    NT2 = [c.bf("AR3", 2048 + b * 1024, DM) for b in range(2)]
    SUMS = c.f32("AR2", 2048, NCH)
    SSQ = c.f32("AR2", 2176, NCH)
    MU = c.f32("AR2", 2304, NCH)
    ld_toks = [None] * NCH
    red_toks = [None] * NCH
    sq_toks = [None] * NCH
    zz = k.op("vector", lambda e: e.memset(SSQ[:], 0.0))
    for i in range(NCH):
        b = i % 4
        dep = [red_toks[i - 4], sq_toks[i - 4]] if i >= 4 else []
        ld = k.dma("sync", XT[b][:], c.x_main[i * Q:(i + 1) * Q, :],
                   (c.s_x + c.s_pa)[b], deps=dep)
        ld_toks[i] = ld
        red_toks[i] = k.op("vector", lambda e, i=i, b=b: e.tensor_reduce(
            SUMS[:, i:i + 1], XT[b][:], AX.X, ALU.add), deps=[ld])
        sq_toks[i] = k.op("scalar", lambda e, i=i, b=b: e.activation(
            SQ[:], XT[b][:], AF.Square, accum_out=SSQ[:, i:i + 1]), deps=[ld, zz])
    tA = k.op("vector", lambda e: e.tensor_scalar(MU[:], SUMS[:], 1.0 / DM, None, ALU.mult),
              deps=[red_toks[-1]])
    tB = k.op("vector", lambda e: e.tensor_scalar(SSQ[:], SSQ[:], 1.0 / DM, None, ALU.mult),
              deps=[sq_toks[-1]])
    tC = k.op("vector", lambda e: e.tensor_tensor(c.rho[:], MU[:], MU[:], ALU.mult))
    tD = k.op("vector", lambda e: e.tensor_tensor(SSQ[:], SSQ[:], c.rho[:], ALU.subtract))
    tE = k.op("vector", lambda e: e.tensor_scalar(c.rho[:], SSQ[:], EPS, None, ALU.add))
    tF = k.op("vector", lambda e: e.tensor_scalar(c.rho2[:], SSQ[:], 1.0 + EPS, EPS * EPS,
                                                  ALU.mult, ALU.add))
    l1 = k.op("scalar", lambda e: e.activation(c.rho[:], c.rho[:], AF.Ln), deps=[tE])
    l2 = k.op("scalar", lambda e: e.activation(c.rho2[:], c.rho2[:], AF.Ln), deps=[tF])
    h1 = k.op("vector", lambda e: e.tensor_scalar(c.rho[:], c.rho[:], -0.5, None, ALU.mult),
              deps=[l1])
    h2 = k.op("vector", lambda e: e.tensor_scalar(c.rho2[:], c.rho2[:], -0.5, None, ALU.mult),
              deps=[l2])
    e1 = k.op("scalar", lambda e: e.activation(c.rho[:], c.rho[:], AF.Exp), deps=[h1])
    e2 = k.op("scalar", lambda e: e.activation(c.rho2[:], c.rho2[:], AF.Exp), deps=[h2])
    n1 = k.op("vector", lambda e: e.tensor_tensor(c.nmr[:], MU[:], c.rho[:], ALU.mult),
              deps=[e1])
    n1b = k.op("vector", lambda e: e.tensor_scalar(c.nmr[:], c.nmr[:], -1.0, None, ALU.mult))
    n2 = k.op("vector", lambda e: e.tensor_tensor(c.nmr2[:], MU[:], c.rho2[:], ALU.mult),
              deps=[e2])
    n2b = k.op("vector", lambda e: e.tensor_scalar(c.nmr2[:], c.nmr2[:], -1.0, None, ALU.mult))
    sp_toks = {}
    for i in range(NCH):
        b = i % 4
        nb = i % 2
        dep = [sp_toks.get(i - 4, None)] if i >= 4 else []
        dep = [t for t in dep if t]
        ld = k.dma("sync", XT[b][:], c.x_main[i * Q:(i + 1) * Q, :],
                   (c.s_x + c.s_pa)[b],
                   deps=dep + ([sq_toks[-1], red_toks[-1]] if i < 4 else []))
        wdep = [c.s_sp.tok()] if i >= 2 else []
        x1 = k.op("scalar", lambda e, i=i, b=b, nb=nb: e.activation(
            NT[nb][:], XT[b][:], AF.Identity, scale=c.rho[:, i:i + 1], bias=c.nmr[:, i:i + 1]),
            deps=[ld, n1b] + wdep)
        k.dma("scalar", c.xn_tm_d[i * Q:(i + 1) * Q, :], NT[nb][:], c.s_sp, deps=[x1])
        x2 = k.op("scalar", lambda e, i=i, b=b, nb=nb: e.activation(
            NT2[nb][:], XT[b][:], AF.Identity, scale=c.rho2[:, i:i + 1], bias=c.nmr2[:, i:i + 1]),
            deps=[n2b])
        sp_toks[i] = k.dma("scalar", c.xn2_tm_d[i * Q:(i + 1) * Q, :], NT2[nb][:], c.s_sp,
                           deps=[x2])
    HX = c.f32("AR2", 4096, DM, 16)
    HS = c.f32("AR2", 8192, DM, 16)
    HB = c.bf("AR2", 12288, DM, 16)
    hld = k.dma("sync", HX[0:16, :], c.x_halo, c.s_in)
    h_z = k.op("vector", lambda e: e.memset(c.hstat[:, 1:2], 0.0))
    h_r = k.op("vector", lambda e: e.tensor_reduce(c.hstat[:, 0:1], HX[0:16, :], AX.X, ALU.add),
               deps=[hld])
    h_q = k.op("scalar", lambda e: e.activation(HS[0:16, :], HX[0:16, :], AF.Square,
                                                accum_out=c.hstat[:, 1:2]), deps=[hld, h_z])
    k.op("vector", lambda e: e.tensor_scalar(c.hstat[:, 2:3], c.hstat[:, 0:1], 1.0 / DM, None,
                                             ALU.mult), deps=[h_r])
    k.op("vector", lambda e: e.tensor_tensor(c.hstat[:, 3:4], c.hstat[:, 2:3], c.hstat[:, 2:3],
                                             ALU.mult))
    k.op("vector", lambda e: e.tensor_scalar(c.hstat[:, 4:5], c.hstat[:, 1:2], 1.0 / DM, None,
                                             ALU.mult), deps=[h_q])
    hv = k.op("vector", lambda e: e.tensor_tensor(c.hstat[:, 4:5], c.hstat[:, 4:5],
                                                  c.hstat[:, 3:4], ALU.subtract))
    he = k.op("vector", lambda e: e.tensor_scalar(c.hstat[:, 5:6], c.hstat[:, 4:5],
                                                  1.0 + EPS, EPS * EPS, ALU.mult, ALU.add),
              deps=[hv])
    hl = k.op("scalar", lambda e: e.activation(c.hstat[:, 5:6], c.hstat[:, 5:6], AF.Ln),
              deps=[he])
    hh = k.op("vector", lambda e: e.tensor_scalar(c.hstat[:, 5:6], c.hstat[:, 5:6],
                                                  -0.5, None, ALU.mult), deps=[hl])
    hr = k.op("scalar", lambda e: e.activation(c.hstat[:, 5:6], c.hstat[:, 5:6], AF.Exp),
              deps=[hh])
    hm = k.op("vector", lambda e: e.tensor_tensor(c.hstat[:, 6:7], c.hstat[:, 5:6],
                                                  c.hstat[:, 2:3], ALU.mult), deps=[hr])
    hm2 = k.op("vector", lambda e: e.tensor_scalar(c.hstat[:, 6:7], c.hstat[:, 6:7], -1.0, None,
                                                   ALU.mult))
    hx2 = k.op("scalar", lambda e: e.activation(HB[0:16, :], HX[0:16, :], AF.Identity,
                                                scale=c.hstat[:, 5:6], bias=c.hstat[:, 6:7]),
               deps=[hm2])
    k.dma("sync", c.xn2h_tm_d, HB[0:16, :], c.s_bnc, deps=[hx2])
    if c.stage == "p0" and hasattr(c, "dbg0"):
        k.dma("sync", c.dbg0[:, 0:32], c.rho[:], c.s_bnc, deps=[k.tok("vector"), k.tok("scalar")])
        k.dma("sync", c.dbg0[:, 32:64], c.nmr[:], c.s_bnc)
        k.dma("sync", c.dbg0[:, 64:96], c.rho2[:], c.s_bnc)
        k.dma("sync", c.dbg0[:, 96:128], c.nmr2[:], c.s_bnc)
        k.dma("sync", c.dbg0[:, 128:160], c.nmr[:], c.s_bnc)


def phase_p1a(c):
    k = c.k
    k.fence([c.s_sp, c.s_in, c.s_w])
    XF = [c.bf("AR1", j * 4096, 4096) for j in range(8)]
    WIN = [c.bf("AR2", j * 2320, 2320) for j in range(8)]
    DIAG = [c.bf("AR2", 18560 + t * 2304, 2304) for t in range(4)]
    XH = [c.bf("AR2", 27776 + j * 16, 16) for j in range(8)]
    XBCR = c.bf("AR3", 0, 18 * 516)
    CSTG = c.bf("AR3", 9288, 18 * 512)
    DTR = c.f32("AR3", 18504, T, 16)
    xbv = c.AR3[:, 0:9288].rearrange("p (j w) -> p j w", w=516)

    xf_tok = []
    for j in range(8):
        xf_tok.append(k.dma("sync", XF[j][:], c.xn2_tm_d[0:T, j * 128:(j + 1) * 128],
                            c.s_tr, transpose=True))
        k.dma("gpsimd", WIN[j][:], c.w_in[j * 128:(j + 1) * 128, 2048:4368], c.s_w)
        k.dma("sync", XH[j][:], c.xn2h_tm_d[0:16, j * 128:(j + 1) * 128], c.s_tr, transpose=True)
    for t in range(4):
        k.dma("gpsimd", DIAG[t][:], c.conv_diag[t].rearrange("j p w -> p j w"), c.s_w)
    wtok = c.s_w.tok()
    trtok = c.s_tr.tok()
    aff = []
    for j in range(8):
        aff.append(k.op("scalar", lambda e, j=j: e.activation(
            XF[j][:], XF[j][:], AF.Identity, scale=c.lnw_t[:, j:j + 1], bias=c.lnb_t[:, j:j + 1]),
            deps=[trtok]))
    aff_tok = k.tok("scalar")
    for j in range(8):
        k.op("scalar", lambda e, j=j: e.activation(
            XH[j][:], XH[j][:], AF.Identity, scale=c.lnw_t[:, j:j + 1], bias=c.lnb_t[:, j:j + 1]))
    hf_tok = k.tok("scalar")
    hev = None
    for j in range(18):
        mm = None
        for kc in range(8):
            mm = k.op("tensor", lambda e, j=j, kc=kc: e.matmul(
                c.ps[4][:, 0:16], WIN[kc][:, j * 128:(j + 1) * 128], XH[kc][:],
                start=(kc == 0), stop=(kc == 7)),
                deps=[hf_tok, wtok] + ([hev] if hev else []), inc=(kc == 7))
        hev = k.op("scalar", lambda e, j=j: e.activation(
            xbv[:, j, 0:3],
            c.ps[4][:, 0:3], AF.Identity, scale=c.maskc[:, 0:1]), deps=[mm, c.consts])
    halo_tok = k.tok("scalar")

    NG = 8
    GW = 512
    jw = [(j * 128, 128) for j in range(18)] + [(2304, 16)]
    conv_done = [None] * NG
    evac_done = [None] * NG
    halo_cp = [None] * NG
    spl_prev = None
    for g in range(NG):
        ev = []
        for jj, (joff, jwid) in enumerate(jw):
            bank = c.ps[jj % 2]
            bdep = [evac_done[g - 1]] if (g >= 1 and jj < 2) else []
            mm = None
            for kc in range(8):
                mm = k.op("tensor", lambda e, jj=jj, joff=joff, jwid=jwid, kc=kc, g=g, bank=bank:
                          e.matmul(bank[0:jwid if jwid < 128 else 128, 0:GW],
                                   WIN[kc][:, joff:joff + jwid],
                                   XF[kc][:, g * GW:(g + 1) * GW],
                                   start=(kc == 0), stop=(kc == 7)),
                          deps=([aff_tok, wtok, halo_tok] + bdep +
                                ([ev[jj - 2]] if jj >= 2 else [])),
                          inc=(kc == 7))
            if jj < 18:
                edep = [mm] + ([halo_cp[g - 1]] if g >= 1 else [])
                e1 = k.op("scalar", lambda e, jj=jj, g=g, bank=bank: e.activation(
                    xbv[:, jj, 3:3 + GW], bank[:, 0:GW], AF.Identity), deps=edep)
            else:
                e1 = k.op("scalar", lambda e, g=g, bank=bank: e.activation(
                    DTR[0:16, g * GW:(g + 1) * GW], bank[0:16, 0:GW], AF.Identity), deps=[mm])
            ev.append(e1)
        evac_done[g] = ev[-1]
        wv_tok = k.tok("scalar")
        cv = []
        for j in range(18):
            bank = c.ps[2 + (j % 2)]
            bdep = [cv[j - 2]] if j >= 2 else ([conv_done[g - 1]] if g >= 1 else [])
            mm = None
            for t in range(4):
                mm = k.op("tensor", lambda e, j=j, t=t, g=g, bank=bank: e.matmul(
                    bank[:, 0:GW], DIAG[t][:, j * 128:(j + 1) * 128],
                    xbv[:, j, t:t + GW], start=(t == 0), stop=(t == 3)),
                    deps=[ev[j], wtok] + bdep, inc=(t == 3))
            cv.append(k.op("scalar", lambda e, j=j, g=g, bank=bank: e.activation(
                CSTG[:, j * 512:j * 512 + GW], bank[:, 0:GW], AF.Silu,
                bias=c.convb_t[:, j:j + 1]),
                deps=[mm] + ([spl_prev] if (j == 0 and spl_prev) else [])))
        conv_done[g] = cv[-1]
        halo_cp[g] = k.op("vector", lambda e: e.tensor_copy(
            xbv[:, :, 0:3], xbv[:, :, 512:515]), deps=[k.tok("tensor")])
        spl_prev = k.dma("scalar",
                         c.xbc_fm_d.rearrange("(j p) t -> p j t", p=128)[:, :, g * GW:(g + 1) * GW],
                         c.AR3[:, 9288:9288 + 9216].rearrange("p (j w) -> p j w", w=512),
                         c.s_sp, deps=[conv_done[g]])


def phase_p1b(c):
    k = c.k
    k.fence([c.s_sp, c.s_w, c.s_tr])
    DTR = c.f32("AR3", 18504, T, 16)
    DTF = c.f32("AR1", 0, T, 16)
    SPT = c.f32("AR1", 8192, T, 16)
    AFM = c.f32("AR1", 8192, T, 16)
    XHT = [c.bf("AR1", 16384 + b * 2048, 2048) for b in range(2)]
    XDC = [c.bf("AR1", 20480 + b * 2048, 2048) for b in range(2)]
    BTM = [c.bf("AR1", 24576 + b * 128, 128) for b in range(2)]
    ATM = c.f32("AR1", 24832, 512)
    SFM = [c.f32("AR2", 0, 128, 17), c.f32("AR2", 256, 128, 17)]
    EQROW = c.f32("AR2", 512, 16, 1)
    EQR = c.f32("AR2", 576, 16)
    DECF = [c.f32("AR2", 640 + b * 256, 128, 16) for b in range(2)]
    STMP = c.f32("AR1", 26112, 2048)

    t1 = k.op("vector", lambda e: e.tensor_scalar(DTF[0:16, :], DTR[0:16, :],
                                                  c.dtb_t[0:16, 0:1], None, ALU.add),
              deps=[c.consts])
    t2 = k.op("scalar", lambda e: e.activation(SPT[0:16, :], DTF[0:16, :], AF.Abs), deps=[t1])
    t2b = k.op("vector", lambda e: e.tensor_scalar(SPT[0:16, :], SPT[0:16, :], -1.0, None,
                                                   ALU.mult), deps=[t2])
    t3 = k.op("scalar", lambda e: e.activation(SPT[0:16, :], SPT[0:16, :], AF.Exp), deps=[t2b])
    t3b = k.op("vector", lambda e: e.tensor_scalar(SPT[0:16, :], SPT[0:16, :], 1.0, None,
                                                   ALU.add), deps=[t3])
    t4 = k.op("scalar", lambda e: e.activation(SPT[0:16, :], SPT[0:16, :], AF.Ln), deps=[t3b])
    t5 = k.op("scalar", lambda e: e.activation(DTF[0:16, :], DTF[0:16, :], AF.Relu), deps=[t2])
    t6 = k.op("vector", lambda e: e.tensor_tensor(DTF[0:16, :], DTF[0:16, :], SPT[0:16, :],
                                                  ALU.add), deps=[t4, t5])
    t7 = k.op("vector", lambda e: e.tensor_scalar(AFM[0:16, :], DTF[0:16, :],
                                                  c.acol[0:16, 0:1], None, ALU.mult),
              deps=[t6, c.acol_tok])
    ev_hist = {}
    for cc in range(NCH):
        bk = c.ps[cc % 2]
        pdep = [ev_hist.get(cc - 2)] if cc >= 2 else []
        pdep = [t for t in pdep if t]
        m1 = k.op("tensor", lambda e, cc=cc, bk=bk: e.matmul(
            bk[:, 0:16], DTF[0:16, cc * Q:(cc + 1) * Q], c.i128_t[0:16, 0:16],
            start=True, stop=True), deps=[t6, c.consts] + pdep)
        s1 = k.op("scalar", lambda e, cc=cc, bk=bk: e.activation(
            c.dttm_all[:, cc * 16:(cc + 1) * 16], bk[:, 0:16], AF.Identity), deps=[m1])
        m2 = k.op("tensor", lambda e, cc=cc, bk=bk: e.matmul(
            bk[:, 16:32], AFM[0:16, cc * Q:(cc + 1) * Q], c.i128_t[0:16, 0:16],
            start=True, stop=True), deps=[t7] + pdep)
        s2 = k.op("scalar", lambda e, cc=cc, bk=bk: e.activation(
            ATM[:, cc * 16:(cc + 1) * 16], bk[:, 16:32], AF.Identity), deps=[m2])
        m3 = k.op("tensor", lambda e, cc=cc, bk=bk: e.matmul(
            bk[:, 32:48], c.triu_t[:], ATM[:, cc * 16:(cc + 1) * 16],
            start=True, stop=True), deps=[s2])
        s3 = k.op("scalar", lambda e, cc=cc, bk=bk: e.activation(
            c.stm_all[:, cc * 16:(cc + 1) * 16], bk[:, 32:48], AF.Identity), deps=[m3])
        ev_hist[cc] = s3
    for b_ in range(2):
        k.dma("sync", SFM[b_][16:17, :], c.ones_row[0:1, 0:128], c.s_const)
    ones_tok = c.s_const.tok()
    sp_hist = {}
    dec_hist = {}
    for cc in range(NCH):
        b = cc % 2
        bk = c.ps[2 + (cc % 2)]
        sfm = SFM[b]
        pdep = [c.s_sp.tok()] if cc >= 2 else []
        m4 = k.op("tensor", lambda e, cc=cc, bk=bk: e.matmul(
            bk[:, 0:128][0:16, :], c.stm_all[:, cc * 16:(cc + 1) * 16], c.i128_t[:],
            start=True, stop=True), deps=[ev_hist[cc]] + ([dec_hist.get(cc - 2)] if cc >= 2 else []))
        s4 = k.op("scalar", lambda e, cc=cc, bk=bk, sfm=sfm: e.activation(
            sfm[0:16, :], bk[:, 0:128][0:16, :], AF.Identity),
            deps=[m4, c.consts] + pdep)
        sp1 = k.dma("scalar", c.sfm_d[cc], sfm[0:17, :], c.s_sp, deps=[s4, ones_tok])
        sp2 = k.dma("scalar", c.sflat_d[cc:cc + 1, :].rearrange("a (h t) -> (a h) t", h=16),
                    sfm[0:16, :], c.s_sp, deps=[s4])
        sp_hist[cc] = k.dma("scalar", c.sq_d[cc:cc + 1, :].rearrange("a h -> h a"),
                            sfm[0:16, 127:128], c.s_sp, deps=[s4])
        d0 = k.op("vector", lambda e, b=b, sfm=sfm: e.tensor_scalar(
            DECF[b][0:16, :], sfm[0:16, :], -1.0, None, ALU.mult),
            deps=[s4] + ([dec_hist.get(cc - 2)] if cc >= 2 else []))
        d0b = k.op("vector", lambda e, b=b, sfm=sfm: e.tensor_scalar(
            DECF[b][0:16, :], DECF[b][0:16, :], sfm[0:16, 127:128], None, ALU.add))
        d1 = k.op("scalar", lambda e, b=b: e.activation(DECF[b][0:16, :], DECF[b][0:16, :],
                                                        AF.Exp), deps=[d0b])
        m6 = k.op("tensor", lambda e, b=b, bk=bk: e.matmul(
            bk[:, 128:144], DECF[b][0:16, :], c.i128_t[0:16, 0:16], start=True, stop=True),
            deps=[d1])
        d2 = k.op("scalar", lambda e, cc=cc, bk=bk: e.activation(
            c.dectm_all[:, cc * 16:(cc + 1) * 16], bk[:, 128:144], AF.Identity), deps=[m6])
        dec_hist[cc] = k.op("vector", lambda e, cc=cc: e.tensor_tensor(
            c.dectm_all[:, cc * 16:(cc + 1) * 16], c.dectm_all[:, cc * 16:(cc + 1) * 16],
            c.dttm_all[:, cc * 16:(cc + 1) * 16], ALU.mult), deps=[d2])
    dec_all = k.tok("vector")
    sq_all = c.s_sp.tok()
    sz = k.op("vector", lambda e: e.memset(c.S_t[:], 0.0))
    upd_tok = sz
    add_hist = {}
    xd_hist = {}
    pe_hist = {}
    q2_hist = {}
    xput = [None, None]
    for cc in range(NCH):
        b = cc % 2
        ldep = [t for t in (pe_hist.get(cc - 2), xd_hist.get(cc - 2)) if t]
        k.dma("sync", XHT[b][:], c.xbc_fm_d[0:2048, cc * Q:(cc + 1) * Q], c.s_xh[b],
              transpose=True, deps=ldep)
        xput[b] = k.dma("sync", BTM[b][:], c.xbc_fm_d[2048:2176, cc * Q:(cc + 1) * Q],
                        c.s_xh[b], transpose=True)
        xd = k.op("gpsimd", lambda e, cc=cc, b=b: e.tensor_tensor(
            XDC[b][:].rearrange("p (h w) -> p h w", w=128),
            XHT[b][:].rearrange("p (h w) -> p h w", w=128),
            c.dectm_all[:, cc * 16:(cc + 1) * 16].unsqueeze(2).broadcast_to((128, 16, 128)),
            ALU.mult), deps=[dec_all, xput[b]] + ([pe_hist.get(cc - 2)] if cc >= 2 else []))
        xd_hist[cc] = xd
        lq = k.dma("sync", EQROW[0:1, :], c.sq_d[cc:cc + 1, :], c.s_bnc,
                   deps=[sq_all] + [t for t in (pe_hist.get(cc - 1),) if t])
        q1 = k.op("scalar", lambda e: e.activation(EQROW[0:1, :], EQROW[0:1, :], AF.Exp),
                  deps=[lq] + [t for t in (q2_hist.get(cc - 1),) if t])
        m5 = k.op("tensor", lambda e: e.matmul(c.ps[6][:, 0:16], c.ones1[:],
                                               EQROW[0:1, :], start=True, stop=True),
                  deps=[q1] + [t for t in (q2_hist.get(cc - 1),) if t])
        q2 = k.op("scalar", lambda e: e.activation(EQR[:, 0:16], c.ps[6][:, 0:16],
                                                   AF.Identity), deps=[m5])
        q2_hist[cc] = q2
        mm = None
        for h in range(16):
            pt = c.ps[2 + h // 4]
            mm = k.op("tensor", lambda e, h=h, pt=pt, b=b: e.matmul(
                pt[:, (h % 4) * 128:(h % 4 + 1) * 128], BTM[b][:],
                XDC[b][:, h * 128:(h + 1) * 128], start=True, stop=True),
                deps=[xd, xput[b]] + ([add_hist.get(cc - 1)] if cc >= 1 else []),
                inc=(h % 4 == 3))
        mu0 = k.op("vector", lambda e, cc=cc: e.tensor_tensor(
            STMP[:, 0:1024].rearrange("p (h w) -> p h w", w=128),
            c.S_t[:, 0:1024].rearrange("p (h w) -> p h w", w=128),
            EQR[:, 0:8].unsqueeze(2).broadcast_to((128, 8, 128)), ALU.mult),
            deps=[q2, upd_tok])
        mu1 = k.op("gpsimd", lambda e, cc=cc: e.tensor_tensor(
            STMP[:, 1024:2048].rearrange("p (h w) -> p h w", w=128),
            c.S_t[:, 1024:2048].rearrange("p (h w) -> p h w", w=128),
            EQR[:, 8:16].unsqueeze(2).broadcast_to((128, 8, 128)), ALU.mult),
            deps=[q2, upd_tok])
        aa = None
        for j in range(4):
            aa = k.op("vector", lambda e, j=j: e.tensor_tensor(
                c.S_t[:, j * 512:(j + 1) * 512], STMP[:, j * 512:(j + 1) * 512],
                c.ps[2 + j][:], ALU.add), deps=[mu0 if j < 2 else mu1, mm])
        add_hist[cc] = aa
        upd_tok = aa
        pe_hist[cc] = k.tok("tensor")
    if c.mode == "twoA":
        k.dma("sync", c.s_half_out.rearrange("(h n) p -> n h p", n=128),
              c.S_t[:].rearrange("n (h p) -> n h p", p=128), c.s_out, deps=[upd_tok])
    elif c.mode == "fused":
        sp = k.dma("sync", c.coll_in_d.rearrange("(h n) p -> n h p", n=128),
                   c.S_t[:].rearrange("n (h p) -> n h p", p=128), c.s_bnc, deps=[upd_tok])

        def coll(e):
            return e.collective_compute(
                "AllGather", ALU.bypass, [[0, 1, 2, 3, 4, 5, 6, 7]],
                [c.coll_in_d.opt()], [c.coll_out_d.opt()]).then_inc(c.s_coll.sem, 1)
        c.s_coll.count += 1
        k.op("gpsimd", coll, deps=[sp], inc=False)


def phase_p3z(c):
    k = c.k
    k.fence([c.s_sp, c.s_tr, c.s_w, c.s_bnc])
    XF = [c.bf("AR1", j * 4096, 4096) for j in range(8)]
    WZ = [c.bf("AR2", j * 2048, 2048) for j in range(8)]
    ZST = c.bf("AR3", 0, 16 * 512)
    for j in range(8):
        k.dma("sync", XF[j][:], c.xn2_tm_d[0:T, j * 128:(j + 1) * 128], c.s_tr, transpose=True)
        k.dma("gpsimd", WZ[j][:], c.w_in[j * 128:(j + 1) * 128, 0:2048], c.s_w)
    trtok, wtok = c.s_tr.tok(), c.s_w.tok()
    for j in range(8):
        k.op("scalar", lambda e, j=j: e.activation(
            XF[j][:], XF[j][:], AF.Identity, scale=c.lnw_t[:, j:j + 1], bias=c.lnb_t[:, j:j + 1]),
            deps=[trtok])
    aff_tok = k.tok("scalar")
    ev = []
    sp_prev = None
    for g in range(8):
        for j in range(16):
            bank = c.ps[j % 2]
            mm = None
            for kc in range(8):
                mm = k.op("tensor", lambda e, j=j, kc=kc, g=g, bank=bank: e.matmul(
                    bank[:, 0:512], WZ[kc][:, j * 128:(j + 1) * 128],
                    XF[kc][:, g * 512:(g + 1) * 512], start=(kc == 0), stop=(kc == 7)),
                    deps=[aff_tok, wtok] + ([ev[-2]] if len(ev) >= 2 else []), inc=(kc == 7))
            ev.append(k.op("scalar", lambda e, j=j, bank=bank: e.activation(
                ZST[:, j * 512:(j + 1) * 512], bank[:, 0:512], AF.Silu),
                deps=[mm] + ([sp_prev] if (j < 2 and sp_prev) else [])))
        sp_prev = k.dma("scalar", c.z_fm_d.rearrange("(j p) t -> p j t", p=128)[:, :, g * 512:(g + 1) * 512],
                        ZST[:].rearrange("p (j w) -> p j w", w=512), c.s_sp, deps=[ev[-1]])


def phase_p3(c):
    k = c.k
    k.fence([c.s_sp, c.s_tr, c.s_w])
    XF = [c.bf("AR1", j * 4096, 4096) for j in range(8)]
    for j in range(8):
        k.dma("sync", XF[j][:], c.xn_tm_d[0:T, j * 128:(j + 1) * 128], c.s_tr, transpose=True)
    xn_tok = c.s_tr.tok()
    RW = [c.bf("AR3", 26816 + j * 16, 16) for j in range(8)]
    for j in range(8):
        k.dma("gpsimd", RW[j][:], c.router_wT[j * 128:(j + 1) * 128, :], c.s_w)
    rw_tok = c.s_w.tok()
    lg_ev = [None, None]
    for i in range(NCH):
        bank = c.ps[i % 2]
        mm = None
        for kc in range(8):
            mm = k.op("tensor", lambda e, i=i, kc=kc, bank=bank: e.matmul(
                bank[:, 0:16], XF[kc][:, i * 128:(i + 1) * 128], RW[kc][:],
                start=(kc == 0), stop=(kc == 7)),
                deps=[xn_tok, rw_tok] + ([lg_ev[i % 2]] if lg_ev[i % 2] else []),
                inc=(kc == 7))
        lg_ev[i % 2] = k.op("scalar", lambda e, i=i, bank=bank: e.activation(
            c.comb[:, i * 16:(i + 1) * 16], bank[:, 0:16], AF.Identity), deps=[mm])
    lg_tok = k.tok("scalar")
    _routing_elementwise(c, lg_tok)
    cb_ev = None
    cb_sp = None
    for i in range(NCH):
        m = k.op("tensor", lambda e, i=i: e.matmul(
            c.ps[2][:, 0:128][0:16, :], c.comb[:, i * 16:(i + 1) * 16], c.i128_t[:],
            start=True, stop=True), deps=[k.tok("vector")] + ([cb_ev] if cb_ev else []))
        cb_ev = k.op("scalar", lambda e, i=i: e.activation(c.tiny16[0:16, :],
                                                           c.ps[2][:, 0:128][0:16, :],
                                                           AF.Identity),
                     deps=[m] + ([cb_sp] if cb_sp else []))
        cb_sp = k.dma("scalar", c.comb_fm_d[0:16, i * 128:(i + 1) * 128], c.tiny16[0:16, :],
                      c.s_sp, deps=[cb_ev])
    comb_sp_tok = cb_sp
    route_free = k.tok("vector")
    SGT = [c.bf("AR2", j * 2048, 2048) for j in range(8)]
    SUT = [c.bf("AR2", 16384 + j * 2048, 2048) for j in range(8)]
    for j in range(8):
        k.dma("gpsimd", SGT[j][:], c.sgT[j * 128:(j + 1) * 128, :], c.s_w)
        k.dma("gpsimd", SUT[j][:], c.suT[j * 128:(j + 1) * 128, :], c.s_w)
    sw_tok = c.s_w.tok()
    su_tok = sw_tok
    HST = c.bf("AR3", 0, 16 * 512)
    TMP = [c.f32("AR3", 8192 + t * 1024, 512) for t in range(2)]
    sp_prev = None
    ev_hist = {}
    nj = 0
    for g in range(8):
        ev_last = None
        for j in range(16):
            pg, pu = c.ps[4 + 2 * (j % 2)], c.ps[5 + 2 * (j % 2)]
            mm_g = mm_u = None
            pdep = [ev_hist.get(nj - 2)] if ev_hist.get(nj - 2) else []
            for kc in range(8):
                mm_g = k.op("tensor", lambda e, j=j, kc=kc, g=g, pg=pg: e.matmul(
                    pg[:, 0:512], SGT[kc][:, j * 128:(j + 1) * 128],
                    XF[kc][:, g * 512:(g + 1) * 512], start=(kc == 0), stop=(kc == 7)),
                    deps=[sw_tok, xn_tok] + pdep, inc=(kc == 7))
            for kc in range(8):
                mm_u = k.op("tensor", lambda e, j=j, kc=kc, g=g, pu=pu: e.matmul(
                    pu[:, 0:512], SUT[kc][:, j * 128:(j + 1) * 128],
                    XF[kc][:, g * 512:(g + 1) * 512], start=(kc == 0), stop=(kc == 7)),
                    deps=[su_tok] + pdep, inc=(kc == 7))
            tmp = TMP[j % 2]
            s1 = k.op("scalar", lambda e, tmp=tmp, pg=pg: e.activation(tmp[:], pg[:], AF.Silu),
                      deps=[mm_g] + pdep)
            ev_last = k.op("vector", lambda e, j=j, tmp=tmp, pu=pu: e.tensor_tensor(
                HST[:, j * 512:(j + 1) * 512], pu[:], tmp[:], ALU.mult),
                deps=[s1, mm_u] + ([sp_prev] if (j < 1 and sp_prev) else []))
            ev_hist[nj] = ev_last
            nj += 1
        sp_prev = k.dma("scalar",
                        c.h_fm_d.rearrange("(j p) t -> p j t", p=128)[:, 32:48, g * 512:(g + 1) * 512],
                        HST[:].rearrange("p (j w) -> p j w", w=512), c.s_sp, deps=[ev_last])
    GWB = [[c.bf("AR2", b * 8192 + j * 256, 256) for j in range(8)] for b in range(2)]
    UWB = [[c.bf("AR2", 16384 + b * 8192 + j * 256, 256) for j in range(2 * 4)] for b in range(2)]
    CRP = [c.f32("AR3", 10240 + b * 1024, 512) for b in range(2)]
    TMPE = [c.f32("AR3", 12288 + t * 1024, 512) for t in range(2)]
    HS2 = c.bf("AR3", 0, 2 * 512)
    sh_sp_tok = sp_prev
    sh_pe_tok = k.tok("tensor")
    pair_free = [ev_hist.get(nj - 2), ev_hist.get(nj - 1)]
    wtoks = [None, None]
    ctoks = [None, None]
    sp_toks = {}
    pe_ev = None
    for e_i in range(16):
        b = e_i % 2
        wdep = [c.s_sp.tok()] if e_i >= 2 else [sh_pe_tok]
        k.dma("gpsimd", c.AR2[:, b * 8192:b * 8192 + 2048].rearrange(
            "p (kc j) -> p kc j", j=256),
            c.gate_w[e_i].rearrange("(kc p) j -> p kc j", p=128), c.s_we[b], deps=wdep)
        wtoks[b] = k.dma("gpsimd", c.AR2[:, 16384 + b * 8192:16384 + b * 8192 + 2048].rearrange(
            "p (kc j) -> p kc j", j=256),
            c.up_w[e_i].rearrange("(kc p) j -> p kc j", p=128), c.s_we[b])
        for g in range(8):
            bb = g % 2
            cdep = [c.s_sp.tok()]
            ctoks[bb] = k.dma("sync", CRP[bb][:],
                              c.comb_fm_d[e_i:e_i + 1, g * 512:(g + 1) * 512].broadcast_to((128, 512)),
                              c.s_cr[bb], deps=cdep + [comb_sp_tok, route_free])
            v2 = None
            for jj in range(2):
                pg, pu = c.ps[4 + 2 * jj], c.ps[5 + 2 * jj]
                mm_g = mm_u = None
                pbank = [t for t in (pair_free[jj],) if t]
                for kc in range(8):
                    mm_g = k.op("tensor", lambda e, jj=jj, kc=kc, g=g, b=b, pg=pg: e.matmul(
                        pg[:, 0:512],
                        c.AR2[:, b * 8192 + kc * 256 + jj * 128: b * 8192 + kc * 256 + jj * 128 + 128],
                        XF[kc][:, g * 512:(g + 1) * 512], start=(kc == 0), stop=(kc == 7)),
                        deps=[wtoks[b], xn_tok] + pbank, inc=(kc == 7))
                for kc in range(8):
                    mm_u = k.op("tensor", lambda e, jj=jj, kc=kc, g=g, b=b, pu=pu: e.matmul(
                        pu[:, 0:512],
                        c.AR2[:, 16384 + b * 8192 + kc * 256 + jj * 128: 16384 + b * 8192 + kc * 256 + jj * 128 + 128],
                        XF[kc][:, g * 512:(g + 1) * 512], start=(kc == 0), stop=(kc == 7)),
                        deps=pbank, inc=(kc == 7))
                tmp = TMPE[jj]
                s1 = k.op("scalar", lambda e, tmp=tmp, pg=pg: e.activation(tmp[:], pg[:], AF.Silu),
                          deps=[mm_g] +
                          ([sh_sp_tok] if (e_i == 0 and g == 0 and jj == 0) else []))
                v1 = k.op("vector", lambda e, tmp=tmp, pu=pu: e.tensor_tensor(
                    tmp[:], pu[:], tmp[:], ALU.mult), deps=[s1, mm_u])
                v2 = k.op("vector", lambda e, jj=jj, bb=bb, tmp=tmp: e.tensor_tensor(
                    HS2[:, jj * 512:(jj + 1) * 512], tmp[:], CRP[bb][:], ALU.mult),
                    deps=[v1, ctoks[bb]] +
                    ([sh_sp_tok] if (e_i == 0 and g == 0 and jj == 0) else []))
                pair_free[jj] = v2
            pe_ev = k.tok("vector")
            sp_toks[(e_i, g)] = k.dma(
                "scalar",
                c.h_fm_d.rearrange("(j p) t -> p j t", p=128)[:, 2 * e_i:2 * e_i + 2,
                                                              g * 512:(g + 1) * 512],
                HS2[:].rearrange("p (j w) -> p j w", w=512), c.s_sp,
                deps=[pe_ev])
    dwt = {}
    wqs = [c.s_w, c.s_w2, c.s_w3]
    qi = 0
    for e_i in range(16):
        if e_i % 4 == 0:
            continue
        dwt[e_i] = k.dma("gpsimd",
                         c.AR2[:, e_i * 2048:(e_i + 1) * 2048].rearrange("p (kc d) -> p kc d", d=1024),
                         c.down_w[e_i].rearrange("(kc p) d -> p kc d", p=128),
                         wqs[qi % 3], deps=[sh_pe_tok])
        qi += 1
    k.fence([c.s_sp, c.s_g2])
    DW = [c.bf("AR2", i * 1024, 1024) for i in range(32)]
    SD = [c.bf("AR1", i * 1024, 1024) for i in range(16)]
    for e_i in (0, 4, 8, 12):
        dwt[e_i] = k.dma("gpsimd",
                         c.AR2[:, e_i * 2048:(e_i + 1) * 2048].rearrange("p (kc d) -> p kc d", d=1024),
                         c.down_w[e_i].rearrange("(kc p) d -> p kc d", p=128), wqs[e_i // 4 % 3])
    sdt = {}
    for j in range(16):
        sdt[j] = k.dma("gpsimd", SD[j][:], c.sdT[j * 128:(j + 1) * 128, :], wqs[j % 3])
    HLD = [c.bf("AR3", b * 6144, 6144) for b in range(2)]
    GT = c.f32("AR3", 12288, 1024)
    htoks = [None, None]
    gsp = None
    gev = [None] * NCH
    pe_hist = [None] * NCH
    for g in range(NCH):
        b = g % 2
        hdep = [pe_hist[g - 2]] if g >= 2 and pe_hist[g - 2] else []
        htoks[b] = k.dma("sync", HLD[b][:].rearrange("p (j w) -> p j w", w=128),
                         c.h_fm_d.rearrange("(j p) t -> p j t", p=128)[:, :, g * Q:(g + 1) * Q],
                         c.s_hl[b], deps=hdep)
        pd = [c.ps[0 + 2 * (g % 2)], c.ps[1 + 2 * (g % 2)]]
        mm = None
        for half in range(2):
            first = True
            for j in range(48):
                src = (c.AR2[:, (j // 2) * 2048 + (j % 2) * 1024 + half * 512:
                             (j // 2) * 2048 + (j % 2) * 1024 + half * 512 + 512]
                       if j < 32 else SD[j - 32][:, half * 512:half * 512 + 512])
                bankdep = [gev[g - 2]] if (first and g >= 2 and gev[g - 2]) else []
                wtok = dwt[j // 2] if j < 32 else sdt[j - 32]
                mm = k.op("tensor", lambda e, j=j, half=half, b=b, src=src, pd=pd, first=first:
                          e.matmul(pd[half][:, 0:512],
                                   HLD[b][:, j * 128:(j + 1) * 128], src,
                                   start=first, stop=(j == 47)),
                          deps=[htoks[b], wtok] + bankdep, inc=(j == 47))
                first = False
        pe_hist[g] = k.tok("tensor")
        gdep = [gsp] if gsp else []
        e1 = k.op("scalar", lambda e, pd=pd: e.activation(GT[:, 0:512], pd[0][:], AF.Silu),
                  deps=[mm] + gdep)
        e2 = k.op("scalar", lambda e, pd=pd: e.activation(GT[:, 512:1024], pd[1][:], AF.Silu))
        gev[g] = e2
        gsp = k.dma("scalar", c.g_tm_d[g * Q:(g + 1) * Q, :], GT[:], c.s_sp, deps=[e2])


def _routing_elementwise(c, lg_tok):
    k = c.k
    SC = c.f32("AR3", 0, 512)
    SBv = c.f32("AR3", 1024, 512)
    GS = c.f32("AR3", 2048, 128)
    GM = c.f32("AR3", 2304, 128)
    E1 = c.f32("AR3", 2560, 128)
    E2 = c.f32("AR3", 2816, 128)
    MK = c.f32("AR3", 3072, 512)
    SEL = c.f32("AR3", 4096, 512)
    EQ = c.f32("AR3", 5120, 512)
    CU = c.f32("AR3", 6144, 512)
    CV = c.f32("AR3", 7168, 512)
    MX = c.f32("AR3", 8192, 32)
    CB = c.f32("AR3", 8256, 16)
    WS = c.f32("AR3", 8288, 32)
    t0 = k.op("vector", lambda e: e.tensor_scalar(SC[:], c.comb[:], 0.5, None, ALU.mult),
              deps=[lg_tok])
    t1 = k.op("scalar", lambda e: e.activation(SC[:], SC[:], AF.Tanh), deps=[t0])
    t2 = k.op("vector", lambda e: e.tensor_scalar(SC[:], SC[:], 0.5, 0.5, ALU.mult, ALU.add),
              deps=[t1])
    m0 = k.op("tensor", lambda e: e.matmul(c.ps[3][:, 0:16], c.ones1[:], c.corrb_t[:],
                                           start=True, stop=True), deps=[c.consts])
    c0 = k.op("scalar", lambda e: e.activation(CB[:, 0:16], c.ps[3][:, 0:16], AF.Identity),
              deps=[m0])
    t3 = k.op("vector", lambda e: e.tensor_tensor(
        SBv[:].rearrange("p (a b) -> p a b", b=16), SC[:].rearrange("p (a b) -> p a b", b=16),
        CB[:, 0:16].unsqueeze(1).broadcast_to((128, 32, 16)), ALU.add), deps=[t2, c0])
    V = SBv.rearrange("p (a b) -> p a b", b=4)
    first = True
    for i in range(4):
        for j in range(i + 1, 4):
            if first:
                k.op("vector", lambda e, i=i, j=j: e.tensor_tensor(GS[:], V[:, :, i], V[:, :, j],
                                                                   ALU.add), deps=[t3])
                first = False
            else:
                k.op("vector", lambda e, i=i, j=j: e.tensor_tensor(CU[:, 0:128], V[:, :, i],
                                                                   V[:, :, j], ALU.add))
                k.op("vector", lambda e: e.tensor_tensor(GS[:], GS[:], CU[:, 0:128], ALU.max))
    k.op("vector", lambda e: e.memset(GM[:], 0.0))
    GSv = GS.rearrange("p (a b) -> p a b", b=4)
    for it in range(2):
        k.op("vector", lambda e: e.tensor_reduce(MX[:].rearrange("p (a b) -> p a b", b=1),
                                                 GSv, AX.X, ALU.max))
        k.op("vector", lambda e: e.tensor_tensor(
            E1[:].rearrange("p (a b) -> p a b", b=4), GSv,
            MX[:].unsqueeze(2).broadcast_to((128, 32, 4)), ALU.is_equal))
        k.op("vector", lambda e: e.tensor_copy(E2[:], E1[:]))
        k.op("vector", lambda e: e.tensor_tensor(
            E2[:].rearrange("p (a b) -> p a b", b=4)[:, :, 1:4],
            E1[:].rearrange("p (a b) -> p a b", b=4)[:, :, 1:4],
            E1[:].rearrange("p (a b) -> p a b", b=4)[:, :, 0:3], ALU.add))
        k.op("vector", lambda e: e.tensor_copy(CU[:, 0:128], E2[:]))
        k.op("vector", lambda e: e.tensor_tensor(
            CU[:, 0:128].rearrange("p (a b) -> p a b", b=4)[:, :, 2:4],
            E2[:].rearrange("p (a b) -> p a b", b=4)[:, :, 2:4],
            E2[:].rearrange("p (a b) -> p a b", b=4)[:, :, 0:2], ALU.add))
        k.op("vector", lambda e: e.tensor_tensor(CU[:, 0:128], CU[:, 0:128], E1[:],
                                                 ALU.subtract))
        k.op("vector", lambda e: e.tensor_scalar(CU[:, 0:128], CU[:, 0:128], 0.0, None,
                                                 ALU.is_equal))
        k.op("vector", lambda e: e.tensor_tensor(E1[:], E1[:], CU[:, 0:128], ALU.mult))
        k.op("vector", lambda e: e.tensor_tensor(GM[:], GM[:], E1[:], ALU.add))
        if it == 0:
            k.op("vector", lambda e: e.scalar_tensor_tensor(GS[:], E1[:], NEG, GS[:],
                                                            ALU.mult, ALU.add))
    k.op("vector", lambda e: e.tensor_scalar(GM[:], GM[:], -1.0, NEG, ALU.add, ALU.mult))
    k.op("vector", lambda e: e.tensor_scalar(GM[:], GM[:], -1.0, None, ALU.mult))
    k.op("vector", lambda e: e.tensor_tensor(
        MK[:].rearrange("p (a b) -> p a b", b=4), SBv.rearrange("p (a b) -> p a b", b=4),
        GM[:].unsqueeze(2).broadcast_to((128, 128, 4)), ALU.add))
    k.op("vector", lambda e: e.memset(SEL[:], 0.0))
    MKv = MK.rearrange("p (a b) -> p a b", b=16)
    for it in range(4):
        k.op("vector", lambda e: e.tensor_reduce(MX[:].rearrange("p (a b) -> p a b", b=1),
                                                 MKv, AX.X, ALU.max))
        k.op("vector", lambda e: e.tensor_tensor(
            EQ[:].rearrange("p (a b) -> p a b", b=16), MKv,
            MX[:].unsqueeze(2).broadcast_to((128, 32, 16)), ALU.is_equal))
        k.op("vector", lambda e: e.tensor_copy(CU[:], EQ[:]))
        src, dst = CU, CV
        for sh in (1, 2, 4, 8):
            k.op("vector", lambda e, sh=sh, src=src, dst=dst: e.tensor_copy(
                dst[:].rearrange("p (a b) -> p a b", b=16)[:, :, 0:sh],
                src[:].rearrange("p (a b) -> p a b", b=16)[:, :, 0:sh]))
            k.op("vector", lambda e, sh=sh, src=src, dst=dst: e.tensor_tensor(
                dst[:].rearrange("p (a b) -> p a b", b=16)[:, :, sh:16],
                src[:].rearrange("p (a b) -> p a b", b=16)[:, :, sh:16],
                src[:].rearrange("p (a b) -> p a b", b=16)[:, :, 0:16 - sh], ALU.add))
            src, dst = dst, src
        k.op("vector", lambda e, src=src: e.tensor_tensor(CV[:] if src is CU else CU[:],
                                                          src[:], EQ[:], ALU.subtract))
        exv = CV if src is CU else CU
        k.op("vector", lambda e, exv=exv: e.tensor_scalar(exv[:], exv[:], 0.0, None,
                                                          ALU.is_equal))
        k.op("vector", lambda e, exv=exv: e.tensor_tensor(EQ[:], EQ[:], exv[:], ALU.mult))
        k.op("vector", lambda e: e.tensor_tensor(SEL[:], SEL[:], EQ[:], ALU.add))
        if it < 3:
            k.op("vector", lambda e: e.scalar_tensor_tensor(MK[:], EQ[:], NEG, MK[:],
                                                            ALU.mult, ALU.add))
    k.op("vector", lambda e: e.tensor_tensor(SEL[:], SEL[:], SC[:], ALU.mult))
    k.op("vector", lambda e: e.tensor_reduce(WS[:].rearrange("p (a b) -> p a b", b=1),
                                             SEL[:].rearrange("p (a b) -> p a b", b=16),
                                             AX.X, ALU.add))
    k.op("vector", lambda e: e.tensor_scalar(WS[:], WS[:], 1e-20, None, ALU.add))
    k.op("vector", lambda e: e.reciprocal(WS[:], WS[:]))
    k.op("vector", lambda e: e.tensor_tensor(
        c.comb[:].rearrange("p (a b) -> p a b", b=16),
        SEL[:].rearrange("p (a b) -> p a b", b=16),
        WS[:].unsqueeze(2).broadcast_to((128, 32, 16)), ALU.mult))


import os as _os
DBG_NODEFER = bool(int(_os.environ.get("DBG_NODEFER", "0")))


def phase_p2a(c):
    k = c.k
    if c.mode != "twoB":
        SG2 = [c.f32("AR3", 14336 + b_ * 2048, 1024) for b_ in range(2)]
        init_tok = k.op("vector", lambda e: e.memset(c.S_t[:], 0.0))
        sg_toks = {}
        for idx in range(16):
            r, half = idx // 2, idx % 2
            b2 = idx % 2
            ld = k.dma("sync", SG2[b2][:].rearrange("n (h p) -> n h p", p=128),
                       c.coll_out_d[r * DI + half * 1024:r * DI + (half + 1) * 1024]
                       .rearrange("(h n) p -> n h p", n=128),
                       c.s_in, deps=[(c.s_coll.sem, c.s_coll.count), init_tok] +
                       ([sg_toks[idx - 2]] if idx >= 2 else []))
            sg_toks[idx] = k.op("vector", lambda e, r=r, half=half, b2=b2:
                                e.scalar_tensor_tensor(
                c.S_t[:, half * 1024:(half + 1) * 1024], SG2[b2][:],
                c.smask_t[:, r:r + 1], c.S_t[:, half * 1024:(half + 1) * 1024],
                ALU.mult, ALU.add), deps=[ld, c.consts])
        init_tok = sg_toks[15]
    k.fence([c.s_sp, c.s_tr, c.s_w, c.s_g2, c.s_bnc])
    XHT = [c.bf("AR1", b * 2048, 2048) for b in range(2)]
    SZT = [c.bf("AR1", 4096 + b * 2048, 2048) for b in range(2)]
    LL = [c.bf("AR1", 8192 + b * 2048, 2048) for b in range(2)]
    MT = [c.bf("AR1", 12288 + b * 2048, 2048) for b in range(2)]
    XTI = [c.bf("AR1", 16384 + b * 2048, 2048) for b in range(2)]
    XDC = [c.bf("AR1", 20480 + b * 2048, 2048) for b in range(2)]
    SBF = c.bf("AR1", 24576, 2048)
    YTM = c.f32("AR1", 26624, 2048)
    YBF = [c.bf("AR2", b * 2048, 2048) for b in range(2)]
    BTM = [c.bf("AR2", 4096 + b * 128, 128) for b in range(2)]
    BFM = [c.bf("AR2", 4352 + b * 128, 128) for b in range(2)]
    CFM = [c.bf("AR2", 4608 + b * 128, 128) for b in range(2)]
    GSB = [c.f32("AR2", 4864 + b * 256, 128) for b in range(2)]
    S17 = [c.f32("AR2", 5376 + b * 256, 128, 17) for b in range(2)]
    ECOL = [c.f32("AR2", 5888 + b * 32, 16) for b in range(2)]
    EQR = [c.f32("AR2", 5952 + b * 32, 16) for b in range(2)]
    EQROW = [c.f32("AR2", 6016 + b * 32, 16, 1) for b in range(2)]
    T2E = [c.bf("AR2", 6080, 512), c.bf("AR2", 6592, 512)]
    DMS = [c.f32("AR2", 7104 + b * 4096, 2048) for b in range(2)]
    Y2 = c.f32("AR2", 15296, 2048)
    STMP = c.f32("AR2", 19392, 2048)
    DRHS = [c.f32("AR2", 23488 + b * 4096, 2048, 17) for b in range(2)]
    msqz = k.op("vector", lambda e: e.memset(c.msq_all[:], 0.0))
    nd_tok = [k.dma("sync", DRHS[b_][0:16, :], c.negd, c.s_const) for b_ in range(2)][-1]
    if c.mode == "twoB":
        si = k.dma("sync", c.S_t[:].rearrange("n (h p) -> n h p", p=128),
                   c.s_init_in.rearrange("(h n) p -> n h p", n=128), c.s_in)
        init_tok = k.op("vector", lambda e: e.engine_nop(), deps=[si])
    st = dict(upd=init_tok, yb=None, m5=[None, None], dbuild=[None, None],
              lexp=[None, None], dv_e=[None, None], ge=[None, None], q2=[None, None],
              scst=None, prev_yt=None, prev_yt2=None, dvo_tok=None,
              xput=[None, None], bput=[None, None], szput=[None, None],
              bfput=[None, None], cfput=[None, None], s17put=[None, None],
              mt={}, xt={}, xd={}, ectok={}, eq={}, lastmm1={}, gt={}, ds={},
              gm={}, mm2l={}, mul={}, e2l={}, ms={}, getok={}, q1=[None, None])

    def eqg(cc):
        b = cc % 2
        m5 = k.op("tensor", lambda e, b=b: e.matmul(c.ps[0][:, 128 + b * 16:144 + b * 16],
                                                    c.ones1[:], EQROW[b][0:1, :],
                                                    start=True, stop=True),
                  deps=[st["q1"][b]] + [t for t in (st["q2"][b],) if t])
        q2 = k.op("scalar", lambda e, b=b: e.activation(EQR[b][:, 0:16],
                                                        c.ps[0][:, 128 + b * 16:144 + b * 16],
                                                        AF.Identity),
                  deps=[m5] + [t for t in (st["mul"].get(cc - 2),) if t])
        st["m5"][b] = m5
        st["q2"][b] = q2
        st["eq"][cc] = q2
        gm = k.op("tensor", lambda e, b=b: e.matmul(c.ps[0][:, 0:128], BFM[b][:], CFM[b][:],
                                                    start=True, stop=True),
                  deps=[st["bfput"][b], q2] + [t for t in (st["ge"][1 - b],) if t])
        st["gm"][cc] = gm
        ge = k.op("scalar", lambda e, b=b: e.activation(GSB[b][:], c.ps[0][:, 0:128],
                                                        AF.Identity),
                  deps=[gm] + [t for t in (st["mt"].get(cc - 2),) if t])
        st["ge"][b] = ge
        st["getok"][cc] = ge

    def mtxt(cc):
        b = cc % 2
        mt = k.op("gpsimd", lambda e, b=b: e.tensor_tensor(
            MT[b][:].rearrange("p (h w) -> p h w", w=128),
            LL[b][:].rearrange("p (h w) -> p h w", w=128),
            GSB[b][:].unsqueeze(1).broadcast_to((128, 16, 128)), ALU.mult),
            deps=[st["lexp"][b], st["getok"][cc]] +
            [t for t in (st["lastmm1"].get(cc - 2),) if t])
        xt = k.op("gpsimd", lambda e, cc=cc, b=b: e.tensor_tensor(
            XTI[b][:].rearrange("p (h w) -> p h w", w=128),
            XHT[b][:].rearrange("p (h w) -> p h w", w=128),
            c.dttm_all[:, cc * 16:(cc + 1) * 16].unsqueeze(2).broadcast_to((128, 16, 128)),
            ALU.mult), deps=[st["xput"][b]] + [t for t in (st["lastmm1"].get(cc - 2),) if t])
        st["mt"][cc] = mt
        st["xt"][cc] = xt

    def xdo(cc):
        b = cc % 2
        st["xd"][cc] = k.op("vector", lambda e, cc=cc, b=b: e.tensor_tensor(
            XDC[b][:].rearrange("p (h w) -> p h w", w=128),
            XHT[b][:].rearrange("p (h w) -> p h w", w=128),
            c.dectm_all[:, cc * 16:(cc + 1) * 16].unsqueeze(2).broadcast_to((128, 16, 128)),
            ALU.mult), deps=[st["xput"][b]] + [t for t in (st["ds"].get(cc - 2),) if t])

    def s1(cc):
        b = cc % 2
        eqput = k.dma("sync", EQROW[b][0:1, :], c.sq_d[cc:cc + 1, :], c.s_pb[b],
              deps=[t for t in (st["m5"][b],) if t])
        s17dep = [t for t in (st["dbuild"][b],) if t]
        st["s17put"][b] = k.dma("sync", S17[b][:], c.sfm_d[cc], c.s_pb[b], deps=s17dep)
        drput = k.dma("sync", DRHS[b][16:17, :], c.sflat_d[cc:cc + 1, :], c.s_pb[b])
        gpb = [t for t in (st["gm"].get(cc - 2), st["mm2l"].get(cc - 2)) if t]
        k.dma("sync", BFM[b][:], c.xbc_fm_d[2048:2176, cc * Q:(cc + 1) * Q],
              c.s_pb[b], deps=gpb)
        st["bfput"][b] = st["cfput"][b] = k.dma(
            "sync", CFM[b][:], c.xbc_fm_d[2176:2304, cc * Q:(cc + 1) * Q], c.s_pb[b])
        gpa = [t for t in (st["lastmm1"].get(cc - 2), st["gt"].get(cc - 2),
                           st["ds"].get(cc - 2)) if t]
        k.dma("sync", XHT[b][:], c.xbc_fm_d[0:2048, cc * Q:(cc + 1) * Q], c.s_pa[b],
              transpose=True, deps=gpa)
        k.dma("sync", BTM[b][:], c.xbc_fm_d[2048:2176, cc * Q:(cc + 1) * Q], c.s_pa[b],
              transpose=True)
        st["xput"][b] = st["bput"][b] = st["szput"][b] = k.dma(
            "sync", SZT[b][:], c.z_fm_d[0:2048, cc * Q:(cc + 1) * Q], c.s_pa[b],
            transpose=True)
        ec = k.op("scalar", lambda e, cc=cc, b=b: e.activation(
            ECOL[b][:, 0:16], c.stm_all[:, cc * 16:(cc + 1) * 16], AF.Exp),
            deps=[t for t in (st["e2l"].get(cc - 2),) if t])
        st["ectok"][cc] = ec
        dl = [None, None, None, None]
        for gi in range(4):
            pD = c.ps[1] if gi % 2 == 0 else c.ps[6]
            if gi < 2:
                ddep = [st["s17put"][b], drput, nd_tok]
                ddep += ([t for t in (st["dv_e"][1 - b],) if t] if gi == 0 else
                         [t for t in (st["prev_yt"],) if t])
            else:
                ddep = [dl[gi - 2]]
            dm = k.op("tensor", lambda e, gi=gi, b=b, pD=pD: e.matmul(
                pD[:, 0:512], S17[b][0:17, :], DRHS[b][0:17, gi * 512:(gi + 1) * 512],
                start=True, stop=True), deps=ddep)
            st["dbuild"][b] = dm
            dv = k.op("vector", lambda e, gi=gi, b=b, pD=pD: e.tensor_tensor(
                DMS[b][:, gi * 512:(gi + 1) * 512], pD[:, 0:512], c.tricap_t[:],
                ALU.min), deps=[dm, c.consts] +
                [t for t in (st["lexp"][b],) if t and gi == 0])
            dl[gi] = dv
            k.op("scalar", lambda e, gi=gi, b=b: e.activation(
                LL[b][:, gi * 512:(gi + 1) * 512], DMS[b][:, gi * 512:(gi + 1) * 512],
                AF.Exp), deps=[dv] + [t for t in (st["mt"].get(cc - 2),) if t and gi == 0])
        st["dv_e"][b] = dl[2]
        st["dvo_tok"] = dl[3]
        st["lexp"][b] = k.tok("scalar")
        st["q1"][b] = k.op("scalar", lambda e, b=b: e.activation(
            EQROW[b][0:1, :], EQROW[b][0:1, :], AF.Exp), deps=[eqput])
        if cc == 0 or DBG_NODEFER:
            eqg(cc)
            mtxt(cc)
            xdo(cc)

    def s23(cc):
        b = cc % 2
        sca = k.op("scalar", lambda e: e.activation(SBF[:, 0:1024], c.S_t[:, 0:1024],
                                                    AF.Identity),
                   deps=[st["upd"]] + [t for t in (st["mm2l"].get(cc - 1),) if t])
        scb = k.op("gpsimd", lambda e: e.tensor_copy(SBF[:, 1024:2048], c.S_t[:, 1024:2048]),
                   deps=[st["upd"]] + [t for t in (st["mm2l"].get(cc - 1),) if t])
        ytoks = []
        e2_prev = {0: st["prev_yt2"], 1: st["prev_yt"]}
        mm2 = None
        e2 = None
        for hg in range(4):
            p1 = c.ps[4 + 2 * (hg % 2)]
            p2 = c.ps[5 + 2 * (hg % 2)]
            if hg >= 2:
                bank_free = [ytoks[hg - 2]]
            else:
                bank_free = [t for t in ((st["prev_yt2"],) if hg == 0 else
                                         (st["prev_yt"],)) if t]
            if hg % 2 == 1:
                bank_free = bank_free + [t for t in (st["dvo_tok"],) if t]
            mm1 = None
            for hh in range(4):
                h = hg * 4 + hh
                mm1 = k.op("tensor", lambda e, h=h, hh=hh, p1=p1, b=b: e.matmul(
                    p1[:, hh * 128:(hh + 1) * 128], MT[b][:, h * 128:(h + 1) * 128],
                    XTI[b][:, h * 128:(h + 1) * 128], start=True, stop=False),
                    deps=[st["mt"][cc], st["xt"][cc]] + bank_free, inc=False)
                mm1 = k.op("tensor", lambda e, h=h, hh=hh, p1=p1, b=b: e.matmul(
                    p1[:, hh * 128:(hh + 1) * 128], c.dhi[:, h * 128:(h + 1) * 128],
                    XHT[b][:, h * 128:(h + 1) * 128], start=False, stop=True),
                    deps=[c.dhi_tok], inc=(hh == 3))
            scd = sca if hg < 2 else scb
            for hh in range(4):
                h = hg * 4 + hh
                mm2 = k.op("tensor", lambda e, h=h, hh=hh, p2=p2, b=b: e.matmul(
                    p2[:, hh * 128:(hh + 1) * 128], CFM[b][:], SBF[:, h * 128:(h + 1) * 128],
                    start=True, stop=True), deps=[scd, st["cfput"][b]] + bank_free,
                    inc=(hh == 3))
            for hh in range(4):
                h = hg * 4 + hh
                e2 = k.op("scalar", lambda e, h=h, hh=hh, p2=p2, hg=hg, b=b: e.activation(
                    T2E[hg % 2][:, hh * 128:(hh + 1) * 128], p2[:, hh * 128:(hh + 1) * 128],
                    AF.Identity, scale=ECOL[b][:, h:h + 1]),
                    deps=[mm2, st["ectok"][cc]] +
                    ([e2_prev[hg % 2]] if (hg < 2 and e2_prev[hg % 2]) else
                     [ytoks[hg - 2]] if hg >= 2 else []))
            yt = k.op("vector", lambda e, hg=hg, p1=p1: e.tensor_tensor(
                YTM[:, hg * 512:(hg + 1) * 512], p1[:, 0:512], T2E[hg % 2][:], ALU.add),
                deps=[e2, mm1] + [t for t in (st["ms"].get(cc - 1),) if t])
            ytoks.append(yt)
        st["mm2l"][cc] = mm2
        st["e2l"][cc] = e2
        st["prev_yt"] = ytoks[3]
        st["prev_yt2"] = ytoks[2]
        st["lastmm1"][cc] = ytoks[3]
        if cc + 1 < NCH and not DBG_NODEFER:
            eqg(cc + 1)
        mm_ = None
        for half in range(2):
            pa, pb2 = c.ps[2], c.ps[3]
            for hh in range(8):
                h = half * 8 + hh
                pt = pa if hh < 4 else pb2
                mm_ = k.op("tensor", lambda e, h=h, hh=hh, pt=pt, b=b: e.matmul(
                    pt[:, (hh % 4) * 128:(hh % 4 + 1) * 128], BTM[b][:],
                    XDC[b][:, h * 128:(h + 1) * 128], start=True, stop=True),
                    deps=[st["xd"][cc], st["bput"][b]] +
                    ([st["upd"], sca, scb] if hh == 0 else []), inc=(hh == 7))
            mu = k.op("gpsimd", lambda e, half=half, b=b: e.tensor_tensor(
                STMP[:, half * 1024:(half + 1) * 1024].rearrange("p (h w) -> p h w", w=128),
                c.S_t[:, half * 1024:(half + 1) * 1024].rearrange("p (h w) -> p h w", w=128),
                EQR[b][:, half * 8:half * 8 + 8].unsqueeze(2).broadcast_to((128, 8, 128)),
                ALU.mult), deps=[st["eq"][cc], st["upd"], sca, scb])
            st["mul"][cc] = mu
            a1 = k.op("vector", lambda e, half=half, pa=pa: e.tensor_tensor(
                c.S_t[:, half * 1024:half * 1024 + 512], STMP[:, half * 1024:half * 1024 + 512],
                pa[:], ALU.add), deps=[mu, mm_])
            st["upd"] = k.op("vector", lambda e, half=half, pb2=pb2: e.tensor_tensor(
                c.S_t[:, half * 1024 + 512:half * 1024 + 1024],
                STMP[:, half * 1024 + 512:half * 1024 + 1024], pb2[:], ALU.add))
        st["ds"][cc] = mm_
        if cc + 1 < NCH and not DBG_NODEFER:
            xdo(cc + 1)
            mtxt(cc + 1)
        gt = k.op("vector", lambda e, b=b: e.tensor_tensor(YTM[:], YTM[:], SZT[b][:], ALU.mult),
                  deps=[ytoks[-1], st["szput"][b]])
        st["gt"][cc] = gt
        ms = k.op("scalar", lambda e, cc=cc: e.activation(Y2[:], YTM[:], AF.Square,
                                                          accum_out=c.msq_all[:, cc:cc + 1]),
                  deps=[gt, msqz])
        st["ms"][cc] = ms
        r1 = k.op("vector", lambda e, cc=cc: e.tensor_scalar(
            c.tiny[:, 32:33], c.msq_all[:, cc:cc + 1], 1.0 / DI, EPS, ALU.mult, ALU.add),
            deps=[ms])
        r2 = k.op("scalar", lambda e: e.activation(c.tiny[:, 33:34], c.tiny[:, 32:33], AF.Ln),
                  deps=[r1])
        r2b = k.op("vector", lambda e: e.tensor_scalar(c.tiny[:, 33:34], c.tiny[:, 33:34],
                                                       -0.5, None, ALU.mult), deps=[r2])
        r3 = k.op("scalar", lambda e, cc=cc: e.activation(c.rstd_all[:, cc:cc + 1],
                                                          c.tiny[:, 33:34], AF.Exp), deps=[r2b])
        yb = k.op("vector", lambda e, cc=cc, b=b: e.tensor_scalar(
            YBF[b][:], YTM[:], c.rstd_all[:, cc:cc + 1], None, ALU.mult),
            deps=[r3] + ([c.s_sp.tok()] if cc >= 2 else []))
        st["yb"] = yb
        k.dma("scalar", c.y_tm_d[cc * Q:(cc + 1) * Q, :], YBF[b][:], c.s_sp, deps=[yb])

    s1(0)
    for cc in range(1, NCH):
        s1(cc)
        s23(cc - 1)
    s23(NCH - 1)


def phase_p2b(c):
    k = c.k
    k.fence([c.s_sp, c.s_tr, c.s_w, c.s_g2, c.s_in])
    WO = [c.bf("AR2", j * 1024, 1024) for j in range(16)]
    YF2 = [[c.bf("AR1", p_ * 16384 + j * 1024, 1024) for j in range(16)] for p_ in range(2)]
    XT = [c.f32("AR3", b * 2048, DM) for b in range(2)]
    GTL = [c.f32("AR3", 4096 + b * 2048, DM) for b in range(2)]
    XN = c.f32("AR3", 8192, DM)
    SS = c.f32("AR3", 10240, DM)
    OT = [c.f32("AR3", 12288 + b * 2048, DM) for b in range(2)]
    for j in range(16):
        k.dma("gpsimd", WO[j][:], c.w_out[j * 128:(j + 1) * 128, :], c.s_w2)
    wld = c.s_w2.tok()
    for j in range(16):
        k.op("scalar", lambda e, j=j: e.activation(WO[j][:], WO[j][:], AF.Identity,
                                                   scale=c.rmsw_t[:, j:j + 1]),
             deps=[wld, c.consts])
    wo_tok = k.tok("scalar")
    otok = [None, None]
    vfree = [None, None]
    vprev = None
    ptoks = {}
    yput = {}
    for sg in range(4):
        p_ = sg % 2
        for j in range(16):
            dep = [ptoks[sg - 2]] if sg >= 2 else []
            yput[(sg, j)] = k.dma("sync", YF2[p_][j][:],
                                  c.y_tm_d[sg * 1024:(sg + 1) * 1024, j * 128:(j + 1) * 128],
                                  c.s_tr, transpose=True, deps=dep)
        for tc in range(8):
            g = sg * 8 + tc
            b = tc % 2
            pd = [c.ps[0 + 2 * b], c.ps[1 + 2 * b]]
            xl = k.dma("sync", XT[b][:], c.x_main[g * Q:(g + 1) * Q, :], c.s_xl[b],
                       deps=[otok[b]] if otok[b] else [])
            gl = k.dma("gpsimd", GTL[b][:], c.g_tm_d[g * Q:(g + 1) * Q, :], c.s_gl[b],
                       deps=[otok[b]] if otok[b] else [])
            mm = [None, None]
            for half in range(2):
                first = True
                for j in range(16):
                    mm[half] = k.op("tensor", lambda e, j=j, half=half, tc=tc, pd=pd,
                                    first=first, p_=p_:
                                    e.matmul(pd[half][:, 0:512],
                                             YF2[p_][j][:, tc * 128:(tc + 1) * 128],
                                             WO[j][:, half * 512:half * 512 + 512],
                                             start=first, stop=(j == 15)),
                                    deps=[yput[(sg, j)], wo_tok] +
                                    ([vfree[b]] if (first and vfree[b]) else []),
                                    inc=(j == 15))
                    first = False
            ptoks[sg] = k.tok("tensor")
            xn = k.op("scalar", lambda e, g=g, b=b: e.activation(
                XN[:], XT[b][:], AF.Identity, scale=c.rho[:, g:g + 1], bias=c.nmr[:, g:g + 1]),
                deps=[xl] + ([vprev] if vprev else []))
            v1 = k.op("vector", lambda e, pd=pd: e.tensor_tensor(SS[:, 0:512], pd[0][:], XN[:, 0:512],
                                                                 ALU.add), deps=[mm[0], xn])
            v2 = k.op("vector", lambda e, pd=pd: e.tensor_tensor(SS[:, 512:1024], pd[1][:],
                                                                 XN[:, 512:1024], ALU.add),
                      deps=[mm[1]])
            v3 = k.op("vector", lambda e, b=b: e.tensor_tensor(SS[:], SS[:], GTL[b][:], ALU.mult),
                      deps=[v2, gl])
            v4 = k.op("vector", lambda e, b=b: e.tensor_tensor(OT[b][:], SS[:], XT[b][:], ALU.add),
                      deps=[v3])
            vprev = v4
            vfree[b] = v2
            otok[b] = k.dma("sync", c.out_t[g * Q:(g + 1) * Q, :], OT[b][:], c.s_out, deps=[v4])


def finish(c):
    k = c.k
    if c.stage == "full" or c.mode == "twoA":
        k.op("sync", lambda e: e.nop(), deps=[c.s_out.tok()])
    k.fence([c.s_sp, c.s_out])


def _layout_inputs(inputs, n_cores=8):
    f = np.float32
    asf = lambda a: np.ascontiguousarray(np.asarray(a, dtype=f))
    x = asf(inputs["L0_attn_x"])
    w_in = asf(inputs["L0_attn_W_in"])
    conv_w = asf(inputs["L0_attn_conv_w"])
    conv_diag = np.zeros((4, 18, 128, 128), f)
    for t in range(4):
        for j in range(18):
            np.fill_diagonal(conv_diag[t, j], conv_w[j * 128:(j + 1) * 128, t])
    tri = np.triu(np.ones((128, 128), f))
    cap = np.where(tri > 0, 0.0, -1e9).astype(f)
    tricap = np.tile(cap, (1, 4))
    negd = np.zeros((16, 2048), f)
    for h in range(16):
        negd[h, h * 128:(h + 1) * 128] = -1.0
    com = dict(
        w_in=w_in,
        w_out=asf(inputs["L0_attn_W_out"]),
        lnw_col=asf(inputs["L0_attn_ln_w"]).reshape(8, 128).T.copy(),
        lnb_col=asf(inputs["L0_attn_ln_b"]).reshape(8, 128).T.copy(),
        conv_diag=conv_diag,
        convb_col=asf(inputs["L0_attn_conv_b"]).reshape(18, 128).T.copy(),
        dtb_col=asf(inputs["L0_attn_dt_bias"]).reshape(16, 1),
        alog_col=asf(inputs["L0_attn_A_log"]).reshape(16, 1),
        dhead_row=asf(inputs["L0_attn_D"]).reshape(1, 16),
        rmsw_col=asf(inputs["L0_attn_rms_w"]).reshape(16, 128).T.copy(),
        router_wT=asf(inputs["L0_ffn_router_w"]).T.copy(),
        corrb_row=asf(inputs["L0_ffn_correction_bias"]).reshape(1, 16),
        gate_w=asf(inputs["L0_ffn_gate_w"]),
        up_w=asf(inputs["L0_ffn_up_w"]),
        down_w=asf(inputs["L0_ffn_down_w"]),
        sgT=asf(inputs["L0_ffn_shared_gate_w"]).T.copy(),
        suT=asf(inputs["L0_ffn_shared_up_w"]).T.copy(),
        sdT=asf(inputs["L0_ffn_shared_down_w"]).T.copy(),
        i128=np.eye(128, dtype=f),
        fconst=np.tile(np.array([[1e-5, 1.0 + 1e-5, 1e-10, -0.5, -1.0, 1.0, 0.5, 2.0]], f),
                       (128, 1)),
        ones_row=np.ones((1, 512), f),
        triu=tri, tricap=tricap, negd=negd,
    )
    maps = []
    for core in range(n_cores):
        b, h = core // 2, core % 2
        m = dict(com)
        m["x_main"] = x[b, h * T:(h + 1) * T]
        halo = np.zeros((16, DM), f)
        if h == 1:
            halo[0:3] = x[b, T - 3:T]
        m["x_halo"] = halo
        m["mask_col"] = np.full((128, 1), 1.0 if h == 1 else 0.0, f)
        sm = np.zeros((128, 8), f)
        if h == 1:
            sm[:, core - 1] = 1.0
        m["smask"] = sm
        maps.append(m)
    return maps


def kernel(**inputs):
    maps = _layout_inputs(inputs)
    import os
    fused = os.environ.get("KMODE", "fused") == "fused"
    if fused:
        nc, es, c = build_nc(mode="fused", stage="full")
        res = run_bass_kernel_spmd(nc, maps, core_ids=list(range(8)))
        results = res.results
    else:
        ncA, esA, cA = build_nc(mode="twoA", stage="p1b")
        resA = run_bass_kernel_spmd(ncA, maps, core_ids=list(range(8)))
        mapsB = []
        for core in range(8):
            m = dict(maps[core])
            if core % 2 == 1:
                m["s_init_in"] = np.ascontiguousarray(resA.results[core - 1]["s_half_out"])
            else:
                m["s_init_in"] = np.zeros((DI, NST), np.float32)
            mapsB.append(m)
        ncB, esB, cB = build_nc(mode="twoB", stage="full")
        resB = run_bass_kernel_spmd(ncB, mapsB, core_ids=list(range(8)))
        results = resB.results
    out = np.zeros((4, 8192, 1024), np.float32)
    for core in range(8):
        b, h = core // 2, core % 2
        out[b, h * T:(h + 1) * T] = results[core]["out_t"]
    return out


# version 13
# speedup vs baseline: 1.0178x; 1.0004x over previous
import contextlib
import numpy as np
import ml_dtypes
import concourse.bass as bass
import concourse.mybir as mybir
from concourse.bass_utils import run_bass_kernel_spmd

F32 = mybir.dt.float32
BF16 = mybir.dt.bfloat16
AF = mybir.ActivationFunctionType
ALU = mybir.AluOpType
AX = mybir.AxisListType

T = 4096
Q = 128
NCH = T // Q
NH, HD, NST = 16, 128, 128
DM, DI = 1024, 2048
EPS = 1e-5
NEG = -1e30

ENGINES = ("tensor", "vector", "scalar", "gpsimd", "sync")
FCONST_VALS = [EPS, 1.0 + EPS, EPS * EPS, -0.5, -1.0, 1.0, 0.5, 2.0]


class Stream:
    def __init__(self, k, name):
        self.sem = k.es.enter_context(k.nc.semaphore(name))
        self.count = 0

    def tok(self):
        return (self.sem, self.count)


class Eng:
    def __init__(self, k, name):
        self.name = name
        self.ops = []
        self.sem = k.es.enter_context(k.nc.semaphore("prog_" + name))
        self.count = 0
        self.waited = {}


class K:

    def __init__(self, nc, es):
        self.nc = nc
        self.es = es
        self.eng = {n: Eng(self, n) for n in ENGINES}

    def stream(self, name):
        return Stream(self, name)

    def _filt(self, e, deps, keep_self=False):
        out = []
        for (sem, val) in deps:
            if (sem is e.sem and not keep_self) or val <= 0:
                continue
            if e.waited.get(id(sem), -1) >= val:
                continue
            e.waited[id(sem)] = val
            out.append((sem, val))
        return out

    def op(self, engine, fn, deps=(), inc=True):
        e = self.eng[engine]
        mydeps = self._filt(e, deps)
        if inc:
            e.count += 1
        e.ops.append((fn, mydeps, e.sem if inc else None))
        return (e.sem, e.count)

    def tok(self, engine):
        e = self.eng[engine]
        return (e.sem, e.count)

    def dma(self, queue, out, in_, stream, deps=(), **kw):
        stream.count += 16
        e = self.eng[queue]
        mydeps = self._filt(e, deps, keep_self=True)

        def fn(eng):
            return eng.dma_start(out, in_, **kw)

        e.ops.append((fn, mydeps, ("dma", stream.sem)))
        return (stream.sem, stream.count)

    def raw16(self, queue, fn, stream, deps=()):
        stream.count += 16
        e = self.eng[queue]
        mydeps = self._filt(e, deps, keep_self=True)
        e.ops.append((fn, mydeps, ("dma", stream.sem)))
        return (stream.sem, stream.count)

    def fence(self, streams=()):
        toks = [(e.sem, e.count) for e in self.eng.values()]
        toks += [s.tok() for s in streams]
        for name in ENGINES:
            self.op(name, lambda e: e.nop(), deps=toks, inc=True)

    def emit(self):
        block = self.es.enter_context(self.nc.Block())
        for name in ENGINES:
            e = self.eng[name]
            selfsync = name in ("vector", "scalar", "gpsimd")

            def make(e, selfsync):
                def body(eng):
                    n = 0
                    for fn, deps, inc in e.ops:
                        if selfsync and n > 0 and inc is not None and not isinstance(inc, tuple):
                            eng.wait_ge(e.sem, n)
                        for (sem, val) in deps:
                            eng.wait_ge(sem, val)
                        ins = fn(eng)
                        if inc is None:
                            continue
                        if isinstance(ins, (list, tuple)):
                            ins = ins[-1]
                        if isinstance(inc, tuple):
                            ins.then_inc(inc[1], 16)
                        else:
                            ins.then_inc(inc, 1)
                            n += 1
                return body

            getattr(block, name)(make(e, selfsync))


class Ctx:
    pass


def build_nc(mode="fused", stage="full", dbg=False):
    nc = bass.Bass("TRN2", target_bir_lowering=False, detect_race_conditions=False)
    es = contextlib.ExitStack()
    k = K(nc, es)
    c = Ctx()
    c.nc, c.es, c.k, c.mode, c.stage = nc, es, k, mode, stage

    def din(name, shape, dt=F32):
        setattr(c, name, nc.dram_tensor(name, shape, dt, kind="ExternalInput").ap())

    def dout(name, shape, dt=F32):
        setattr(c, name, nc.dram_tensor(name, shape, dt, kind="ExternalOutput").ap())

    def dint(name, shape, dt):
        kind = "ExternalOutput" if dbg else "Internal"
        setattr(c, name, nc.dram_tensor(name, shape, dt, kind=kind).ap())

    din("x_main", [T, DM]); din("x_halo", [16, DM])
    din("mask_col", [128, 1]); din("smask", [128, 8])
    din("w_in", [DM, 4368]); din("w_out", [DI, DM])
    din("lnw_col", [128, 8]); din("lnb_col", [128, 8])
    din("conv_diag", [4, 18, 128, 128]); din("convb_col", [128, 18])
    din("dtb_col", [16, 1]); din("alog_col", [16, 1]); din("dhead_row", [1, 16])
    din("rmsw_col", [128, 16])
    din("router_wT", [DM, 16]); din("corrb_row", [1, 16])
    din("gate_w", [16, DM, 256]); din("up_w", [16, DM, 256]); din("down_w", [16, 256, DM])
    din("sgT", [DM, DI]); din("suT", [DM, DI]); din("sdT", [DI, DM])
    din("i128", [128, 128]); din("ones_row", [1, 512]); din("fconst", [128, 8])
    din("triu", [128, 128]); din("tricap", [128, 512]); din("negd", [16, 2048])
    dout("out_t", [T, DM])
    if dbg:
        dout("dbg0", [128, 160])
        dout("dbg1", [128, 512])
    if mode == "twoA":
        dout("s_half_out", [DI, NST])
    if mode == "twoB":
        din("s_init_in", [DI, NST])
    dint("xn_tm_d", [T, DM], BF16)
    dint("xn2_tm_d", [T, DM], BF16)
    dint("xn2h_tm_d", [16, DM], BF16)
    dint("xbc_fm_d", [2304, T], BF16)
    dint("z_fm_d", [DI, T], BF16)
    dint("h_fm_d", [16 * 256 + DI, T], BF16)
    dint("y_tm_d", [T, DI], BF16)
    dint("g_tm_d", [T, DM], F32)
    dint("sflat_d", [NCH, 2048], F32)
    dint("sfm_d", [NCH, 17, 128], F32)
    dint("sq_d", [NCH, 16], F32)
    dint("comb_fm_d", [16, T], F32)
    if mode == "fused":
        dint("coll_in_d", [DI, NST], F32)
        dint("coll_out_d", [8 * DI, NST], F32)

    SB = {}

    def st(name, shape, dt):
        SB[name] = es.enter_context(nc.sbuf_tensor(name, shape, dt))
        setattr(c, name, SB[name])
        return SB[name]

    st("S_t", [128, DI], F32)
    st("stm_all", [128, NCH * 16], F32)
    st("dttm_all", [128, NCH * 16], F32)
    st("dectm_all", [128, NCH * 16], F32)
    st("comb", [128, NCH * 16], F32)
    st("msq_all", [128, NCH], F32)
    st("rho", [128, NCH], F32); st("nmr", [128, NCH], F32)
    st("rho2", [128, NCH], F32); st("nmr2", [128, NCH], F32)
    st("rstd_all", [128, NCH], F32)
    st("hstat", [16, 8], F32)
    st("i128_t", [128, 128], F32); st("i128b_t", [128, 128], BF16)
    st("ones1", [1, 128], F32)
    st("triu_t", [128, 128], F32); st("tricap_t", [128, 512], F32)
    st("dhi", [128, NH * 128], BF16)
    st("sfm", [17, 128], F32)
    st("lnw_t", [128, 8], F32); st("lnb_t", [128, 8], F32)
    st("maskc", [128, 1], F32); st("smask_t", [128, 8], F32)
    st("dtb_t", [16, 1], F32); st("alog_t", [16, 1], F32); st("acol", [16, 1], F32)
    st("dhead_t", [1, 16], F32); st("corrb_t", [1, 16], F32)
    st("rmsw_t", [128, 16], F32)
    st("convb_t", [128, 18], F32)
    st("fconst_t", [128, 8], F32)
    st("tiny", [128, 64], F32)
    st("tiny16", [16, 128], F32)
    st("AR1", [128, 32768], BF16)
    st("AR2", [128, 32768], BF16)
    st("AR3", [128, 26944], BF16)

    c.ps = [es.enter_context(nc.psum_tensor(f"ps{i}", [128, 512], F32)) for i in range(8)]

    c.s_in = k.stream("s_in")
    c.s_w = k.stream("s_w")
    c.s_w2 = k.stream("s_w2")
    c.s_w3 = k.stream("s_w3")
    c.s_sp = k.stream("s_sp")
    c.s_tr = k.stream("s_tr")
    c.s_g2 = k.stream("s_g2")
    c.s_out = k.stream("s_out")
    c.s_bnc = k.stream("s_bnc")
    c.s_coll = k.stream("s_coll")
    c.s_const = k.stream("s_const")
    c.s_x = [k.stream("s_x0"), k.stream("s_x1")]
    c.s_xh = [k.stream("s_xh0"), k.stream("s_xh1")]
    c.s_pa = [k.stream("s_pa0"), k.stream("s_pa1")]
    c.s_pb = [k.stream("s_pb0"), k.stream("s_pb1")]
    c.s_cr = [k.stream("s_cr0"), k.stream("s_cr1")]
    c.s_we = [k.stream("s_we0"), k.stream("s_we1")]
    c.s_hl = [k.stream("s_hl0"), k.stream("s_hl1")]
    c.s_xl = [k.stream("s_xl0"), k.stream("s_xl1")]
    c.s_gl = [k.stream("s_gl0"), k.stream("s_gl1")]

    def bf(arena, off, n, parts=128):
        return SB[arena][0:parts, off:off + n]

    def f32(arena, off, n, parts=128):
        return SB[arena][0:parts, off:off + 2 * n].bitcast(F32)

    c.bf, c.f32 = bf, f32

    k.dma("sync", SB["ones1"][:], c.ones_row[0:1, 0:128], c.s_const)
    for dst, src in [("i128_t", "i128"), ("triu_t", "triu"),
                     ("tricap_t", "tricap"), ("lnw_t", "lnw_col"), ("lnb_t", "lnb_col"),
                     ("maskc", "mask_col"), ("smask_t", "smask"), ("dtb_t", "dtb_col"),
                     ("alog_t", "alog_col"), ("dhead_t", "dhead_row"), ("corrb_t", "corrb_row"),
                     ("rmsw_t", "rmsw_col"), ("convb_t", "convb_col")]:
        k.dma("sync", SB[dst][:], getattr(c, src), c.s_const)
    k.dma("sync", SB["sfm"][16:17, :], c.ones_row[0:1, 0:128], c.s_in)
    k.dma("sync", SB["fconst_t"][:], c.fconst, c.s_const)
    k.dma("gpsimd", SB["i128b_t"][:], c.i128, c.s_w)
    c.consts = c.s_const.tok()
    for i, v in enumerate(FCONST_VALS):
        nc.const_aps.register(F32, float(v), SB["fconst_t"][:, i:i + 1])
        if float(np.float32(v)) != float(v):
            nc.const_aps.register(F32, float(np.float32(v)), SB["fconst_t"][:, i:i + 1])
    k.fence([c.s_const])
    c.constsw = c.s_w.tok()
    k.op("scalar", lambda e: e.activation(c.acol[:], c.alog_t[:], AF.Exp), deps=[c.consts])
    k.op("vector", lambda e: e.tensor_scalar(c.acol[:], c.acol[:], -1.0, None, ALU.mult),
         deps=[k.tok("scalar")])
    c.acol_tok = k.tok("vector")
    t0 = k.op("tensor", lambda e: e.matmul(c.ps[0][:, 0:16], c.ones1[:],
                                           c.dhead_t[:], start=True, stop=True), deps=[c.consts])
    t1 = k.op("scalar", lambda e: e.activation(c.tiny[:, 0:16], c.ps[0][:, 0:16], AF.Identity),
              deps=[t0])
    for h in range(NH):
        k.op("vector", lambda e, h=h: e.tensor_scalar(
            c.dhi[:, h * 128:(h + 1) * 128], c.i128b_t[:],
            c.tiny[:, h:h + 1], None, ALU.mult), deps=[t1, c.constsw])
    c.dhi_tok = k.tok("vector")

    phase_p0(c)
    if stage != "p0":
        phase_p1a(c)
    if stage not in ("p0", "p1a"):
        phase_p1b(c)
    if mode != "twoA" and stage not in ("p0", "p1a", "p1b"):
        phase_p3z(c)
        if stage != "p3z":
            phase_p3(c)
        if stage not in ("p3z", "p3sh", "p3e", "p3dn"):
            phase_p2a(c)
        if stage == "full":
            phase_p2b(c)
    finish(c)
    k.emit()
    return nc, es, c


def phase_p0(c):
    k = c.k
    XT = [c.f32("AR1", b * 2048, DM) for b in range(4)]
    SQ = c.f32("AR2", 0, DM)
    NT = [c.bf("AR3", b * 1024, DM) for b in range(2)]
    NT2 = [c.bf("AR3", 2048 + b * 1024, DM) for b in range(2)]
    SUMS = c.f32("AR2", 2048, NCH)
    SSQ = c.f32("AR2", 2176, NCH)
    MU = c.f32("AR2", 2304, NCH)
    ld_toks = [None] * NCH
    red_toks = [None] * NCH
    sq_toks = [None] * NCH
    zz = k.op("vector", lambda e: e.memset(SSQ[:], 0.0))
    for i in range(NCH):
        b = i % 4
        dep = [red_toks[i - 4], sq_toks[i - 4]] if i >= 4 else []
        ld = k.dma("sync", XT[b][:], c.x_main[i * Q:(i + 1) * Q, :],
                   (c.s_x + c.s_pa)[b], deps=dep)
        ld_toks[i] = ld
        red_toks[i] = k.op("vector", lambda e, i=i, b=b: e.tensor_reduce(
            SUMS[:, i:i + 1], XT[b][:], AX.X, ALU.add), deps=[ld])
        sq_toks[i] = k.op("scalar", lambda e, i=i, b=b: e.activation(
            SQ[:], XT[b][:], AF.Square, accum_out=SSQ[:, i:i + 1]), deps=[ld, zz])
    tA = k.op("vector", lambda e: e.tensor_scalar(MU[:], SUMS[:], 1.0 / DM, None, ALU.mult),
              deps=[red_toks[-1]])
    tB = k.op("vector", lambda e: e.tensor_scalar(SSQ[:], SSQ[:], 1.0 / DM, None, ALU.mult),
              deps=[sq_toks[-1]])
    tC = k.op("vector", lambda e: e.tensor_tensor(c.rho[:], MU[:], MU[:], ALU.mult))
    tD = k.op("vector", lambda e: e.tensor_tensor(SSQ[:], SSQ[:], c.rho[:], ALU.subtract))
    tE = k.op("vector", lambda e: e.tensor_scalar(c.rho[:], SSQ[:], EPS, None, ALU.add))
    tF = k.op("vector", lambda e: e.tensor_scalar(c.rho2[:], SSQ[:], 1.0 + EPS, EPS * EPS,
                                                  ALU.mult, ALU.add))
    l1 = k.op("scalar", lambda e: e.activation(c.rho[:], c.rho[:], AF.Ln), deps=[tE])
    l2 = k.op("scalar", lambda e: e.activation(c.rho2[:], c.rho2[:], AF.Ln), deps=[tF])
    h1 = k.op("vector", lambda e: e.tensor_scalar(c.rho[:], c.rho[:], -0.5, None, ALU.mult),
              deps=[l1])
    h2 = k.op("vector", lambda e: e.tensor_scalar(c.rho2[:], c.rho2[:], -0.5, None, ALU.mult),
              deps=[l2])
    e1 = k.op("scalar", lambda e: e.activation(c.rho[:], c.rho[:], AF.Exp), deps=[h1])
    e2 = k.op("scalar", lambda e: e.activation(c.rho2[:], c.rho2[:], AF.Exp), deps=[h2])
    n1 = k.op("vector", lambda e: e.tensor_tensor(c.nmr[:], MU[:], c.rho[:], ALU.mult),
              deps=[e1])
    n1b = k.op("vector", lambda e: e.tensor_scalar(c.nmr[:], c.nmr[:], -1.0, None, ALU.mult))
    n2 = k.op("vector", lambda e: e.tensor_tensor(c.nmr2[:], MU[:], c.rho2[:], ALU.mult),
              deps=[e2])
    n2b = k.op("vector", lambda e: e.tensor_scalar(c.nmr2[:], c.nmr2[:], -1.0, None, ALU.mult))
    sp_toks = {}
    for i in range(NCH):
        b = i % 4
        nb = i % 2
        dep = [sp_toks.get(i - 4, None)] if i >= 4 else []
        dep = [t for t in dep if t]
        ld = k.dma("sync", XT[b][:], c.x_main[i * Q:(i + 1) * Q, :],
                   (c.s_x + c.s_pa)[b],
                   deps=dep + ([sq_toks[-1], red_toks[-1]] if i < 4 else []))
        wdep = [c.s_sp.tok()] if i >= 2 else []
        x1 = k.op("scalar", lambda e, i=i, b=b, nb=nb: e.activation(
            NT[nb][:], XT[b][:], AF.Identity, scale=c.rho[:, i:i + 1], bias=c.nmr[:, i:i + 1]),
            deps=[ld, n1b] + wdep)
        k.dma("scalar", c.xn_tm_d[i * Q:(i + 1) * Q, :], NT[nb][:], c.s_sp, deps=[x1])
        x2 = k.op("scalar", lambda e, i=i, b=b, nb=nb: e.activation(
            NT2[nb][:], XT[b][:], AF.Identity, scale=c.rho2[:, i:i + 1], bias=c.nmr2[:, i:i + 1]),
            deps=[n2b])
        sp_toks[i] = k.dma("scalar", c.xn2_tm_d[i * Q:(i + 1) * Q, :], NT2[nb][:], c.s_sp,
                           deps=[x2])
    HX = c.f32("AR2", 4096, DM, 16)
    HS = c.f32("AR2", 8192, DM, 16)
    HB = c.bf("AR2", 12288, DM, 16)
    hld = k.dma("sync", HX[0:16, :], c.x_halo, c.s_in)
    h_z = k.op("vector", lambda e: e.memset(c.hstat[:, 1:2], 0.0))
    h_r = k.op("vector", lambda e: e.tensor_reduce(c.hstat[:, 0:1], HX[0:16, :], AX.X, ALU.add),
               deps=[hld])
    h_q = k.op("scalar", lambda e: e.activation(HS[0:16, :], HX[0:16, :], AF.Square,
                                                accum_out=c.hstat[:, 1:2]), deps=[hld, h_z])
    k.op("vector", lambda e: e.tensor_scalar(c.hstat[:, 2:3], c.hstat[:, 0:1], 1.0 / DM, None,
                                             ALU.mult), deps=[h_r])
    k.op("vector", lambda e: e.tensor_tensor(c.hstat[:, 3:4], c.hstat[:, 2:3], c.hstat[:, 2:3],
                                             ALU.mult))
    k.op("vector", lambda e: e.tensor_scalar(c.hstat[:, 4:5], c.hstat[:, 1:2], 1.0 / DM, None,
                                             ALU.mult), deps=[h_q])
    hv = k.op("vector", lambda e: e.tensor_tensor(c.hstat[:, 4:5], c.hstat[:, 4:5],
                                                  c.hstat[:, 3:4], ALU.subtract))
    he = k.op("vector", lambda e: e.tensor_scalar(c.hstat[:, 5:6], c.hstat[:, 4:5],
                                                  1.0 + EPS, EPS * EPS, ALU.mult, ALU.add),
              deps=[hv])
    hl = k.op("scalar", lambda e: e.activation(c.hstat[:, 5:6], c.hstat[:, 5:6], AF.Ln),
              deps=[he])
    hh = k.op("vector", lambda e: e.tensor_scalar(c.hstat[:, 5:6], c.hstat[:, 5:6],
                                                  -0.5, None, ALU.mult), deps=[hl])
    hr = k.op("scalar", lambda e: e.activation(c.hstat[:, 5:6], c.hstat[:, 5:6], AF.Exp),
              deps=[hh])
    hm = k.op("vector", lambda e: e.tensor_tensor(c.hstat[:, 6:7], c.hstat[:, 5:6],
                                                  c.hstat[:, 2:3], ALU.mult), deps=[hr])
    hm2 = k.op("vector", lambda e: e.tensor_scalar(c.hstat[:, 6:7], c.hstat[:, 6:7], -1.0, None,
                                                   ALU.mult))
    hx2 = k.op("scalar", lambda e: e.activation(HB[0:16, :], HX[0:16, :], AF.Identity,
                                                scale=c.hstat[:, 5:6], bias=c.hstat[:, 6:7]),
               deps=[hm2])
    k.dma("sync", c.xn2h_tm_d, HB[0:16, :], c.s_bnc, deps=[hx2])
    if c.stage == "p0" and hasattr(c, "dbg0"):
        k.dma("sync", c.dbg0[:, 0:32], c.rho[:], c.s_bnc, deps=[k.tok("vector"), k.tok("scalar")])
        k.dma("sync", c.dbg0[:, 32:64], c.nmr[:], c.s_bnc)
        k.dma("sync", c.dbg0[:, 64:96], c.rho2[:], c.s_bnc)
        k.dma("sync", c.dbg0[:, 96:128], c.nmr2[:], c.s_bnc)
        k.dma("sync", c.dbg0[:, 128:160], c.nmr[:], c.s_bnc)


def phase_p1a(c):
    k = c.k
    k.fence([c.s_sp, c.s_in, c.s_w])
    XF = [c.bf("AR1", j * 4096, 4096) for j in range(8)]
    WIN = [c.bf("AR2", j * 2320, 2320) for j in range(8)]
    DIAG = [c.bf("AR2", 18560 + t * 2304, 2304) for t in range(4)]
    XH = [c.bf("AR2", 27776 + j * 16, 16) for j in range(8)]
    XBCR = c.bf("AR3", 0, 18 * 516)
    CSTG = c.bf("AR3", 9288, 18 * 512)
    DTR = c.f32("AR3", 18504, T, 16)
    xbv = c.AR3[:, 0:9288].rearrange("p (j w) -> p j w", w=516)

    xf_tok = []
    for j in range(8):
        xf_tok.append(k.dma("sync", XF[j][:], c.xn2_tm_d[0:T, j * 128:(j + 1) * 128],
                            c.s_tr, transpose=True))
        k.dma("gpsimd", WIN[j][:], c.w_in[j * 128:(j + 1) * 128, 2048:4368], c.s_w)
        k.dma("sync", XH[j][:], c.xn2h_tm_d[0:16, j * 128:(j + 1) * 128], c.s_tr, transpose=True)
    for t in range(4):
        k.dma("gpsimd", DIAG[t][:], c.conv_diag[t].rearrange("j p w -> p j w"), c.s_w)
    wtok = c.s_w.tok()
    trtok = c.s_tr.tok()
    aff = []
    for j in range(8):
        aff.append(k.op("scalar", lambda e, j=j: e.activation(
            XF[j][:], XF[j][:], AF.Identity, scale=c.lnw_t[:, j:j + 1], bias=c.lnb_t[:, j:j + 1]),
            deps=[trtok]))
    aff_tok = k.tok("scalar")
    for j in range(8):
        k.op("scalar", lambda e, j=j: e.activation(
            XH[j][:], XH[j][:], AF.Identity, scale=c.lnw_t[:, j:j + 1], bias=c.lnb_t[:, j:j + 1]))
    hf_tok = k.tok("scalar")
    hev = None
    for j in range(18):
        mm = None
        for kc in range(8):
            mm = k.op("tensor", lambda e, j=j, kc=kc: e.matmul(
                c.ps[4][:, 0:16], WIN[kc][:, j * 128:(j + 1) * 128], XH[kc][:],
                start=(kc == 0), stop=(kc == 7)),
                deps=[hf_tok, wtok] + ([hev] if hev else []), inc=(kc == 7))
        hev = k.op("scalar", lambda e, j=j: e.activation(
            xbv[:, j, 0:3],
            c.ps[4][:, 0:3], AF.Identity, scale=c.maskc[:, 0:1]), deps=[mm, c.consts])
    halo_tok = k.tok("scalar")

    NG = 8
    GW = 512
    jw = [(j * 128, 128) for j in range(18)] + [(2304, 16)]
    conv_done = [None] * NG
    evac_done = [None] * NG
    halo_cp = [None] * NG
    spl_prev = None
    for g in range(NG):
        ev = []
        for jj, (joff, jwid) in enumerate(jw):
            bank = c.ps[jj % 2]
            bdep = [evac_done[g - 1]] if (g >= 1 and jj < 2) else []
            mm = None
            for kc in range(8):
                mm = k.op("tensor", lambda e, jj=jj, joff=joff, jwid=jwid, kc=kc, g=g, bank=bank:
                          e.matmul(bank[0:jwid if jwid < 128 else 128, 0:GW],
                                   WIN[kc][:, joff:joff + jwid],
                                   XF[kc][:, g * GW:(g + 1) * GW],
                                   start=(kc == 0), stop=(kc == 7)),
                          deps=([aff_tok, wtok, halo_tok] + bdep +
                                ([ev[jj - 2]] if jj >= 2 else [])),
                          inc=(kc == 7))
            if jj < 18:
                edep = [mm] + ([halo_cp[g - 1]] if g >= 1 else [])
                e1 = k.op("scalar", lambda e, jj=jj, g=g, bank=bank: e.activation(
                    xbv[:, jj, 3:3 + GW], bank[:, 0:GW], AF.Identity), deps=edep)
            else:
                e1 = k.op("scalar", lambda e, g=g, bank=bank: e.activation(
                    DTR[0:16, g * GW:(g + 1) * GW], bank[0:16, 0:GW], AF.Identity), deps=[mm])
            ev.append(e1)
        evac_done[g] = ev[-1]
        wv_tok = k.tok("scalar")
        cv = []
        for j in range(18):
            bank = c.ps[2 + (j % 2)]
            bdep = [cv[j - 2]] if j >= 2 else ([conv_done[g - 1]] if g >= 1 else [])
            mm = None
            for t in range(4):
                mm = k.op("tensor", lambda e, j=j, t=t, g=g, bank=bank: e.matmul(
                    bank[:, 0:GW], DIAG[t][:, j * 128:(j + 1) * 128],
                    xbv[:, j, t:t + GW], start=(t == 0), stop=(t == 3)),
                    deps=[ev[j], wtok] + bdep, inc=(t == 3))
            cv.append(k.op("scalar", lambda e, j=j, g=g, bank=bank: e.activation(
                CSTG[:, j * 512:j * 512 + GW], bank[:, 0:GW], AF.Silu,
                bias=c.convb_t[:, j:j + 1]),
                deps=[mm] + ([spl_prev] if (j == 0 and spl_prev) else [])))
        conv_done[g] = cv[-1]
        halo_cp[g] = k.op("vector", lambda e: e.tensor_copy(
            xbv[:, :, 0:3], xbv[:, :, 512:515]), deps=[k.tok("tensor")])
        spl_prev = k.dma("scalar",
                         c.xbc_fm_d.rearrange("(j p) t -> p j t", p=128)[:, :, g * GW:(g + 1) * GW],
                         c.AR3[:, 9288:9288 + 9216].rearrange("p (j w) -> p j w", w=512),
                         c.s_sp, deps=[conv_done[g]])


def phase_p1b(c):
    k = c.k
    k.fence([c.s_sp, c.s_w, c.s_tr])
    DTR = c.f32("AR3", 18504, T, 16)
    DTF = c.f32("AR1", 0, T, 16)
    SPT = c.f32("AR1", 8192, T, 16)
    AFM = c.f32("AR1", 8192, T, 16)
    XHT = [c.bf("AR1", 16384 + b * 2048, 2048) for b in range(2)]
    XDC = [c.bf("AR1", 20480 + b * 2048, 2048) for b in range(2)]
    BTM = [c.bf("AR1", 24576 + b * 128, 128) for b in range(2)]
    ATM = c.f32("AR1", 24832, 512)
    SFM = [c.f32("AR2", 0, 128, 17), c.f32("AR2", 256, 128, 17)]
    EQROW = c.f32("AR2", 512, 16, 1)
    EQR = c.f32("AR2", 576, 16)
    DECF = [c.f32("AR2", 640 + b * 256, 128, 16) for b in range(2)]
    STMP = c.f32("AR1", 26112, 2048)

    t1 = k.op("vector", lambda e: e.tensor_scalar(DTF[0:16, :], DTR[0:16, :],
                                                  c.dtb_t[0:16, 0:1], None, ALU.add),
              deps=[c.consts])
    t2 = k.op("scalar", lambda e: e.activation(SPT[0:16, :], DTF[0:16, :], AF.Abs), deps=[t1])
    t2b = k.op("vector", lambda e: e.tensor_scalar(SPT[0:16, :], SPT[0:16, :], -1.0, None,
                                                   ALU.mult), deps=[t2])
    t3 = k.op("scalar", lambda e: e.activation(SPT[0:16, :], SPT[0:16, :], AF.Exp), deps=[t2b])
    t3b = k.op("vector", lambda e: e.tensor_scalar(SPT[0:16, :], SPT[0:16, :], 1.0, None,
                                                   ALU.add), deps=[t3])
    t4 = k.op("scalar", lambda e: e.activation(SPT[0:16, :], SPT[0:16, :], AF.Ln), deps=[t3b])
    t5 = k.op("scalar", lambda e: e.activation(DTF[0:16, :], DTF[0:16, :], AF.Relu), deps=[t2])
    t6 = k.op("vector", lambda e: e.tensor_tensor(DTF[0:16, :], DTF[0:16, :], SPT[0:16, :],
                                                  ALU.add), deps=[t4, t5])
    t7 = k.op("vector", lambda e: e.tensor_scalar(AFM[0:16, :], DTF[0:16, :],
                                                  c.acol[0:16, 0:1], None, ALU.mult),
              deps=[t6, c.acol_tok])
    ev_hist = {}
    for cc in range(NCH):
        bk = c.ps[cc % 2]
        pdep = [ev_hist.get(cc - 2)] if cc >= 2 else []
        pdep = [t for t in pdep if t]
        m1 = k.op("tensor", lambda e, cc=cc, bk=bk: e.matmul(
            bk[:, 0:16], DTF[0:16, cc * Q:(cc + 1) * Q], c.i128_t[0:16, 0:16],
            start=True, stop=True), deps=[t6, c.consts] + pdep)
        s1 = k.op("scalar", lambda e, cc=cc, bk=bk: e.activation(
            c.dttm_all[:, cc * 16:(cc + 1) * 16], bk[:, 0:16], AF.Identity), deps=[m1])
        m2 = k.op("tensor", lambda e, cc=cc, bk=bk: e.matmul(
            bk[:, 16:32], AFM[0:16, cc * Q:(cc + 1) * Q], c.i128_t[0:16, 0:16],
            start=True, stop=True), deps=[t7] + pdep)
        s2 = k.op("scalar", lambda e, cc=cc, bk=bk: e.activation(
            ATM[:, cc * 16:(cc + 1) * 16], bk[:, 16:32], AF.Identity), deps=[m2])
        m3 = k.op("tensor", lambda e, cc=cc, bk=bk: e.matmul(
            bk[:, 32:48], c.triu_t[:], ATM[:, cc * 16:(cc + 1) * 16],
            start=True, stop=True), deps=[s2])
        s3 = k.op("scalar", lambda e, cc=cc, bk=bk: e.activation(
            c.stm_all[:, cc * 16:(cc + 1) * 16], bk[:, 32:48], AF.Identity), deps=[m3])
        ev_hist[cc] = s3
    for b_ in range(2):
        k.dma("sync", SFM[b_][16:17, :], c.ones_row[0:1, 0:128], c.s_const)
    ones_tok = c.s_const.tok()
    sp_hist = {}
    dec_hist = {}
    for cc in range(NCH):
        b = cc % 2
        bk = c.ps[2 + (cc % 2)]
        sfm = SFM[b]
        pdep = [c.s_sp.tok()] if cc >= 2 else []
        m4 = k.op("tensor", lambda e, cc=cc, bk=bk: e.matmul(
            bk[:, 0:128][0:16, :], c.stm_all[:, cc * 16:(cc + 1) * 16], c.i128_t[:],
            start=True, stop=True), deps=[ev_hist[cc]] + ([dec_hist.get(cc - 2)] if cc >= 2 else []))
        s4 = k.op("scalar", lambda e, cc=cc, bk=bk, sfm=sfm: e.activation(
            sfm[0:16, :], bk[:, 0:128][0:16, :], AF.Identity),
            deps=[m4, c.consts] + pdep)
        sp1 = k.dma("scalar", c.sfm_d[cc], sfm[0:17, :], c.s_sp, deps=[s4, ones_tok])
        sp2 = k.dma("scalar", c.sflat_d[cc:cc + 1, :].rearrange("a (h t) -> (a h) t", h=16),
                    sfm[0:16, :], c.s_sp, deps=[s4])
        sp_hist[cc] = k.dma("scalar", c.sq_d[cc:cc + 1, :].rearrange("a h -> h a"),
                            sfm[0:16, 127:128], c.s_sp, deps=[s4])
        d0 = k.op("vector", lambda e, b=b, sfm=sfm: e.tensor_scalar(
            DECF[b][0:16, :], sfm[0:16, :], -1.0, None, ALU.mult),
            deps=[s4] + ([dec_hist.get(cc - 2)] if cc >= 2 else []))
        d0b = k.op("vector", lambda e, b=b, sfm=sfm: e.tensor_scalar(
            DECF[b][0:16, :], DECF[b][0:16, :], sfm[0:16, 127:128], None, ALU.add))
        d1 = k.op("scalar", lambda e, b=b: e.activation(DECF[b][0:16, :], DECF[b][0:16, :],
                                                        AF.Exp), deps=[d0b])
        m6 = k.op("tensor", lambda e, b=b, bk=bk: e.matmul(
            bk[:, 128:144], DECF[b][0:16, :], c.i128_t[0:16, 0:16], start=True, stop=True),
            deps=[d1])
        d2 = k.op("scalar", lambda e, cc=cc, bk=bk: e.activation(
            c.dectm_all[:, cc * 16:(cc + 1) * 16], bk[:, 128:144], AF.Identity), deps=[m6])
        dec_hist[cc] = k.op("vector", lambda e, cc=cc: e.tensor_tensor(
            c.dectm_all[:, cc * 16:(cc + 1) * 16], c.dectm_all[:, cc * 16:(cc + 1) * 16],
            c.dttm_all[:, cc * 16:(cc + 1) * 16], ALU.mult), deps=[d2])
    dec_all = k.tok("vector")
    sq_all = c.s_sp.tok()
    sz = k.op("vector", lambda e: e.memset(c.S_t[:], 0.0))
    upd_tok = sz
    add_hist = {}
    xd_hist = {}
    pe_hist = {}
    q2_hist = {}
    xput = [None, None]
    for cc in range(NCH):
        b = cc % 2
        ldep = [t for t in (pe_hist.get(cc - 2), xd_hist.get(cc - 2)) if t]
        k.dma("sync", XHT[b][:], c.xbc_fm_d[0:2048, cc * Q:(cc + 1) * Q], c.s_xh[b],
              transpose=True, deps=ldep)
        xput[b] = k.dma("sync", BTM[b][:], c.xbc_fm_d[2048:2176, cc * Q:(cc + 1) * Q],
                        c.s_xh[b], transpose=True)
        xd = k.op("gpsimd", lambda e, cc=cc, b=b: e.tensor_tensor(
            XDC[b][:].rearrange("p (h w) -> p h w", w=128),
            XHT[b][:].rearrange("p (h w) -> p h w", w=128),
            c.dectm_all[:, cc * 16:(cc + 1) * 16].unsqueeze(2).broadcast_to((128, 16, 128)),
            ALU.mult), deps=[dec_all, xput[b]] + ([pe_hist.get(cc - 2)] if cc >= 2 else []))
        xd_hist[cc] = xd
        lq = k.dma("sync", EQROW[0:1, :], c.sq_d[cc:cc + 1, :], c.s_bnc,
                   deps=[sq_all] + [t for t in (pe_hist.get(cc - 1),) if t])
        q1 = k.op("scalar", lambda e: e.activation(EQROW[0:1, :], EQROW[0:1, :], AF.Exp),
                  deps=[lq] + [t for t in (q2_hist.get(cc - 1),) if t])
        m5 = k.op("tensor", lambda e: e.matmul(c.ps[6][:, 0:16], c.ones1[:],
                                               EQROW[0:1, :], start=True, stop=True),
                  deps=[q1] + [t for t in (q2_hist.get(cc - 1),) if t])
        q2 = k.op("scalar", lambda e: e.activation(EQR[:, 0:16], c.ps[6][:, 0:16],
                                                   AF.Identity), deps=[m5])
        q2_hist[cc] = q2
        mm = None
        for h in range(16):
            pt = c.ps[2 + h // 4]
            mm = k.op("tensor", lambda e, h=h, pt=pt, b=b: e.matmul(
                pt[:, (h % 4) * 128:(h % 4 + 1) * 128], BTM[b][:],
                XDC[b][:, h * 128:(h + 1) * 128], start=True, stop=True),
                deps=[xd, xput[b]] + ([add_hist.get(cc - 1)] if cc >= 1 else []),
                inc=(h % 4 == 3))
        mu0 = k.op("vector", lambda e, cc=cc: e.tensor_tensor(
            STMP[:, 0:1024].rearrange("p (h w) -> p h w", w=128),
            c.S_t[:, 0:1024].rearrange("p (h w) -> p h w", w=128),
            EQR[:, 0:8].unsqueeze(2).broadcast_to((128, 8, 128)), ALU.mult),
            deps=[q2, upd_tok])
        mu1 = k.op("gpsimd", lambda e, cc=cc: e.tensor_tensor(
            STMP[:, 1024:2048].rearrange("p (h w) -> p h w", w=128),
            c.S_t[:, 1024:2048].rearrange("p (h w) -> p h w", w=128),
            EQR[:, 8:16].unsqueeze(2).broadcast_to((128, 8, 128)), ALU.mult),
            deps=[q2, upd_tok])
        aa = None
        for j in range(4):
            aa = k.op("vector", lambda e, j=j: e.tensor_tensor(
                c.S_t[:, j * 512:(j + 1) * 512], STMP[:, j * 512:(j + 1) * 512],
                c.ps[2 + j][:], ALU.add), deps=[mu0 if j < 2 else mu1, mm])
        add_hist[cc] = aa
        upd_tok = aa
        pe_hist[cc] = k.tok("tensor")
    if c.mode == "twoA":
        k.dma("sync", c.s_half_out.rearrange("(h n) p -> n h p", n=128),
              c.S_t[:].rearrange("n (h p) -> n h p", p=128), c.s_out, deps=[upd_tok])
    elif c.mode == "fused":
        sp = k.dma("sync", c.coll_in_d.rearrange("(h n) p -> n h p", n=128),
                   c.S_t[:].rearrange("n (h p) -> n h p", p=128), c.s_bnc, deps=[upd_tok])

        def coll(e):
            return e.collective_compute(
                "AllGather", ALU.bypass, [[0, 1, 2, 3, 4, 5, 6, 7]],
                [c.coll_in_d.opt()], [c.coll_out_d.opt()]).then_inc(c.s_coll.sem, 1)
        c.s_coll.count += 1
        k.op("gpsimd", coll, deps=[sp], inc=False)


def phase_p3z(c):
    k = c.k
    k.fence([c.s_sp, c.s_tr, c.s_w, c.s_bnc])
    XF = [c.bf("AR1", j * 4096, 4096) for j in range(8)]
    WZ = [c.bf("AR2", j * 2048, 2048) for j in range(8)]
    ZST = c.bf("AR3", 0, 16 * 512)
    for j in range(8):
        k.dma("sync", XF[j][:], c.xn2_tm_d[0:T, j * 128:(j + 1) * 128], c.s_tr, transpose=True)
        k.dma("gpsimd", WZ[j][:], c.w_in[j * 128:(j + 1) * 128, 0:2048], c.s_w)
    trtok, wtok = c.s_tr.tok(), c.s_w.tok()
    for j in range(8):
        k.op("scalar", lambda e, j=j: e.activation(
            XF[j][:], XF[j][:], AF.Identity, scale=c.lnw_t[:, j:j + 1], bias=c.lnb_t[:, j:j + 1]),
            deps=[trtok])
    aff_tok = k.tok("scalar")
    ev = []
    sp_prev = None
    for g in range(8):
        for j in range(16):
            bank = c.ps[j % 2]
            mm = None
            for kc in range(8):
                mm = k.op("tensor", lambda e, j=j, kc=kc, g=g, bank=bank: e.matmul(
                    bank[:, 0:512], WZ[kc][:, j * 128:(j + 1) * 128],
                    XF[kc][:, g * 512:(g + 1) * 512], start=(kc == 0), stop=(kc == 7)),
                    deps=[aff_tok, wtok] + ([ev[-2]] if len(ev) >= 2 else []), inc=(kc == 7))
            ev.append(k.op("scalar", lambda e, j=j, bank=bank: e.activation(
                ZST[:, j * 512:(j + 1) * 512], bank[:, 0:512], AF.Silu),
                deps=[mm] + ([sp_prev] if (j < 2 and sp_prev) else [])))
        sp_prev = k.dma("scalar", c.z_fm_d.rearrange("(j p) t -> p j t", p=128)[:, :, g * 512:(g + 1) * 512],
                        ZST[:].rearrange("p (j w) -> p j w", w=512), c.s_sp, deps=[ev[-1]])


def phase_p3(c):
    k = c.k
    k.fence([c.s_sp, c.s_tr, c.s_w])
    XF = [c.bf("AR1", j * 4096, 4096) for j in range(8)]
    for j in range(8):
        k.dma("sync", XF[j][:], c.xn_tm_d[0:T, j * 128:(j + 1) * 128], c.s_tr, transpose=True)
    xn_tok = c.s_tr.tok()
    RW = [c.bf("AR3", 26816 + j * 16, 16) for j in range(8)]
    for j in range(8):
        k.dma("gpsimd", RW[j][:], c.router_wT[j * 128:(j + 1) * 128, :], c.s_w)
    rw_tok = c.s_w.tok()
    lg_ev = [None, None]
    for i in range(NCH):
        bank = c.ps[i % 2]
        mm = None
        for kc in range(8):
            mm = k.op("tensor", lambda e, i=i, kc=kc, bank=bank: e.matmul(
                bank[:, 0:16], XF[kc][:, i * 128:(i + 1) * 128], RW[kc][:],
                start=(kc == 0), stop=(kc == 7)),
                deps=[xn_tok, rw_tok] + ([lg_ev[i % 2]] if lg_ev[i % 2] else []),
                inc=(kc == 7))
        lg_ev[i % 2] = k.op("scalar", lambda e, i=i, bank=bank: e.activation(
            c.comb[:, i * 16:(i + 1) * 16], bank[:, 0:16], AF.Identity), deps=[mm])
    lg_tok = k.tok("scalar")
    _routing_elementwise(c, lg_tok)
    cb_ev = None
    cb_sp = None
    for i in range(NCH):
        m = k.op("tensor", lambda e, i=i: e.matmul(
            c.ps[2][:, 0:128][0:16, :], c.comb[:, i * 16:(i + 1) * 16], c.i128_t[:],
            start=True, stop=True), deps=[k.tok("vector")] + ([cb_ev] if cb_ev else []))
        cb_ev = k.op("scalar", lambda e, i=i: e.activation(c.tiny16[0:16, :],
                                                           c.ps[2][:, 0:128][0:16, :],
                                                           AF.Identity),
                     deps=[m] + ([cb_sp] if cb_sp else []))
        cb_sp = k.dma("scalar", c.comb_fm_d[0:16, i * 128:(i + 1) * 128], c.tiny16[0:16, :],
                      c.s_sp, deps=[cb_ev])
    comb_sp_tok = cb_sp
    route_free = k.tok("vector")
    SGT = [c.bf("AR2", j * 2048, 2048) for j in range(8)]
    SUT = [c.bf("AR2", 16384 + j * 2048, 2048) for j in range(8)]
    for j in range(8):
        k.dma("gpsimd", SGT[j][:], c.sgT[j * 128:(j + 1) * 128, :], c.s_w)
        k.dma("gpsimd", SUT[j][:], c.suT[j * 128:(j + 1) * 128, :], c.s_w)
    sw_tok = c.s_w.tok()
    su_tok = sw_tok
    HST = c.bf("AR3", 0, 16 * 512)
    TMP = [c.f32("AR3", 8192 + t * 1024, 512) for t in range(2)]
    sp_prev = None
    ev_hist = {}
    nj = 0
    for g in range(8):
        ev_last = None
        for j in range(16):
            pg, pu = c.ps[4 + 2 * (j % 2)], c.ps[5 + 2 * (j % 2)]
            mm_g = mm_u = None
            pdep = [ev_hist.get(nj - 2)] if ev_hist.get(nj - 2) else []
            for kc in range(8):
                mm_g = k.op("tensor", lambda e, j=j, kc=kc, g=g, pg=pg: e.matmul(
                    pg[:, 0:512], SGT[kc][:, j * 128:(j + 1) * 128],
                    XF[kc][:, g * 512:(g + 1) * 512], start=(kc == 0), stop=(kc == 7)),
                    deps=[sw_tok, xn_tok] + pdep, inc=(kc == 7))
            for kc in range(8):
                mm_u = k.op("tensor", lambda e, j=j, kc=kc, g=g, pu=pu: e.matmul(
                    pu[:, 0:512], SUT[kc][:, j * 128:(j + 1) * 128],
                    XF[kc][:, g * 512:(g + 1) * 512], start=(kc == 0), stop=(kc == 7)),
                    deps=[su_tok] + pdep, inc=(kc == 7))
            tmp = TMP[j % 2]
            s1 = k.op("scalar", lambda e, tmp=tmp, pg=pg: e.activation(tmp[:], pg[:], AF.Silu),
                      deps=[mm_g] + pdep)
            ev_last = k.op("vector", lambda e, j=j, tmp=tmp, pu=pu: e.tensor_tensor(
                HST[:, j * 512:(j + 1) * 512], pu[:], tmp[:], ALU.mult),
                deps=[s1, mm_u] + ([sp_prev] if (j < 1 and sp_prev) else []))
            ev_hist[nj] = ev_last
            nj += 1
        sp_prev = k.dma("scalar",
                        c.h_fm_d.rearrange("(j p) t -> p j t", p=128)[:, 32:48, g * 512:(g + 1) * 512],
                        HST[:].rearrange("p (j w) -> p j w", w=512), c.s_sp, deps=[ev_last])
    GWB = [[c.bf("AR2", b * 8192 + j * 256, 256) for j in range(8)] for b in range(2)]
    UWB = [[c.bf("AR2", 16384 + b * 8192 + j * 256, 256) for j in range(2 * 4)] for b in range(2)]
    CRP = [c.f32("AR3", 10240 + b * 1024, 512) for b in range(2)]
    TMPE = [c.f32("AR3", 12288 + t * 1024, 512) for t in range(2)]
    HS2 = c.bf("AR3", 0, 2 * 512)
    sh_sp_tok = sp_prev
    sh_pe_tok = k.tok("tensor")
    pair_free = [ev_hist.get(nj - 2), ev_hist.get(nj - 1)]
    wtoks = [None, None]
    ctoks = [None, None]
    sp_toks = {}
    pe_ev = None
    for e_i in range(16):
        b = e_i % 2
        wdep = [c.s_sp.tok()] if e_i >= 2 else [sh_pe_tok]
        k.dma("gpsimd", c.AR2[:, b * 8192:b * 8192 + 2048].rearrange(
            "p (kc j) -> p kc j", j=256),
            c.gate_w[e_i].rearrange("(kc p) j -> p kc j", p=128), c.s_we[b], deps=wdep)
        wtoks[b] = k.dma("gpsimd", c.AR2[:, 16384 + b * 8192:16384 + b * 8192 + 2048].rearrange(
            "p (kc j) -> p kc j", j=256),
            c.up_w[e_i].rearrange("(kc p) j -> p kc j", p=128), c.s_we[b])
        for g in range(8):
            bb = g % 2
            cdep = [c.s_sp.tok()]
            ctoks[bb] = k.dma("sync", CRP[bb][:],
                              c.comb_fm_d[e_i:e_i + 1, g * 512:(g + 1) * 512].broadcast_to((128, 512)),
                              c.s_cr[bb], deps=cdep + [comb_sp_tok, route_free])
            v2 = None
            for jj in range(2):
                pg, pu = c.ps[4 + 2 * jj], c.ps[5 + 2 * jj]
                mm_g = mm_u = None
                pbank = [t for t in (pair_free[jj],) if t]
                for kc in range(8):
                    mm_g = k.op("tensor", lambda e, jj=jj, kc=kc, g=g, b=b, pg=pg: e.matmul(
                        pg[:, 0:512],
                        c.AR2[:, b * 8192 + kc * 256 + jj * 128: b * 8192 + kc * 256 + jj * 128 + 128],
                        XF[kc][:, g * 512:(g + 1) * 512], start=(kc == 0), stop=(kc == 7)),
                        deps=[wtoks[b], xn_tok] + pbank, inc=(kc == 7))
                for kc in range(8):
                    mm_u = k.op("tensor", lambda e, jj=jj, kc=kc, g=g, b=b, pu=pu: e.matmul(
                        pu[:, 0:512],
                        c.AR2[:, 16384 + b * 8192 + kc * 256 + jj * 128: 16384 + b * 8192 + kc * 256 + jj * 128 + 128],
                        XF[kc][:, g * 512:(g + 1) * 512], start=(kc == 0), stop=(kc == 7)),
                        deps=pbank, inc=(kc == 7))
                tmp = TMPE[jj]
                s1 = k.op("scalar", lambda e, tmp=tmp, pg=pg: e.activation(tmp[:], pg[:], AF.Silu),
                          deps=[mm_g] +
                          ([sh_sp_tok] if (e_i == 0 and g == 0 and jj == 0) else []))
                v1 = k.op("vector", lambda e, tmp=tmp, pu=pu: e.tensor_tensor(
                    tmp[:], pu[:], tmp[:], ALU.mult), deps=[s1, mm_u])
                v2 = k.op("vector", lambda e, jj=jj, bb=bb, tmp=tmp: e.tensor_tensor(
                    HS2[:, jj * 512:(jj + 1) * 512], tmp[:], CRP[bb][:], ALU.mult),
                    deps=[v1, ctoks[bb]] +
                    ([sh_sp_tok] if (e_i == 0 and g == 0 and jj == 0) else []))
                pair_free[jj] = v2
            pe_ev = k.tok("vector")
            sp_toks[(e_i, g)] = k.dma(
                "scalar",
                c.h_fm_d.rearrange("(j p) t -> p j t", p=128)[:, 2 * e_i:2 * e_i + 2,
                                                              g * 512:(g + 1) * 512],
                HS2[:].rearrange("p (j w) -> p j w", w=512), c.s_sp,
                deps=[pe_ev])
    dwt = {}
    wqs = [c.s_w, c.s_w2, c.s_w3]
    qi = 0
    for e_i in range(16):
        if e_i % 4 == 0:
            continue
        dwt[e_i] = k.dma("gpsimd",
                         c.AR2[:, e_i * 2048:(e_i + 1) * 2048].rearrange("p (kc d) -> p kc d", d=1024),
                         c.down_w[e_i].rearrange("(kc p) d -> p kc d", p=128),
                         wqs[qi % 3], deps=[sh_pe_tok])
        qi += 1
    k.fence([c.s_sp, c.s_g2])
    DW = [c.bf("AR2", i * 1024, 1024) for i in range(32)]
    SD = [c.bf("AR1", i * 1024, 1024) for i in range(16)]
    for e_i in (0, 4, 8, 12):
        dwt[e_i] = k.dma("gpsimd",
                         c.AR2[:, e_i * 2048:(e_i + 1) * 2048].rearrange("p (kc d) -> p kc d", d=1024),
                         c.down_w[e_i].rearrange("(kc p) d -> p kc d", p=128), wqs[e_i // 4 % 3])
    sdt = {}
    for j in range(16):
        sdt[j] = k.dma("gpsimd", SD[j][:], c.sdT[j * 128:(j + 1) * 128, :], wqs[j % 3])
    HLD = [c.bf("AR3", b * 6144, 6144) for b in range(2)]
    GT = c.f32("AR3", 12288, 1024)
    htoks = [None, None]
    gsp = None
    gev = [None] * NCH
    pe_hist = [None] * NCH
    for g in range(NCH):
        b = g % 2
        hdep = [pe_hist[g - 2]] if g >= 2 and pe_hist[g - 2] else []
        htoks[b] = k.dma("sync", HLD[b][:].rearrange("p (j w) -> p j w", w=128),
                         c.h_fm_d.rearrange("(j p) t -> p j t", p=128)[:, :, g * Q:(g + 1) * Q],
                         c.s_hl[b], deps=hdep)
        pd = [c.ps[0 + 2 * (g % 2)], c.ps[1 + 2 * (g % 2)]]
        mm = None
        for half in range(2):
            first = True
            for j in range(48):
                src = (c.AR2[:, (j // 2) * 2048 + (j % 2) * 1024 + half * 512:
                             (j // 2) * 2048 + (j % 2) * 1024 + half * 512 + 512]
                       if j < 32 else SD[j - 32][:, half * 512:half * 512 + 512])
                bankdep = [gev[g - 2]] if (first and g >= 2 and gev[g - 2]) else []
                wtok = dwt[j // 2] if j < 32 else sdt[j - 32]
                mm = k.op("tensor", lambda e, j=j, half=half, b=b, src=src, pd=pd, first=first:
                          e.matmul(pd[half][:, 0:512],
                                   HLD[b][:, j * 128:(j + 1) * 128], src,
                                   start=first, stop=(j == 47)),
                          deps=[htoks[b], wtok] + bankdep, inc=(j == 47))
                first = False
        pe_hist[g] = k.tok("tensor")
        gdep = [gsp] if gsp else []
        e1 = k.op("scalar", lambda e, pd=pd: e.activation(GT[:, 0:512], pd[0][:], AF.Silu),
                  deps=[mm] + gdep)
        e2 = k.op("scalar", lambda e, pd=pd: e.activation(GT[:, 512:1024], pd[1][:], AF.Silu))
        gev[g] = e2
        gsp = k.dma("scalar", c.g_tm_d[g * Q:(g + 1) * Q, :], GT[:], c.s_sp, deps=[e2])


def _routing_elementwise(c, lg_tok):
    k = c.k
    SC = c.f32("AR3", 0, 512)
    SBv = c.f32("AR3", 1024, 512)
    GS = c.f32("AR3", 2048, 128)
    GM = c.f32("AR3", 2304, 128)
    E1 = c.f32("AR3", 2560, 128)
    E2 = c.f32("AR3", 2816, 128)
    MK = c.f32("AR3", 3072, 512)
    SEL = c.f32("AR3", 4096, 512)
    EQ = c.f32("AR3", 5120, 512)
    CU = c.f32("AR3", 6144, 512)
    CV = c.f32("AR3", 7168, 512)
    MX = c.f32("AR3", 8192, 32)
    CB = c.f32("AR3", 8256, 16)
    WS = c.f32("AR3", 8288, 32)
    t0 = k.op("vector", lambda e: e.tensor_scalar(SC[:], c.comb[:], 0.5, None, ALU.mult),
              deps=[lg_tok])
    t1 = k.op("scalar", lambda e: e.activation(SC[:], SC[:], AF.Tanh), deps=[t0])
    t2 = k.op("vector", lambda e: e.tensor_scalar(SC[:], SC[:], 0.5, 0.5, ALU.mult, ALU.add),
              deps=[t1])
    m0 = k.op("tensor", lambda e: e.matmul(c.ps[3][:, 0:16], c.ones1[:], c.corrb_t[:],
                                           start=True, stop=True), deps=[c.consts])
    c0 = k.op("scalar", lambda e: e.activation(CB[:, 0:16], c.ps[3][:, 0:16], AF.Identity),
              deps=[m0])
    t3 = k.op("vector", lambda e: e.tensor_tensor(
        SBv[:].rearrange("p (a b) -> p a b", b=16), SC[:].rearrange("p (a b) -> p a b", b=16),
        CB[:, 0:16].unsqueeze(1).broadcast_to((128, 32, 16)), ALU.add), deps=[t2, c0])
    V = SBv.rearrange("p (a b) -> p a b", b=4)
    first = True
    for i in range(4):
        for j in range(i + 1, 4):
            if first:
                k.op("vector", lambda e, i=i, j=j: e.tensor_tensor(GS[:], V[:, :, i], V[:, :, j],
                                                                   ALU.add), deps=[t3])
                first = False
            else:
                k.op("vector", lambda e, i=i, j=j: e.tensor_tensor(CU[:, 0:128], V[:, :, i],
                                                                   V[:, :, j], ALU.add))
                k.op("vector", lambda e: e.tensor_tensor(GS[:], GS[:], CU[:, 0:128], ALU.max))
    k.op("vector", lambda e: e.memset(GM[:], 0.0))
    GSv = GS.rearrange("p (a b) -> p a b", b=4)
    for it in range(2):
        k.op("vector", lambda e: e.tensor_reduce(MX[:].rearrange("p (a b) -> p a b", b=1),
                                                 GSv, AX.X, ALU.max))
        k.op("vector", lambda e: e.tensor_tensor(
            E1[:].rearrange("p (a b) -> p a b", b=4), GSv,
            MX[:].unsqueeze(2).broadcast_to((128, 32, 4)), ALU.is_equal))
        k.op("vector", lambda e: e.tensor_copy(E2[:], E1[:]))
        k.op("vector", lambda e: e.tensor_tensor(
            E2[:].rearrange("p (a b) -> p a b", b=4)[:, :, 1:4],
            E1[:].rearrange("p (a b) -> p a b", b=4)[:, :, 1:4],
            E1[:].rearrange("p (a b) -> p a b", b=4)[:, :, 0:3], ALU.add))
        k.op("vector", lambda e: e.tensor_copy(CU[:, 0:128], E2[:]))
        k.op("vector", lambda e: e.tensor_tensor(
            CU[:, 0:128].rearrange("p (a b) -> p a b", b=4)[:, :, 2:4],
            E2[:].rearrange("p (a b) -> p a b", b=4)[:, :, 2:4],
            E2[:].rearrange("p (a b) -> p a b", b=4)[:, :, 0:2], ALU.add))
        k.op("vector", lambda e: e.tensor_tensor(CU[:, 0:128], CU[:, 0:128], E1[:],
                                                 ALU.subtract))
        k.op("vector", lambda e: e.tensor_scalar(CU[:, 0:128], CU[:, 0:128], 0.0, None,
                                                 ALU.is_equal))
        k.op("vector", lambda e: e.tensor_tensor(E1[:], E1[:], CU[:, 0:128], ALU.mult))
        k.op("vector", lambda e: e.tensor_tensor(GM[:], GM[:], E1[:], ALU.add))
        if it == 0:
            k.op("vector", lambda e: e.scalar_tensor_tensor(GS[:], E1[:], NEG, GS[:],
                                                            ALU.mult, ALU.add))
    k.op("vector", lambda e: e.tensor_scalar(GM[:], GM[:], -1.0, NEG, ALU.add, ALU.mult))
    k.op("vector", lambda e: e.tensor_scalar(GM[:], GM[:], -1.0, None, ALU.mult))
    k.op("vector", lambda e: e.tensor_tensor(
        MK[:].rearrange("p (a b) -> p a b", b=4), SBv.rearrange("p (a b) -> p a b", b=4),
        GM[:].unsqueeze(2).broadcast_to((128, 128, 4)), ALU.add))
    k.op("vector", lambda e: e.memset(SEL[:], 0.0))
    MKv = MK.rearrange("p (a b) -> p a b", b=16)
    for it in range(4):
        k.op("vector", lambda e: e.tensor_reduce(MX[:].rearrange("p (a b) -> p a b", b=1),
                                                 MKv, AX.X, ALU.max))
        k.op("vector", lambda e: e.tensor_tensor(
            EQ[:].rearrange("p (a b) -> p a b", b=16), MKv,
            MX[:].unsqueeze(2).broadcast_to((128, 32, 16)), ALU.is_equal))
        k.op("vector", lambda e: e.tensor_copy(CU[:], EQ[:]))
        src, dst = CU, CV
        for sh in (1, 2, 4, 8):
            k.op("vector", lambda e, sh=sh, src=src, dst=dst: e.tensor_copy(
                dst[:].rearrange("p (a b) -> p a b", b=16)[:, :, 0:sh],
                src[:].rearrange("p (a b) -> p a b", b=16)[:, :, 0:sh]))
            k.op("vector", lambda e, sh=sh, src=src, dst=dst: e.tensor_tensor(
                dst[:].rearrange("p (a b) -> p a b", b=16)[:, :, sh:16],
                src[:].rearrange("p (a b) -> p a b", b=16)[:, :, sh:16],
                src[:].rearrange("p (a b) -> p a b", b=16)[:, :, 0:16 - sh], ALU.add))
            src, dst = dst, src
        k.op("vector", lambda e, src=src: e.tensor_tensor(CV[:] if src is CU else CU[:],
                                                          src[:], EQ[:], ALU.subtract))
        exv = CV if src is CU else CU
        k.op("vector", lambda e, exv=exv: e.tensor_scalar(exv[:], exv[:], 0.0, None,
                                                          ALU.is_equal))
        k.op("vector", lambda e, exv=exv: e.tensor_tensor(EQ[:], EQ[:], exv[:], ALU.mult))
        k.op("vector", lambda e: e.tensor_tensor(SEL[:], SEL[:], EQ[:], ALU.add))
        if it < 3:
            k.op("vector", lambda e: e.scalar_tensor_tensor(MK[:], EQ[:], NEG, MK[:],
                                                            ALU.mult, ALU.add))
    k.op("vector", lambda e: e.tensor_tensor(SEL[:], SEL[:], SC[:], ALU.mult))
    k.op("vector", lambda e: e.tensor_reduce(WS[:].rearrange("p (a b) -> p a b", b=1),
                                             SEL[:].rearrange("p (a b) -> p a b", b=16),
                                             AX.X, ALU.add))
    k.op("vector", lambda e: e.tensor_scalar(WS[:], WS[:], 1e-20, None, ALU.add))
    k.op("vector", lambda e: e.reciprocal(WS[:], WS[:]))
    k.op("vector", lambda e: e.tensor_tensor(
        c.comb[:].rearrange("p (a b) -> p a b", b=16),
        SEL[:].rearrange("p (a b) -> p a b", b=16),
        WS[:].unsqueeze(2).broadcast_to((128, 32, 16)), ALU.mult))


import os as _os
DBG_NODEFER = bool(int(_os.environ.get("DBG_NODEFER", "0")))


def phase_p2a(c):
    k = c.k
    if c.mode != "twoB":
        SG2 = [c.f32("AR3", 14336 + b_ * 2048, 1024) for b_ in range(2)]
        init_tok = k.op("vector", lambda e: e.memset(c.S_t[:], 0.0))
        sg_toks = {}
        for idx in range(16):
            r, half = idx // 2, idx % 2
            b2 = idx % 2
            ld = k.dma("sync", SG2[b2][:].rearrange("n (h p) -> n h p", p=128),
                       c.coll_out_d[r * DI + half * 1024:r * DI + (half + 1) * 1024]
                       .rearrange("(h n) p -> n h p", n=128),
                       c.s_in, deps=[(c.s_coll.sem, c.s_coll.count), init_tok] +
                       ([sg_toks[idx - 2]] if idx >= 2 else []))
            sg_toks[idx] = k.op("vector", lambda e, r=r, half=half, b2=b2:
                                e.scalar_tensor_tensor(
                c.S_t[:, half * 1024:(half + 1) * 1024], SG2[b2][:],
                c.smask_t[:, r:r + 1], c.S_t[:, half * 1024:(half + 1) * 1024],
                ALU.mult, ALU.add), deps=[ld, c.consts])
        init_tok = sg_toks[15]
    k.fence([c.s_sp, c.s_tr, c.s_w, c.s_g2, c.s_bnc])
    XHT = [c.bf("AR1", b * 2048, 2048) for b in range(2)]
    SZT = [c.bf("AR1", 4096 + b * 2048, 2048) for b in range(2)]
    LL = [c.bf("AR1", 8192 + b * 2048, 2048) for b in range(2)]
    MT = [c.bf("AR1", 12288 + b * 2048, 2048) for b in range(2)]
    XTI = [c.bf("AR1", 16384 + b * 2048, 2048) for b in range(2)]
    XDC = [c.bf("AR1", 20480 + b * 2048, 2048) for b in range(2)]
    SBF = c.bf("AR1", 24576, 2048)
    YTM = c.f32("AR1", 26624, 2048)
    YBF = [c.bf("AR2", b * 2048, 2048) for b in range(2)]
    BTM = [c.bf("AR2", 4096 + b * 128, 128) for b in range(2)]
    BFM = [c.bf("AR2", 4352 + b * 128, 128) for b in range(2)]
    CFM = [c.bf("AR2", 4608 + b * 128, 128) for b in range(2)]
    GSB = [c.f32("AR2", 4864 + b * 256, 128) for b in range(2)]
    S17 = [c.f32("AR2", 5376 + b * 256, 128, 17) for b in range(2)]
    ECOL = [c.f32("AR2", 5888 + b * 32, 16) for b in range(2)]
    EQR = [c.f32("AR2", 5952 + b * 32, 16) for b in range(2)]
    EQROW = [c.f32("AR2", 6016 + b * 32, 16, 1) for b in range(2)]
    T2E = [c.bf("AR2", 6080, 512), c.bf("AR2", 6592, 512)]
    DMS = [c.f32("AR2", 7104 + b * 4096, 2048) for b in range(2)]
    Y2 = c.f32("AR2", 15296, 2048)
    STMP = c.f32("AR2", 19392, 2048)
    DRHS = [c.f32("AR2", 23488 + b * 4096, 2048, 17) for b in range(2)]
    msqz = k.op("vector", lambda e: e.memset(c.msq_all[:], 0.0))
    nd_tok = [k.dma("sync", DRHS[b_][0:16, :], c.negd, c.s_const) for b_ in range(2)][-1]
    if c.mode == "twoB":
        si = k.dma("sync", c.S_t[:].rearrange("n (h p) -> n h p", p=128),
                   c.s_init_in.rearrange("(h n) p -> n h p", n=128), c.s_in)
        init_tok = k.op("vector", lambda e: e.engine_nop(), deps=[si])
    st = dict(upd=init_tok, yb=None, m5=[None, None], dbuild=[None, None],
              lexp=[None, None], dv_e=[None, None], ge=[None, None], q2=[None, None],
              scst=None, prev_yt=None, prev_yt2=None, dvo_tok=None,
              xput=[None, None], bput=[None, None], szput=[None, None],
              bfput=[None, None], cfput=[None, None], s17put=[None, None],
              mt={}, xt={}, xd={}, ectok={}, eq={}, lastmm1={}, gt={}, ds={},
              gm={}, mm2l={}, mul={}, e2l={}, ms={}, getok={}, q1=[None, None],
              dl01=[None, None], dv3=None)

    def d23(cc, ytk3=None):
        b = cc % 2
        dl = st["dl01"][b]
        for gi in (2, 3):
            pD = c.ps[1] if gi % 2 == 0 else c.ps[6]
            ddep = [dl[gi - 2]] + ([ytk3] if (gi == 3 and ytk3) else [])
            dm = k.op("tensor", lambda e, gi=gi, b=b, pD=pD: e.matmul(
                pD[:, 0:512], S17[b][0:17, :], DRHS[b][0:17, gi * 512:(gi + 1) * 512],
                start=True, stop=True), deps=ddep)
            st["dbuild"][b] = dm
            dv = k.op("vector", lambda e, gi=gi, b=b, pD=pD: e.tensor_tensor(
                DMS[b][:, gi * 512:(gi + 1) * 512], pD[:, 0:512], c.tricap_t[:],
                ALU.min), deps=[dm, c.consts])
            if gi == 2:
                st["dv_e"][b] = dv
            else:
                st["dv3"] = dv
            k.op("scalar", lambda e, gi=gi, b=b: e.activation(
                LL[b][:, gi * 512:(gi + 1) * 512], DMS[b][:, gi * 512:(gi + 1) * 512],
                AF.Exp), deps=[dv])
        st["lexp"][b] = k.tok("scalar")

    def eqg(cc):
        b = cc % 2
        m5 = k.op("tensor", lambda e, b=b: e.matmul(c.ps[0][:, 128 + b * 16:144 + b * 16],
                                                    c.ones1[:], EQROW[b][0:1, :],
                                                    start=True, stop=True),
                  deps=[st["q1"][b]] + [t for t in (st["q2"][b],) if t])
        q2 = k.op("scalar", lambda e, b=b: e.activation(EQR[b][:, 0:16],
                                                        c.ps[0][:, 128 + b * 16:144 + b * 16],
                                                        AF.Identity),
                  deps=[m5] + [t for t in (st["mul"].get(cc - 2),) if t])
        st["m5"][b] = m5
        st["q2"][b] = q2
        st["eq"][cc] = q2
        gm = k.op("tensor", lambda e, b=b: e.matmul(c.ps[0][:, 0:128], BFM[b][:], CFM[b][:],
                                                    start=True, stop=True),
                  deps=[st["bfput"][b], q2] + [t for t in (st["ge"][1 - b],) if t])
        st["gm"][cc] = gm
        ge = k.op("scalar", lambda e, b=b: e.activation(GSB[b][:], c.ps[0][:, 0:128],
                                                        AF.Identity),
                  deps=[gm] + [t for t in (st["mt"].get(cc - 2),) if t])
        st["ge"][b] = ge
        st["getok"][cc] = ge

    def mtxt(cc):
        b = cc % 2
        mt = k.op("gpsimd", lambda e, b=b: e.tensor_tensor(
            MT[b][:].rearrange("p (h w) -> p h w", w=128),
            LL[b][:].rearrange("p (h w) -> p h w", w=128),
            GSB[b][:].unsqueeze(1).broadcast_to((128, 16, 128)), ALU.mult),
            deps=[st["lexp"][b], st["getok"][cc]] +
            [t for t in (st["lastmm1"].get(cc - 2),) if t])
        xt = k.op("gpsimd", lambda e, cc=cc, b=b: e.tensor_tensor(
            XTI[b][:].rearrange("p (h w) -> p h w", w=128),
            XHT[b][:].rearrange("p (h w) -> p h w", w=128),
            c.dttm_all[:, cc * 16:(cc + 1) * 16].unsqueeze(2).broadcast_to((128, 16, 128)),
            ALU.mult), deps=[st["xput"][b]] + [t for t in (st["lastmm1"].get(cc - 2),) if t])
        st["mt"][cc] = mt
        st["xt"][cc] = xt

    def xdo(cc):
        b = cc % 2
        st["xd"][cc] = k.op("vector", lambda e, cc=cc, b=b: e.tensor_tensor(
            XDC[b][:].rearrange("p (h w) -> p h w", w=128),
            XHT[b][:].rearrange("p (h w) -> p h w", w=128),
            c.dectm_all[:, cc * 16:(cc + 1) * 16].unsqueeze(2).broadcast_to((128, 16, 128)),
            ALU.mult), deps=[st["xput"][b]] + [t for t in (st["ds"].get(cc - 2),) if t])

    def s1(cc):
        b = cc % 2
        eqput = k.dma("sync", EQROW[b][0:1, :], c.sq_d[cc:cc + 1, :], c.s_pb[b],
              deps=[t for t in (st["m5"][b],) if t])
        s17dep = [t for t in (st["dbuild"][b],) if t]
        st["s17put"][b] = k.dma("sync", S17[b][:], c.sfm_d[cc], c.s_pb[b], deps=s17dep)
        drput = k.dma("sync", DRHS[b][16:17, :], c.sflat_d[cc:cc + 1, :], c.s_pb[b])
        gpb = [t for t in (st["gm"].get(cc - 2), st["mm2l"].get(cc - 2)) if t]
        k.dma("sync", BFM[b][:], c.xbc_fm_d[2048:2176, cc * Q:(cc + 1) * Q],
              c.s_pb[b], deps=gpb)
        st["bfput"][b] = st["cfput"][b] = k.dma(
            "sync", CFM[b][:], c.xbc_fm_d[2176:2304, cc * Q:(cc + 1) * Q], c.s_pb[b])
        gpa = [t for t in (st["lastmm1"].get(cc - 2), st["gt"].get(cc - 2),
                           st["ds"].get(cc - 2)) if t]
        k.dma("sync", XHT[b][:], c.xbc_fm_d[0:2048, cc * Q:(cc + 1) * Q], c.s_pa[b],
              transpose=True, deps=gpa)
        k.dma("sync", BTM[b][:], c.xbc_fm_d[2048:2176, cc * Q:(cc + 1) * Q], c.s_pa[b],
              transpose=True)
        st["xput"][b] = st["bput"][b] = st["szput"][b] = k.dma(
            "sync", SZT[b][:], c.z_fm_d[0:2048, cc * Q:(cc + 1) * Q], c.s_pa[b],
            transpose=True)
        ec = k.op("scalar", lambda e, cc=cc, b=b: e.activation(
            ECOL[b][:, 0:16], c.stm_all[:, cc * 16:(cc + 1) * 16], AF.Exp),
            deps=[t for t in (st["e2l"].get(cc - 2),) if t])
        st["ectok"][cc] = ec
        dl = [None, None]
        for gi in range(2):
            pD = c.ps[1] if gi % 2 == 0 else c.ps[6]
            ddep = [st["s17put"][b], drput, nd_tok]
            ddep += ([t for t in (st["dv_e"][1 - b],) if t] if gi == 0 else
                     [t for t in (st["prev_yt"], st["dv3"]) if t])
            dm = k.op("tensor", lambda e, gi=gi, b=b, pD=pD: e.matmul(
                pD[:, 0:512], S17[b][0:17, :], DRHS[b][0:17, gi * 512:(gi + 1) * 512],
                start=True, stop=True), deps=ddep)
            dv = k.op("vector", lambda e, gi=gi, b=b, pD=pD: e.tensor_tensor(
                DMS[b][:, gi * 512:(gi + 1) * 512], pD[:, 0:512], c.tricap_t[:],
                ALU.min), deps=[dm, c.consts] +
                [t for t in (st["lexp"][b],) if t and gi == 0])
            dl[gi] = dv
            k.op("scalar", lambda e, gi=gi, b=b: e.activation(
                LL[b][:, gi * 512:(gi + 1) * 512], DMS[b][:, gi * 512:(gi + 1) * 512],
                AF.Exp), deps=[dv] + [t for t in (st["mt"].get(cc - 2),) if t and gi == 0])
        st["dl01"][b] = dl
        st["dvo_tok"] = dl[1]
        st["q1"][b] = k.op("scalar", lambda e, b=b: e.activation(
            EQROW[b][0:1, :], EQROW[b][0:1, :], AF.Exp), deps=[eqput])
        if cc == 0 or DBG_NODEFER:
            d23(0)
            eqg(0)
            mtxt(0)
            xdo(0)

    def s23(cc):
        b = cc % 2
        sca = scb = k.op("scalar", lambda e: e.activation(SBF[:], c.S_t[:], AF.Identity),
                         deps=[st["upd"]] + [t for t in (st["mm2l"].get(cc - 1),) if t])
        ytoks = []
        e2_prev = {0: st["prev_yt2"], 1: st["prev_yt"]}
        mm2 = None
        e2 = None
        for hg in range(4):
            p1 = c.ps[4 + 2 * (hg % 2)]
            p2 = c.ps[5 + 2 * (hg % 2)]
            if hg >= 2:
                bank_free = [ytoks[hg - 2]]
            else:
                bank_free = [t for t in ((st["prev_yt2"],) if hg == 0 else
                                         (st["prev_yt"],)) if t]
            if hg % 2 == 1:
                bank_free = bank_free + [t for t in (st["dvo_tok"], st["dv3"]) if t]
            mm1 = None
            for hh in range(4):
                h = hg * 4 + hh
                mm1 = k.op("tensor", lambda e, h=h, hh=hh, p1=p1, b=b: e.matmul(
                    p1[:, hh * 128:(hh + 1) * 128], MT[b][:, h * 128:(h + 1) * 128],
                    XTI[b][:, h * 128:(h + 1) * 128], start=True, stop=False),
                    deps=[st["mt"][cc], st["xt"][cc]] + bank_free, inc=False)
                mm1 = k.op("tensor", lambda e, h=h, hh=hh, p1=p1, b=b: e.matmul(
                    p1[:, hh * 128:(hh + 1) * 128], c.dhi[:, h * 128:(h + 1) * 128],
                    XHT[b][:, h * 128:(h + 1) * 128], start=False, stop=True),
                    deps=[c.dhi_tok], inc=(hh == 3))
            scd = sca if hg < 2 else scb
            for hh in range(4):
                h = hg * 4 + hh
                mm2 = k.op("tensor", lambda e, h=h, hh=hh, p2=p2, b=b: e.matmul(
                    p2[:, hh * 128:(hh + 1) * 128], CFM[b][:], SBF[:, h * 128:(h + 1) * 128],
                    start=True, stop=True), deps=[scd, st["cfput"][b]] + bank_free,
                    inc=(hh == 3))
            for hh in range(4):
                h = hg * 4 + hh
                e2 = k.op("scalar", lambda e, h=h, hh=hh, p2=p2, hg=hg, b=b: e.activation(
                    T2E[hg % 2][:, hh * 128:(hh + 1) * 128], p2[:, hh * 128:(hh + 1) * 128],
                    AF.Identity, scale=ECOL[b][:, h:h + 1]),
                    deps=[mm2, st["ectok"][cc]] +
                    ([e2_prev[hg % 2]] if (hg < 2 and e2_prev[hg % 2]) else
                     [ytoks[hg - 2]] if hg >= 2 else []))
            yt = k.op("vector", lambda e, hg=hg, p1=p1: e.tensor_tensor(
                YTM[:, hg * 512:(hg + 1) * 512], p1[:, 0:512], T2E[hg % 2][:], ALU.add),
                deps=[e2, mm1] + [t for t in (st["ms"].get(cc - 1),) if t])
            ytoks.append(yt)
        st["mm2l"][cc] = mm2
        st["e2l"][cc] = e2
        st["prev_yt"] = ytoks[3]
        st["prev_yt2"] = ytoks[2]
        st["lastmm1"][cc] = ytoks[3]
        if cc + 1 < NCH and not DBG_NODEFER:
            eqg(cc + 1)
            d23(cc + 1, ytk3=ytoks[3])
        mm_ = None
        for half in range(2):
            pa, pb2 = c.ps[2], c.ps[3]
            for hh in range(8):
                h = half * 8 + hh
                pt = pa if hh < 4 else pb2
                mm_ = k.op("tensor", lambda e, h=h, hh=hh, pt=pt, b=b: e.matmul(
                    pt[:, (hh % 4) * 128:(hh % 4 + 1) * 128], BTM[b][:],
                    XDC[b][:, h * 128:(h + 1) * 128], start=True, stop=True),
                    deps=[st["xd"][cc], st["bput"][b]] +
                    ([st["upd"], sca] if hh == 0 else []), inc=(hh == 7))
            mu = k.op("gpsimd", lambda e, half=half, b=b: e.tensor_tensor(
                STMP[:, half * 1024:(half + 1) * 1024].rearrange("p (h w) -> p h w", w=128),
                c.S_t[:, half * 1024:(half + 1) * 1024].rearrange("p (h w) -> p h w", w=128),
                EQR[b][:, half * 8:half * 8 + 8].unsqueeze(2).broadcast_to((128, 8, 128)),
                ALU.mult), deps=[st["eq"][cc], st["upd"], sca])
            st["mul"][cc] = mu
            a1 = k.op("vector", lambda e, half=half, pa=pa: e.tensor_tensor(
                c.S_t[:, half * 1024:half * 1024 + 512], STMP[:, half * 1024:half * 1024 + 512],
                pa[:], ALU.add), deps=[mu, mm_])
            st["upd"] = k.op("vector", lambda e, half=half, pb2=pb2: e.tensor_tensor(
                c.S_t[:, half * 1024 + 512:half * 1024 + 1024],
                STMP[:, half * 1024 + 512:half * 1024 + 1024], pb2[:], ALU.add))
        st["ds"][cc] = mm_
        gta = k.op("vector", lambda e, b=b: e.tensor_tensor(
            YTM[:, 0:1024], YTM[:, 0:1024], SZT[b][:, 0:1024], ALU.mult),
            deps=[ytoks[-1], st["szput"][b]])
        gtb = k.op("gpsimd", lambda e, b=b: e.tensor_tensor(
            YTM[:, 1024:2048], YTM[:, 1024:2048], SZT[b][:, 1024:2048], ALU.mult),
            deps=[ytoks[-1], st["szput"][b]] + [t for t in (st["ms"].get(cc - 1),) if t])
        gt = gta
        st["gt"][cc] = gtb
        ms = k.op("scalar", lambda e, cc=cc: e.activation(Y2[:], YTM[:], AF.Square,
                                                          accum_out=c.msq_all[:, cc:cc + 1]),
                  deps=[gta, gtb, msqz])
        st["ms"][cc] = ms
        r1 = k.op("vector", lambda e, cc=cc: e.tensor_scalar(
            c.tiny[:, 32:33], c.msq_all[:, cc:cc + 1], 1.0 / DI, EPS, ALU.mult, ALU.add),
            deps=[ms])
        r2 = k.op("scalar", lambda e: e.activation(c.tiny[:, 33:34], c.tiny[:, 32:33], AF.Ln),
                  deps=[r1])
        r2b = k.op("vector", lambda e: e.tensor_scalar(c.tiny[:, 33:34], c.tiny[:, 33:34],
                                                       -0.5, None, ALU.mult), deps=[r2])
        r3 = k.op("scalar", lambda e, cc=cc: e.activation(c.rstd_all[:, cc:cc + 1],
                                                          c.tiny[:, 33:34], AF.Exp), deps=[r2b])
        yb = k.op("vector", lambda e, cc=cc, b=b: e.tensor_scalar(
            YBF[b][:], YTM[:], c.rstd_all[:, cc:cc + 1], None, ALU.mult),
            deps=[r3] + ([c.s_sp.tok()] if cc >= 2 else []))
        st["yb"] = yb
        k.dma("scalar", c.y_tm_d[cc * Q:(cc + 1) * Q, :], YBF[b][:], c.s_sp, deps=[yb])
        if cc + 1 < NCH and not DBG_NODEFER:
            xdo(cc + 1)
            mtxt(cc + 1)

    s1(0)
    for cc in range(1, NCH):
        s1(cc)
        s23(cc - 1)
    s23(NCH - 1)


def phase_p2b(c):
    k = c.k
    k.fence([c.s_sp, c.s_tr, c.s_w, c.s_g2, c.s_in])
    WO = [c.bf("AR2", j * 1024, 1024) for j in range(16)]
    YF2 = [[c.bf("AR1", p_ * 16384 + j * 1024, 1024) for j in range(16)] for p_ in range(2)]
    XT = [c.f32("AR3", b * 2048, DM) for b in range(2)]
    GTL = [c.f32("AR3", 4096 + b * 2048, DM) for b in range(2)]
    XN = c.f32("AR3", 8192, DM)
    SS = c.f32("AR3", 10240, DM)
    OT = [c.f32("AR3", 12288 + b * 2048, DM) for b in range(2)]
    for j in range(16):
        k.dma("gpsimd", WO[j][:], c.w_out[j * 128:(j + 1) * 128, :], c.s_w2)
    wld = c.s_w2.tok()
    for j in range(16):
        k.op("scalar", lambda e, j=j: e.activation(WO[j][:], WO[j][:], AF.Identity,
                                                   scale=c.rmsw_t[:, j:j + 1]),
             deps=[wld, c.consts])
    wo_tok = k.tok("scalar")
    otok = [None, None]
    vfree = [None, None]
    vprev = None
    ptoks = {}
    yput = {}
    for sg in range(4):
        p_ = sg % 2
        for j in range(16):
            dep = [ptoks[sg - 2]] if sg >= 2 else []
            yput[(sg, j)] = k.dma("sync", YF2[p_][j][:],
                                  c.y_tm_d[sg * 1024:(sg + 1) * 1024, j * 128:(j + 1) * 128],
                                  c.s_tr, transpose=True, deps=dep)
        for tc in range(8):
            g = sg * 8 + tc
            b = tc % 2
            pd = [c.ps[0 + 2 * b], c.ps[1 + 2 * b]]
            xl = k.dma("sync", XT[b][:], c.x_main[g * Q:(g + 1) * Q, :], c.s_xl[b],
                       deps=[otok[b]] if otok[b] else [])
            gl = k.dma("gpsimd", GTL[b][:], c.g_tm_d[g * Q:(g + 1) * Q, :], c.s_gl[b],
                       deps=[otok[b]] if otok[b] else [])
            mm = [None, None]
            for half in range(2):
                first = True
                for j in range(16):
                    mm[half] = k.op("tensor", lambda e, j=j, half=half, tc=tc, pd=pd,
                                    first=first, p_=p_:
                                    e.matmul(pd[half][:, 0:512],
                                             YF2[p_][j][:, tc * 128:(tc + 1) * 128],
                                             WO[j][:, half * 512:half * 512 + 512],
                                             start=first, stop=(j == 15)),
                                    deps=[yput[(sg, j)], wo_tok] +
                                    ([vfree[b]] if (first and vfree[b]) else []),
                                    inc=(j == 15))
                    first = False
            ptoks[sg] = k.tok("tensor")
            xn = k.op("scalar", lambda e, g=g, b=b: e.activation(
                XN[:], XT[b][:], AF.Identity, scale=c.rho[:, g:g + 1], bias=c.nmr[:, g:g + 1]),
                deps=[xl] + ([vprev] if vprev else []))
            v1 = k.op("vector", lambda e, pd=pd: e.tensor_tensor(SS[:, 0:512], pd[0][:], XN[:, 0:512],
                                                                 ALU.add), deps=[mm[0], xn])
            v2 = k.op("vector", lambda e, pd=pd: e.tensor_tensor(SS[:, 512:1024], pd[1][:],
                                                                 XN[:, 512:1024], ALU.add),
                      deps=[mm[1]])
            v3 = k.op("vector", lambda e, b=b: e.tensor_tensor(SS[:], SS[:], GTL[b][:], ALU.mult),
                      deps=[v2, gl])
            v4 = k.op("vector", lambda e, b=b: e.tensor_tensor(OT[b][:], SS[:], XT[b][:], ALU.add),
                      deps=[v3])
            vprev = v4
            vfree[b] = v2
            otok[b] = k.dma("sync", c.out_t[g * Q:(g + 1) * Q, :], OT[b][:], c.s_out, deps=[v4])


def finish(c):
    k = c.k
    if c.stage == "full" or c.mode == "twoA":
        k.op("sync", lambda e: e.nop(), deps=[c.s_out.tok()])
    k.fence([c.s_sp, c.s_out])


def _layout_inputs(inputs, n_cores=8):
    f = np.float32
    asf = lambda a: np.ascontiguousarray(np.asarray(a, dtype=f))
    x = asf(inputs["L0_attn_x"])
    w_in = asf(inputs["L0_attn_W_in"])
    conv_w = asf(inputs["L0_attn_conv_w"])
    conv_diag = np.zeros((4, 18, 128, 128), f)
    for t in range(4):
        for j in range(18):
            np.fill_diagonal(conv_diag[t, j], conv_w[j * 128:(j + 1) * 128, t])
    tri = np.triu(np.ones((128, 128), f))
    cap = np.where(tri > 0, 0.0, -1e9).astype(f)
    tricap = np.tile(cap, (1, 4))
    negd = np.zeros((16, 2048), f)
    for h in range(16):
        negd[h, h * 128:(h + 1) * 128] = -1.0
    com = dict(
        w_in=w_in,
        w_out=asf(inputs["L0_attn_W_out"]),
        lnw_col=asf(inputs["L0_attn_ln_w"]).reshape(8, 128).T.copy(),
        lnb_col=asf(inputs["L0_attn_ln_b"]).reshape(8, 128).T.copy(),
        conv_diag=conv_diag,
        convb_col=asf(inputs["L0_attn_conv_b"]).reshape(18, 128).T.copy(),
        dtb_col=asf(inputs["L0_attn_dt_bias"]).reshape(16, 1),
        alog_col=asf(inputs["L0_attn_A_log"]).reshape(16, 1),
        dhead_row=asf(inputs["L0_attn_D"]).reshape(1, 16),
        rmsw_col=asf(inputs["L0_attn_rms_w"]).reshape(16, 128).T.copy(),
        router_wT=asf(inputs["L0_ffn_router_w"]).T.copy(),
        corrb_row=asf(inputs["L0_ffn_correction_bias"]).reshape(1, 16),
        gate_w=asf(inputs["L0_ffn_gate_w"]),
        up_w=asf(inputs["L0_ffn_up_w"]),
        down_w=asf(inputs["L0_ffn_down_w"]),
        sgT=asf(inputs["L0_ffn_shared_gate_w"]).T.copy(),
        suT=asf(inputs["L0_ffn_shared_up_w"]).T.copy(),
        sdT=asf(inputs["L0_ffn_shared_down_w"]).T.copy(),
        i128=np.eye(128, dtype=f),
        fconst=np.tile(np.array([[1e-5, 1.0 + 1e-5, 1e-10, -0.5, -1.0, 1.0, 0.5, 2.0]], f),
                       (128, 1)),
        ones_row=np.ones((1, 512), f),
        triu=tri, tricap=tricap, negd=negd,
    )
    maps = []
    for core in range(n_cores):
        b, h = core // 2, core % 2
        m = dict(com)
        m["x_main"] = x[b, h * T:(h + 1) * T]
        halo = np.zeros((16, DM), f)
        if h == 1:
            halo[0:3] = x[b, T - 3:T]
        m["x_halo"] = halo
        m["mask_col"] = np.full((128, 1), 1.0 if h == 1 else 0.0, f)
        sm = np.zeros((128, 8), f)
        if h == 1:
            sm[:, core - 1] = 1.0
        m["smask"] = sm
        maps.append(m)
    return maps


def kernel(**inputs):
    maps = _layout_inputs(inputs)
    import os
    fused = os.environ.get("KMODE", "fused") == "fused"
    if fused:
        nc, es, c = build_nc(mode="fused", stage="full")
        res = run_bass_kernel_spmd(nc, maps, core_ids=list(range(8)))
        results = res.results
    else:
        ncA, esA, cA = build_nc(mode="twoA", stage="p1b")
        resA = run_bass_kernel_spmd(ncA, maps, core_ids=list(range(8)))
        mapsB = []
        for core in range(8):
            m = dict(maps[core])
            if core % 2 == 1:
                m["s_init_in"] = np.ascontiguousarray(resA.results[core - 1]["s_half_out"])
            else:
                m["s_init_in"] = np.zeros((DI, NST), np.float32)
            mapsB.append(m)
        ncB, esB, cB = build_nc(mode="twoB", stage="full")
        resB = run_bass_kernel_spmd(ncB, mapsB, core_ids=list(range(8)))
        results = resB.results
    out = np.zeros((4, 8192, 1024), np.float32)
    for core in range(8):
        b, h = core // 2, core % 2
        out[b, h * T:(h + 1) * T] = results[core]["out_t"]
    return out


# version 14
# speedup vs baseline: 1.0554x; 1.0370x over previous
import contextlib
import numpy as np
import ml_dtypes
import concourse.bass as bass
import concourse.mybir as mybir
from concourse.bass_utils import run_bass_kernel_spmd

F32 = mybir.dt.float32
BF16 = mybir.dt.bfloat16
AF = mybir.ActivationFunctionType
ALU = mybir.AluOpType
AX = mybir.AxisListType

T = 4096
Q = 128
NCH = T // Q
NH, HD, NST = 16, 128, 128
DM, DI = 1024, 2048
EPS = 1e-5
NEG = -1e30

ENGINES = ("tensor", "vector", "scalar", "gpsimd", "sync")
FCONST_VALS = [EPS, 1.0 + EPS, EPS * EPS, -0.5, -1.0, 1.0, 0.5, 2.0]


class Stream:
    def __init__(self, k, name):
        self.sem = k.es.enter_context(k.nc.semaphore(name))
        self.count = 0

    def tok(self):
        return (self.sem, self.count)


class Eng:
    def __init__(self, k, name):
        self.name = name
        self.ops = []
        self.sem = k.es.enter_context(k.nc.semaphore("prog_" + name))
        self.count = 0
        self.waited = {}


class K:

    def __init__(self, nc, es):
        self.nc = nc
        self.es = es
        self.eng = {n: Eng(self, n) for n in ENGINES}

    def stream(self, name):
        return Stream(self, name)

    def _filt(self, e, deps, keep_self=False):
        out = []
        for (sem, val) in deps:
            if (sem is e.sem and not keep_self) or val <= 0:
                continue
            if e.waited.get(id(sem), -1) >= val:
                continue
            e.waited[id(sem)] = val
            out.append((sem, val))
        return out

    def op(self, engine, fn, deps=(), inc=True):
        e = self.eng[engine]
        mydeps = self._filt(e, deps)
        if inc:
            e.count += 1
        e.ops.append((fn, mydeps, e.sem if inc else None))
        return (e.sem, e.count)

    def tok(self, engine):
        e = self.eng[engine]
        return (e.sem, e.count)

    def dma(self, queue, out, in_, stream, deps=(), **kw):
        stream.count += 16
        e = self.eng[queue]
        mydeps = self._filt(e, deps, keep_self=True)

        def fn(eng):
            return eng.dma_start(out, in_, **kw)

        e.ops.append((fn, mydeps, ("dma", stream.sem)))
        return (stream.sem, stream.count)

    def raw16(self, queue, fn, stream, deps=()):
        stream.count += 16
        e = self.eng[queue]
        mydeps = self._filt(e, deps, keep_self=True)
        e.ops.append((fn, mydeps, ("dma", stream.sem)))
        return (stream.sem, stream.count)

    def fence(self, streams=()):
        toks = [(e.sem, e.count) for e in self.eng.values()]
        toks += [s.tok() for s in streams]
        for name in ENGINES:
            self.op(name, lambda e: e.nop(), deps=toks, inc=True)

    def emit(self):
        block = self.es.enter_context(self.nc.Block())
        for name in ENGINES:
            e = self.eng[name]
            selfsync = name in ("vector", "scalar", "gpsimd")

            def make(e, selfsync):
                def body(eng):
                    n = 0
                    for fn, deps, inc in e.ops:
                        if selfsync and n > 0 and inc is not None and not isinstance(inc, tuple):
                            eng.wait_ge(e.sem, n)
                        for (sem, val) in deps:
                            eng.wait_ge(sem, val)
                        ins = fn(eng)
                        if inc is None:
                            continue
                        if isinstance(ins, (list, tuple)):
                            ins = ins[-1]
                        if isinstance(inc, tuple):
                            ins.then_inc(inc[1], 16)
                        else:
                            ins.then_inc(inc, 1)
                            n += 1
                return body

            getattr(block, name)(make(e, selfsync))


class Ctx:
    pass


def build_nc(mode="fused", stage="full", dbg=False):
    nc = bass.Bass("TRN2", target_bir_lowering=False, detect_race_conditions=False)
    es = contextlib.ExitStack()
    k = K(nc, es)
    c = Ctx()
    c.nc, c.es, c.k, c.mode, c.stage = nc, es, k, mode, stage

    def din(name, shape, dt=F32):
        setattr(c, name, nc.dram_tensor(name, shape, dt, kind="ExternalInput").ap())

    def dout(name, shape, dt=F32):
        setattr(c, name, nc.dram_tensor(name, shape, dt, kind="ExternalOutput").ap())

    def dint(name, shape, dt):
        kind = "ExternalOutput" if dbg else "Internal"
        setattr(c, name, nc.dram_tensor(name, shape, dt, kind=kind).ap())

    din("x_main", [T, DM]); din("x_halo", [16, DM])
    din("mask_col", [128, 1]); din("smask", [128, 8])
    din("w_in", [DM, 4368]); din("w_out", [DI, DM])
    din("lnw_col", [128, 8]); din("lnb_col", [128, 8])
    din("conv_diag", [4, 18, 128, 128]); din("convb_col", [128, 18])
    din("dtb_col", [16, 1]); din("alog_col", [16, 1]); din("dhead_row", [1, 16])
    din("rmsw_col", [128, 16])
    din("router_wT", [DM, 16]); din("corrb_row", [1, 16])
    din("gate_w", [16, DM, 256]); din("up_w", [16, DM, 256]); din("down_w", [16, 256, DM])
    din("sgT", [DM, DI]); din("suT", [DM, DI]); din("sdT", [DI, DM])
    din("i128", [128, 128]); din("ones_row", [1, 512]); din("fconst", [128, 8])
    din("triu", [128, 128]); din("tricap", [128, 512]); din("negd", [16, 2048])
    dout("out_t", [T, DM])
    if dbg:
        dout("dbg0", [128, 160])
        dout("dbg1", [128, 512])
    if mode == "twoA":
        dout("s_half_out", [DI, NST])
    if mode == "twoB":
        din("s_init_in", [DI, NST])
    dint("xn_tm_d", [T, DM], BF16)
    dint("xn2_tm_d", [T, DM], BF16)
    dint("xn2h_tm_d", [16, DM], BF16)
    dint("xbc_fm_d", [2304, T], BF16)
    dint("z_fm_d", [DI, T], BF16)
    dint("h_fm_d", [16 * 256 + DI, T], BF16)
    dint("y_tm_d", [T, DI], BF16)
    dint("g_tm_d", [T, DM], F32)
    dint("sflat_d", [NCH, 2048], F32)
    dint("sfm_d", [NCH, 17, 128], F32)
    dint("sq_d", [NCH, 16], F32)
    dint("comb_fm_d", [16, T], F32)
    if mode == "fused":
        dint("coll_in_d", [DI, NST], F32)
        dint("coll_out_d", [8 * DI, NST], F32)

    SB = {}

    def st(name, shape, dt):
        SB[name] = es.enter_context(nc.sbuf_tensor(name, shape, dt))
        setattr(c, name, SB[name])
        return SB[name]

    st("S_t", [128, DI], F32)
    st("stm_all", [128, NCH * 16], F32)
    st("dttm_all", [128, NCH * 16], F32)
    st("dectm_all", [128, NCH * 16], F32)
    st("comb", [128, NCH * 16], F32)
    st("msq_all", [128, NCH], F32)
    st("rho", [128, NCH], F32); st("nmr", [128, NCH], F32)
    st("rho2", [128, NCH], F32); st("nmr2", [128, NCH], F32)
    st("rstd_all", [128, NCH], F32)
    st("hstat", [16, 8], F32)
    st("i128_t", [128, 128], F32); st("i128b_t", [128, 128], BF16)
    st("ones1", [1, 128], F32)
    st("triu_t", [128, 128], F32); st("tricap_t", [128, 512], F32)
    st("dhi", [128, NH * 128], BF16)
    st("sfm", [17, 128], F32)
    st("lnw_t", [128, 8], F32); st("lnb_t", [128, 8], F32)
    st("maskc", [128, 1], F32); st("smask_t", [128, 8], F32)
    st("dtb_t", [16, 1], F32); st("alog_t", [16, 1], F32); st("acol", [16, 1], F32)
    st("dhead_t", [1, 16], F32); st("corrb_t", [1, 16], F32)
    st("rmsw_t", [128, 16], F32)
    st("convb_t", [128, 18], F32)
    st("fconst_t", [128, 8], F32)
    st("tiny", [128, 64], F32)
    st("tiny16", [16, 128], F32)
    st("AR1", [128, 32768], BF16)
    st("AR2", [128, 32768], BF16)
    st("AR3", [128, 26944], BF16)

    c.ps = [es.enter_context(nc.psum_tensor(f"ps{i}", [128, 512], F32)) for i in range(8)]

    c.s_in = k.stream("s_in")
    c.s_w = k.stream("s_w")
    c.s_w2 = k.stream("s_w2")
    c.s_w3 = k.stream("s_w3")
    c.s_sp = k.stream("s_sp")
    c.s_tr = k.stream("s_tr")
    c.s_g2 = k.stream("s_g2")
    c.s_out = k.stream("s_out")
    c.s_bnc = k.stream("s_bnc")
    c.s_coll = k.stream("s_coll")
    c.s_const = k.stream("s_const")
    c.s_x = [k.stream("s_x0"), k.stream("s_x1")]
    c.s_xh = [k.stream("s_xh0"), k.stream("s_xh1")]
    c.s_pa = [k.stream("s_pa0"), k.stream("s_pa1")]
    c.s_pb = [k.stream("s_pb0"), k.stream("s_pb1")]
    c.s_cr = [k.stream("s_cr0"), k.stream("s_cr1")]
    c.s_we = [k.stream("s_we0"), k.stream("s_we1")]
    c.s_hl = [k.stream("s_hl0"), k.stream("s_hl1")]
    c.s_xl = [k.stream("s_xl0"), k.stream("s_xl1")]
    c.s_gl = [k.stream("s_gl0"), k.stream("s_gl1")]

    def bf(arena, off, n, parts=128):
        return SB[arena][0:parts, off:off + n]

    def f32(arena, off, n, parts=128):
        return SB[arena][0:parts, off:off + 2 * n].bitcast(F32)

    c.bf, c.f32 = bf, f32

    k.dma("sync", SB["ones1"][:], c.ones_row[0:1, 0:128], c.s_const)
    for dst, src in [("i128_t", "i128"), ("triu_t", "triu"),
                     ("tricap_t", "tricap"), ("lnw_t", "lnw_col"), ("lnb_t", "lnb_col"),
                     ("maskc", "mask_col"), ("smask_t", "smask"), ("dtb_t", "dtb_col"),
                     ("alog_t", "alog_col"), ("dhead_t", "dhead_row"), ("corrb_t", "corrb_row"),
                     ("rmsw_t", "rmsw_col"), ("convb_t", "convb_col")]:
        k.dma("sync", SB[dst][:], getattr(c, src), c.s_const)
    k.dma("sync", SB["sfm"][16:17, :], c.ones_row[0:1, 0:128], c.s_in)
    k.dma("sync", SB["fconst_t"][:], c.fconst, c.s_const)
    k.dma("gpsimd", SB["i128b_t"][:], c.i128, c.s_w)
    c.consts = c.s_const.tok()
    for i, v in enumerate(FCONST_VALS):
        nc.const_aps.register(F32, float(v), SB["fconst_t"][:, i:i + 1])
        if float(np.float32(v)) != float(v):
            nc.const_aps.register(F32, float(np.float32(v)), SB["fconst_t"][:, i:i + 1])
    k.fence([c.s_const])
    c.constsw = c.s_w.tok()
    k.op("scalar", lambda e: e.activation(c.acol[:], c.alog_t[:], AF.Exp), deps=[c.consts])
    k.op("vector", lambda e: e.tensor_scalar(c.acol[:], c.acol[:], -1.0, None, ALU.mult),
         deps=[k.tok("scalar")])
    c.acol_tok = k.tok("vector")
    t0 = k.op("tensor", lambda e: e.matmul(c.ps[0][:, 0:16], c.ones1[:],
                                           c.dhead_t[:], start=True, stop=True), deps=[c.consts])
    t1 = k.op("scalar", lambda e: e.activation(c.tiny[:, 0:16], c.ps[0][:, 0:16], AF.Identity),
              deps=[t0])
    for h in range(NH):
        k.op("vector", lambda e, h=h: e.tensor_scalar(
            c.dhi[:, h * 128:(h + 1) * 128], c.i128b_t[:],
            c.tiny[:, h:h + 1], None, ALU.mult), deps=[t1, c.constsw])
    c.dhi_tok = k.tok("vector")

    phase_p0(c)
    if stage != "p0":
        phase_p1a(c)
    if stage not in ("p0", "p1a"):
        phase_p1b(c)
    if mode != "twoA" and stage not in ("p0", "p1a", "p1b"):
        phase_p3z(c)
        if stage != "p3z":
            phase_p3(c)
        if stage not in ("p3z", "p3sh", "p3e", "p3dn"):
            phase_p2a(c)
        if stage == "full":
            phase_p2b(c)
    finish(c)
    k.emit()
    return nc, es, c


def phase_p0(c):
    k = c.k
    XT = [c.f32("AR1", b * 2048, DM) for b in range(4)]
    SQ = c.f32("AR2", 0, DM)
    NT = [c.bf("AR3", b * 1024, DM) for b in range(2)]
    NT2 = [c.bf("AR3", 2048 + b * 1024, DM) for b in range(2)]
    SUMS = c.f32("AR2", 2048, NCH)
    SSQ = c.f32("AR2", 2176, NCH)
    MU = c.f32("AR2", 2304, NCH)
    ld_toks = [None] * NCH
    red_toks = [None] * NCH
    sq_toks = [None] * NCH
    zz = k.op("vector", lambda e: e.memset(SSQ[:], 0.0))
    for i in range(NCH):
        b = i % 4
        dep = [red_toks[i - 4], sq_toks[i - 4]] if i >= 4 else []
        ld = k.dma("sync", XT[b][:], c.x_main[i * Q:(i + 1) * Q, :],
                   (c.s_x + c.s_pa)[b], deps=dep)
        ld_toks[i] = ld
        red_toks[i] = k.op("vector", lambda e, i=i, b=b: e.tensor_reduce(
            SUMS[:, i:i + 1], XT[b][:], AX.X, ALU.add), deps=[ld])
        sq_toks[i] = k.op("scalar", lambda e, i=i, b=b: e.activation(
            SQ[:], XT[b][:], AF.Square, accum_out=SSQ[:, i:i + 1]), deps=[ld, zz])
    tA = k.op("vector", lambda e: e.tensor_scalar(MU[:], SUMS[:], 1.0 / DM, None, ALU.mult),
              deps=[red_toks[-1]])
    tB = k.op("vector", lambda e: e.tensor_scalar(SSQ[:], SSQ[:], 1.0 / DM, None, ALU.mult),
              deps=[sq_toks[-1]])
    tC = k.op("vector", lambda e: e.tensor_tensor(c.rho[:], MU[:], MU[:], ALU.mult))
    tD = k.op("vector", lambda e: e.tensor_tensor(SSQ[:], SSQ[:], c.rho[:], ALU.subtract))
    tE = k.op("vector", lambda e: e.tensor_scalar(c.rho[:], SSQ[:], EPS, None, ALU.add))
    tF = k.op("vector", lambda e: e.tensor_scalar(c.rho2[:], SSQ[:], 1.0 + EPS, EPS * EPS,
                                                  ALU.mult, ALU.add))
    l1 = k.op("scalar", lambda e: e.activation(c.rho[:], c.rho[:], AF.Ln), deps=[tE])
    l2 = k.op("scalar", lambda e: e.activation(c.rho2[:], c.rho2[:], AF.Ln), deps=[tF])
    h1 = k.op("vector", lambda e: e.tensor_scalar(c.rho[:], c.rho[:], -0.5, None, ALU.mult),
              deps=[l1])
    h2 = k.op("vector", lambda e: e.tensor_scalar(c.rho2[:], c.rho2[:], -0.5, None, ALU.mult),
              deps=[l2])
    e1 = k.op("scalar", lambda e: e.activation(c.rho[:], c.rho[:], AF.Exp), deps=[h1])
    e2 = k.op("scalar", lambda e: e.activation(c.rho2[:], c.rho2[:], AF.Exp), deps=[h2])
    n1 = k.op("vector", lambda e: e.tensor_tensor(c.nmr[:], MU[:], c.rho[:], ALU.mult),
              deps=[e1])
    n1b = k.op("vector", lambda e: e.tensor_scalar(c.nmr[:], c.nmr[:], -1.0, None, ALU.mult))
    n2 = k.op("vector", lambda e: e.tensor_tensor(c.nmr2[:], MU[:], c.rho2[:], ALU.mult),
              deps=[e2])
    n2b = k.op("vector", lambda e: e.tensor_scalar(c.nmr2[:], c.nmr2[:], -1.0, None, ALU.mult))
    sp_toks = {}
    for i in range(NCH):
        b = i % 4
        nb = i % 2
        dep = [sp_toks.get(i - 4, None)] if i >= 4 else []
        dep = [t for t in dep if t]
        ld = k.dma("sync", XT[b][:], c.x_main[i * Q:(i + 1) * Q, :],
                   (c.s_x + c.s_pa)[b],
                   deps=dep + ([sq_toks[-1], red_toks[-1]] if i < 4 else []))
        wdep = [c.s_sp.tok()] if i >= 2 else []
        x1 = k.op("scalar", lambda e, i=i, b=b, nb=nb: e.activation(
            NT[nb][:], XT[b][:], AF.Identity, scale=c.rho[:, i:i + 1], bias=c.nmr[:, i:i + 1]),
            deps=[ld, n1b] + wdep)
        k.dma("scalar", c.xn_tm_d[i * Q:(i + 1) * Q, :], NT[nb][:], c.s_sp, deps=[x1])
        x2 = k.op("scalar", lambda e, i=i, b=b, nb=nb: e.activation(
            NT2[nb][:], XT[b][:], AF.Identity, scale=c.rho2[:, i:i + 1], bias=c.nmr2[:, i:i + 1]),
            deps=[n2b])
        sp_toks[i] = k.dma("scalar", c.xn2_tm_d[i * Q:(i + 1) * Q, :], NT2[nb][:], c.s_sp,
                           deps=[x2])
    HX = c.f32("AR2", 4096, DM, 16)
    HS = c.f32("AR2", 8192, DM, 16)
    HB = c.bf("AR2", 12288, DM, 16)
    hld = k.dma("sync", HX[0:16, :], c.x_halo, c.s_in)
    h_z = k.op("vector", lambda e: e.memset(c.hstat[:, 1:2], 0.0))
    h_r = k.op("vector", lambda e: e.tensor_reduce(c.hstat[:, 0:1], HX[0:16, :], AX.X, ALU.add),
               deps=[hld])
    h_q = k.op("scalar", lambda e: e.activation(HS[0:16, :], HX[0:16, :], AF.Square,
                                                accum_out=c.hstat[:, 1:2]), deps=[hld, h_z])
    k.op("vector", lambda e: e.tensor_scalar(c.hstat[:, 2:3], c.hstat[:, 0:1], 1.0 / DM, None,
                                             ALU.mult), deps=[h_r])
    k.op("vector", lambda e: e.tensor_tensor(c.hstat[:, 3:4], c.hstat[:, 2:3], c.hstat[:, 2:3],
                                             ALU.mult))
    k.op("vector", lambda e: e.tensor_scalar(c.hstat[:, 4:5], c.hstat[:, 1:2], 1.0 / DM, None,
                                             ALU.mult), deps=[h_q])
    hv = k.op("vector", lambda e: e.tensor_tensor(c.hstat[:, 4:5], c.hstat[:, 4:5],
                                                  c.hstat[:, 3:4], ALU.subtract))
    he = k.op("vector", lambda e: e.tensor_scalar(c.hstat[:, 5:6], c.hstat[:, 4:5],
                                                  1.0 + EPS, EPS * EPS, ALU.mult, ALU.add),
              deps=[hv])
    hl = k.op("scalar", lambda e: e.activation(c.hstat[:, 5:6], c.hstat[:, 5:6], AF.Ln),
              deps=[he])
    hh = k.op("vector", lambda e: e.tensor_scalar(c.hstat[:, 5:6], c.hstat[:, 5:6],
                                                  -0.5, None, ALU.mult), deps=[hl])
    hr = k.op("scalar", lambda e: e.activation(c.hstat[:, 5:6], c.hstat[:, 5:6], AF.Exp),
              deps=[hh])
    hm = k.op("vector", lambda e: e.tensor_tensor(c.hstat[:, 6:7], c.hstat[:, 5:6],
                                                  c.hstat[:, 2:3], ALU.mult), deps=[hr])
    hm2 = k.op("vector", lambda e: e.tensor_scalar(c.hstat[:, 6:7], c.hstat[:, 6:7], -1.0, None,
                                                   ALU.mult))
    hx2 = k.op("scalar", lambda e: e.activation(HB[0:16, :], HX[0:16, :], AF.Identity,
                                                scale=c.hstat[:, 5:6], bias=c.hstat[:, 6:7]),
               deps=[hm2])
    k.dma("sync", c.xn2h_tm_d, HB[0:16, :], c.s_bnc, deps=[hx2])
    if c.stage == "p0" and hasattr(c, "dbg0"):
        k.dma("sync", c.dbg0[:, 0:32], c.rho[:], c.s_bnc, deps=[k.tok("vector"), k.tok("scalar")])
        k.dma("sync", c.dbg0[:, 32:64], c.nmr[:], c.s_bnc)
        k.dma("sync", c.dbg0[:, 64:96], c.rho2[:], c.s_bnc)
        k.dma("sync", c.dbg0[:, 96:128], c.nmr2[:], c.s_bnc)
        k.dma("sync", c.dbg0[:, 128:160], c.nmr[:], c.s_bnc)


def phase_p1a(c):
    k = c.k
    k.fence([c.s_sp, c.s_in, c.s_w])
    XF = [c.bf("AR1", j * 4096, 4096) for j in range(8)]
    WIN = [c.bf("AR2", j * 2320, 2320) for j in range(8)]
    DIAG = [c.bf("AR2", 18560 + t * 2304, 2304) for t in range(4)]
    XH = [c.bf("AR2", 27776 + j * 16, 16) for j in range(8)]
    XBCR = c.bf("AR3", 0, 18 * 516)
    CSTG = c.bf("AR3", 9288, 18 * 512)
    DTR = c.f32("AR3", 18504, T, 16)
    xbv = c.AR3[:, 0:9288].rearrange("p (j w) -> p j w", w=516)

    xf_tok = []
    for j in range(8):
        xf_tok.append(k.dma("sync", XF[j][:], c.xn2_tm_d[0:T, j * 128:(j + 1) * 128],
                            c.s_tr, transpose=True))
        k.dma("gpsimd", WIN[j][:], c.w_in[j * 128:(j + 1) * 128, 2048:4368], c.s_w)
        k.dma("sync", XH[j][:], c.xn2h_tm_d[0:16, j * 128:(j + 1) * 128], c.s_tr, transpose=True)
    for t in range(4):
        k.dma("gpsimd", DIAG[t][:], c.conv_diag[t].rearrange("j p w -> p j w"), c.s_w)
    wtok = c.s_w.tok()
    trtok = c.s_tr.tok()
    aff = []
    for j in range(8):
        aff.append(k.op("scalar", lambda e, j=j: e.activation(
            XF[j][:], XF[j][:], AF.Identity, scale=c.lnw_t[:, j:j + 1], bias=c.lnb_t[:, j:j + 1]),
            deps=[trtok]))
    aff_tok = k.tok("scalar")
    for j in range(8):
        k.op("scalar", lambda e, j=j: e.activation(
            XH[j][:], XH[j][:], AF.Identity, scale=c.lnw_t[:, j:j + 1], bias=c.lnb_t[:, j:j + 1]))
    hf_tok = k.tok("scalar")
    hev = None
    for j in range(18):
        mm = None
        for kc in range(8):
            mm = k.op("tensor", lambda e, j=j, kc=kc: e.matmul(
                c.ps[4][:, 0:16], WIN[kc][:, j * 128:(j + 1) * 128], XH[kc][:],
                start=(kc == 0), stop=(kc == 7)),
                deps=[hf_tok, wtok] + ([hev] if hev else []), inc=(kc == 7))
        hev = k.op("scalar", lambda e, j=j: e.activation(
            xbv[:, j, 0:3],
            c.ps[4][:, 0:3], AF.Identity, scale=c.maskc[:, 0:1]), deps=[mm, c.consts])
    halo_tok = k.tok("scalar")

    NG = 8
    GW = 512
    jw = [(j * 128, 128) for j in range(18)] + [(2304, 16)]
    conv_done = [None] * NG
    evac_done = [None] * NG
    halo_cp = [None] * NG
    spl_prev = None
    for g in range(NG):
        ev = []
        for jj, (joff, jwid) in enumerate(jw):
            bank = c.ps[jj % 2]
            bdep = [evac_done[g - 1]] if (g >= 1 and jj < 2) else []
            mm = None
            for kc in range(8):
                mm = k.op("tensor", lambda e, jj=jj, joff=joff, jwid=jwid, kc=kc, g=g, bank=bank:
                          e.matmul(bank[0:jwid if jwid < 128 else 128, 0:GW],
                                   WIN[kc][:, joff:joff + jwid],
                                   XF[kc][:, g * GW:(g + 1) * GW],
                                   start=(kc == 0), stop=(kc == 7)),
                          deps=([aff_tok, wtok, halo_tok] + bdep +
                                ([ev[jj - 2]] if jj >= 2 else [])),
                          inc=(kc == 7))
            if jj < 18:
                edep = [mm] + ([halo_cp[g - 1]] if g >= 1 else [])
                e1 = k.op("scalar", lambda e, jj=jj, g=g, bank=bank: e.activation(
                    xbv[:, jj, 3:3 + GW], bank[:, 0:GW], AF.Identity), deps=edep)
            else:
                e1 = k.op("scalar", lambda e, g=g, bank=bank: e.activation(
                    DTR[0:16, g * GW:(g + 1) * GW], bank[0:16, 0:GW], AF.Identity), deps=[mm])
            ev.append(e1)
        evac_done[g] = ev[-1]
        wv_tok = k.tok("scalar")
        cv = []
        for j in range(18):
            bank = c.ps[2 + (j % 2)]
            bdep = [cv[j - 2]] if j >= 2 else ([conv_done[g - 1]] if g >= 1 else [])
            mm = None
            for t in range(4):
                mm = k.op("tensor", lambda e, j=j, t=t, g=g, bank=bank: e.matmul(
                    bank[:, 0:GW], DIAG[t][:, j * 128:(j + 1) * 128],
                    xbv[:, j, t:t + GW], start=(t == 0), stop=(t == 3)),
                    deps=[ev[j], wtok] + bdep, inc=(t == 3))
            cv.append(k.op("scalar", lambda e, j=j, g=g, bank=bank: e.activation(
                CSTG[:, j * 512:j * 512 + GW], bank[:, 0:GW], AF.Silu,
                bias=c.convb_t[:, j:j + 1]),
                deps=[mm] + ([spl_prev] if (j == 0 and spl_prev) else [])))
        conv_done[g] = cv[-1]
        halo_cp[g] = k.op("vector", lambda e: e.tensor_copy(
            xbv[:, :, 0:3], xbv[:, :, 512:515]), deps=[k.tok("tensor")])
        spl_prev = k.dma("scalar",
                         c.xbc_fm_d.rearrange("(j p) t -> p j t", p=128)[:, :, g * GW:(g + 1) * GW],
                         c.AR3[:, 9288:9288 + 9216].rearrange("p (j w) -> p j w", w=512),
                         c.s_sp, deps=[conv_done[g]])


def phase_p1b(c):
    k = c.k
    k.fence([c.s_sp, c.s_w, c.s_tr])
    DTR = c.f32("AR3", 18504, T, 16)
    DTF = c.f32("AR1", 0, T, 16)
    SPT = c.f32("AR1", 8192, T, 16)
    AFM = c.f32("AR1", 8192, T, 16)
    XHT = [c.bf("AR1", 16384 + b * 2048, 2048) for b in range(2)]
    XDC = [c.bf("AR1", 20480 + b * 2048, 2048) for b in range(2)]
    BTM = [c.bf("AR1", 24576 + b * 128, 128) for b in range(2)]
    ATM = c.f32("AR1", 24832, 512)
    SFM = [c.f32("AR2", 0, 128, 17), c.f32("AR2", 256, 128, 17)]
    EQROW = c.f32("AR2", 512, 16, 1)
    EQR = c.f32("AR2", 576, 16)
    DECF = [c.f32("AR2", 640 + b * 256, 128, 16) for b in range(2)]
    STMP = c.f32("AR1", 26112, 2048)

    t1 = k.op("vector", lambda e: e.tensor_scalar(DTF[0:16, :], DTR[0:16, :],
                                                  c.dtb_t[0:16, 0:1], None, ALU.add),
              deps=[c.consts])
    t2 = k.op("scalar", lambda e: e.activation(SPT[0:16, :], DTF[0:16, :], AF.Abs), deps=[t1])
    t2b = k.op("vector", lambda e: e.tensor_scalar(SPT[0:16, :], SPT[0:16, :], -1.0, None,
                                                   ALU.mult), deps=[t2])
    t3 = k.op("scalar", lambda e: e.activation(SPT[0:16, :], SPT[0:16, :], AF.Exp), deps=[t2b])
    t3b = k.op("vector", lambda e: e.tensor_scalar(SPT[0:16, :], SPT[0:16, :], 1.0, None,
                                                   ALU.add), deps=[t3])
    t4 = k.op("scalar", lambda e: e.activation(SPT[0:16, :], SPT[0:16, :], AF.Ln), deps=[t3b])
    t5 = k.op("scalar", lambda e: e.activation(DTF[0:16, :], DTF[0:16, :], AF.Relu), deps=[t2])
    t6 = k.op("vector", lambda e: e.tensor_tensor(DTF[0:16, :], DTF[0:16, :], SPT[0:16, :],
                                                  ALU.add), deps=[t4, t5])
    t7 = k.op("vector", lambda e: e.tensor_scalar(AFM[0:16, :], DTF[0:16, :],
                                                  c.acol[0:16, 0:1], None, ALU.mult),
              deps=[t6, c.acol_tok])
    ev_hist = {}
    for cc in range(NCH):
        bk = c.ps[cc % 2]
        pdep = [ev_hist.get(cc - 2)] if cc >= 2 else []
        pdep = [t for t in pdep if t]
        m1 = k.op("tensor", lambda e, cc=cc, bk=bk: e.matmul(
            bk[:, 0:16], DTF[0:16, cc * Q:(cc + 1) * Q], c.i128_t[0:16, 0:16],
            start=True, stop=True), deps=[t6, c.consts] + pdep)
        s1 = k.op("scalar", lambda e, cc=cc, bk=bk: e.activation(
            c.dttm_all[:, cc * 16:(cc + 1) * 16], bk[:, 0:16], AF.Identity), deps=[m1])
        m2 = k.op("tensor", lambda e, cc=cc, bk=bk: e.matmul(
            bk[:, 16:32], AFM[0:16, cc * Q:(cc + 1) * Q], c.i128_t[0:16, 0:16],
            start=True, stop=True), deps=[t7] + pdep)
        s2 = k.op("scalar", lambda e, cc=cc, bk=bk: e.activation(
            ATM[:, cc * 16:(cc + 1) * 16], bk[:, 16:32], AF.Identity), deps=[m2])
        m3 = k.op("tensor", lambda e, cc=cc, bk=bk: e.matmul(
            bk[:, 32:48], c.triu_t[:], ATM[:, cc * 16:(cc + 1) * 16],
            start=True, stop=True), deps=[s2])
        s3 = k.op("scalar", lambda e, cc=cc, bk=bk: e.activation(
            c.stm_all[:, cc * 16:(cc + 1) * 16], bk[:, 32:48], AF.Identity), deps=[m3])
        ev_hist[cc] = s3
    for b_ in range(2):
        k.dma("sync", SFM[b_][16:17, :], c.ones_row[0:1, 0:128], c.s_const)
    ones_tok = c.s_const.tok()
    sp_hist = {}
    dec_hist = {}
    for cc in range(NCH):
        b = cc % 2
        bk = c.ps[2 + (cc % 2)]
        sfm = SFM[b]
        pdep = [c.s_sp.tok()] if cc >= 2 else []
        m4 = k.op("tensor", lambda e, cc=cc, bk=bk: e.matmul(
            bk[:, 0:128][0:16, :], c.stm_all[:, cc * 16:(cc + 1) * 16], c.i128_t[:],
            start=True, stop=True), deps=[ev_hist[cc]] + ([dec_hist.get(cc - 2)] if cc >= 2 else []))
        s4 = k.op("scalar", lambda e, cc=cc, bk=bk, sfm=sfm: e.activation(
            sfm[0:16, :], bk[:, 0:128][0:16, :], AF.Identity),
            deps=[m4, c.consts] + pdep)
        sp1 = k.dma("scalar", c.sfm_d[cc], sfm[0:17, :], c.s_sp, deps=[s4, ones_tok])
        sp2 = k.dma("scalar", c.sflat_d[cc:cc + 1, :].rearrange("a (h t) -> (a h) t", h=16),
                    sfm[0:16, :], c.s_sp, deps=[s4])
        sp_hist[cc] = k.dma("scalar", c.sq_d[cc:cc + 1, :].rearrange("a h -> h a"),
                            sfm[0:16, 127:128], c.s_sp, deps=[s4])
        d0 = k.op("vector", lambda e, b=b, sfm=sfm: e.tensor_scalar(
            DECF[b][0:16, :], sfm[0:16, :], -1.0, None, ALU.mult),
            deps=[s4] + ([dec_hist.get(cc - 2)] if cc >= 2 else []))
        d0b = k.op("vector", lambda e, b=b, sfm=sfm: e.tensor_scalar(
            DECF[b][0:16, :], DECF[b][0:16, :], sfm[0:16, 127:128], None, ALU.add))
        d1 = k.op("scalar", lambda e, b=b: e.activation(DECF[b][0:16, :], DECF[b][0:16, :],
                                                        AF.Exp), deps=[d0b])
        m6 = k.op("tensor", lambda e, b=b, bk=bk: e.matmul(
            bk[:, 128:144], DECF[b][0:16, :], c.i128_t[0:16, 0:16], start=True, stop=True),
            deps=[d1])
        d2 = k.op("scalar", lambda e, cc=cc, bk=bk: e.activation(
            c.dectm_all[:, cc * 16:(cc + 1) * 16], bk[:, 128:144], AF.Identity), deps=[m6])
        dec_hist[cc] = k.op("vector", lambda e, cc=cc: e.tensor_tensor(
            c.dectm_all[:, cc * 16:(cc + 1) * 16], c.dectm_all[:, cc * 16:(cc + 1) * 16],
            c.dttm_all[:, cc * 16:(cc + 1) * 16], ALU.mult), deps=[d2])
    dec_all = k.tok("vector")
    sq_all = c.s_sp.tok()
    sz = k.op("vector", lambda e: e.memset(c.S_t[:], 0.0))
    upd_tok = sz
    add_hist = {}
    xd_hist = {}
    pe_hist = {}
    q2_hist = {}
    xput = [None, None]
    for cc in range(NCH):
        b = cc % 2
        ldep = [t for t in (pe_hist.get(cc - 2), xd_hist.get(cc - 2)) if t]
        k.dma("sync", XHT[b][:], c.xbc_fm_d[0:2048, cc * Q:(cc + 1) * Q], c.s_xh[b],
              transpose=True, deps=ldep)
        xput[b] = k.dma("sync", BTM[b][:], c.xbc_fm_d[2048:2176, cc * Q:(cc + 1) * Q],
                        c.s_xh[b], transpose=True)
        xd = k.op("gpsimd", lambda e, cc=cc, b=b: e.tensor_tensor(
            XDC[b][:].rearrange("p (h w) -> p h w", w=128),
            XHT[b][:].rearrange("p (h w) -> p h w", w=128),
            c.dectm_all[:, cc * 16:(cc + 1) * 16].unsqueeze(2).broadcast_to((128, 16, 128)),
            ALU.mult), deps=[dec_all, xput[b]] + ([pe_hist.get(cc - 2)] if cc >= 2 else []))
        xd_hist[cc] = xd
        lq = k.dma("sync", EQROW[0:1, :], c.sq_d[cc:cc + 1, :], c.s_bnc,
                   deps=[sq_all] + [t for t in (pe_hist.get(cc - 1),) if t])
        q1 = k.op("scalar", lambda e: e.activation(EQROW[0:1, :], EQROW[0:1, :], AF.Exp),
                  deps=[lq] + [t for t in (q2_hist.get(cc - 1),) if t])
        m5 = k.op("tensor", lambda e: e.matmul(c.ps[6][:, 0:16], c.ones1[:],
                                               EQROW[0:1, :], start=True, stop=True),
                  deps=[q1] + [t for t in (q2_hist.get(cc - 1),) if t])
        q2 = k.op("scalar", lambda e: e.activation(EQR[:, 0:16], c.ps[6][:, 0:16],
                                                   AF.Identity), deps=[m5])
        q2_hist[cc] = q2
        mm = None
        for h in range(16):
            pt = c.ps[2 + h // 4]
            mm = k.op("tensor", lambda e, h=h, pt=pt, b=b: e.matmul(
                pt[:, (h % 4) * 128:(h % 4 + 1) * 128], BTM[b][:],
                XDC[b][:, h * 128:(h + 1) * 128], start=True, stop=True),
                deps=[xd, xput[b]] + ([add_hist.get(cc - 1)] if cc >= 1 else []),
                inc=(h % 4 == 3))
        mu0 = k.op("vector", lambda e, cc=cc: e.tensor_tensor(
            STMP[:, 0:1024].rearrange("p (h w) -> p h w", w=128),
            c.S_t[:, 0:1024].rearrange("p (h w) -> p h w", w=128),
            EQR[:, 0:8].unsqueeze(2).broadcast_to((128, 8, 128)), ALU.mult),
            deps=[q2, upd_tok])
        mu1 = k.op("gpsimd", lambda e, cc=cc: e.tensor_tensor(
            STMP[:, 1024:2048].rearrange("p (h w) -> p h w", w=128),
            c.S_t[:, 1024:2048].rearrange("p (h w) -> p h w", w=128),
            EQR[:, 8:16].unsqueeze(2).broadcast_to((128, 8, 128)), ALU.mult),
            deps=[q2, upd_tok])
        aa = None
        for j in range(4):
            aa = k.op("vector", lambda e, j=j: e.tensor_tensor(
                c.S_t[:, j * 512:(j + 1) * 512], STMP[:, j * 512:(j + 1) * 512],
                c.ps[2 + j][:], ALU.add), deps=[mu0 if j < 2 else mu1, mm])
        add_hist[cc] = aa
        upd_tok = aa
        pe_hist[cc] = k.tok("tensor")
    if c.mode == "twoA":
        k.dma("sync", c.s_half_out.rearrange("(h n) p -> n h p", n=128),
              c.S_t[:].rearrange("n (h p) -> n h p", p=128), c.s_out, deps=[upd_tok])
    elif c.mode == "fused":
        sp = k.dma("sync", c.coll_in_d.rearrange("(h n) p -> n h p", n=128),
                   c.S_t[:].rearrange("n (h p) -> n h p", p=128), c.s_bnc, deps=[upd_tok])

        def coll(e):
            return e.collective_compute(
                "AllGather", ALU.bypass, [[0, 1, 2, 3, 4, 5, 6, 7]],
                [c.coll_in_d.opt()], [c.coll_out_d.opt()]).then_inc(c.s_coll.sem, 1)
        c.s_coll.count += 1
        k.op("gpsimd", coll, deps=[sp], inc=False)


def phase_p3z(c):
    k = c.k
    k.fence([c.s_sp, c.s_tr, c.s_w, c.s_bnc])
    XF = [c.bf("AR1", j * 4096, 4096) for j in range(8)]
    WZ = [c.bf("AR2", j * 2048, 2048) for j in range(8)]
    ZST = c.bf("AR3", 0, 16 * 512)
    for j in range(8):
        k.dma("sync", XF[j][:], c.xn2_tm_d[0:T, j * 128:(j + 1) * 128], c.s_tr, transpose=True)
        k.dma("gpsimd", WZ[j][:], c.w_in[j * 128:(j + 1) * 128, 0:2048], c.s_w)
    trtok, wtok = c.s_tr.tok(), c.s_w.tok()
    for j in range(8):
        k.op("scalar", lambda e, j=j: e.activation(
            XF[j][:], XF[j][:], AF.Identity, scale=c.lnw_t[:, j:j + 1], bias=c.lnb_t[:, j:j + 1]),
            deps=[trtok])
    aff_tok = k.tok("scalar")
    ev = []
    sp_prev = None
    for g in range(8):
        for j in range(16):
            bank = c.ps[j % 2]
            mm = None
            for kc in range(8):
                mm = k.op("tensor", lambda e, j=j, kc=kc, g=g, bank=bank: e.matmul(
                    bank[:, 0:512], WZ[kc][:, j * 128:(j + 1) * 128],
                    XF[kc][:, g * 512:(g + 1) * 512], start=(kc == 0), stop=(kc == 7)),
                    deps=[aff_tok, wtok] + ([ev[-2]] if len(ev) >= 2 else []), inc=(kc == 7))
            ev.append(k.op("scalar", lambda e, j=j, bank=bank: e.activation(
                ZST[:, j * 512:(j + 1) * 512], bank[:, 0:512], AF.Silu),
                deps=[mm] + ([sp_prev] if (j < 2 and sp_prev) else [])))
        sp_prev = k.dma("scalar", c.z_fm_d.rearrange("(j p) t -> p j t", p=128)[:, :, g * 512:(g + 1) * 512],
                        ZST[:].rearrange("p (j w) -> p j w", w=512), c.s_sp, deps=[ev[-1]])


def phase_p3(c):
    k = c.k
    k.fence([c.s_sp, c.s_tr, c.s_w])
    XF = [c.bf("AR1", j * 4096, 4096) for j in range(8)]
    for j in range(8):
        k.dma("sync", XF[j][:], c.xn_tm_d[0:T, j * 128:(j + 1) * 128], c.s_tr, transpose=True)
    xn_tok = c.s_tr.tok()
    RW = [c.bf("AR3", 26816 + j * 16, 16) for j in range(8)]
    for j in range(8):
        k.dma("gpsimd", RW[j][:], c.router_wT[j * 128:(j + 1) * 128, :], c.s_w)
    rw_tok = c.s_w.tok()
    lg_ev = [None, None]
    for i in range(NCH):
        bank = c.ps[i % 2]
        mm = None
        for kc in range(8):
            mm = k.op("tensor", lambda e, i=i, kc=kc, bank=bank: e.matmul(
                bank[:, 0:16], XF[kc][:, i * 128:(i + 1) * 128], RW[kc][:],
                start=(kc == 0), stop=(kc == 7)),
                deps=[xn_tok, rw_tok] + ([lg_ev[i % 2]] if lg_ev[i % 2] else []),
                inc=(kc == 7))
        lg_ev[i % 2] = k.op("scalar", lambda e, i=i, bank=bank: e.activation(
            c.comb[:, i * 16:(i + 1) * 16], bank[:, 0:16], AF.Identity), deps=[mm])
    lg_tok = k.tok("scalar")
    _routing_elementwise(c, lg_tok)
    cb_ev = None
    cb_sp = None
    for i in range(NCH):
        m = k.op("tensor", lambda e, i=i: e.matmul(
            c.ps[2][:, 0:128][0:16, :], c.comb[:, i * 16:(i + 1) * 16], c.i128_t[:],
            start=True, stop=True), deps=[k.tok("vector")] + ([cb_ev] if cb_ev else []))
        cb_ev = k.op("scalar", lambda e, i=i: e.activation(c.tiny16[0:16, :],
                                                           c.ps[2][:, 0:128][0:16, :],
                                                           AF.Identity),
                     deps=[m] + ([cb_sp] if cb_sp else []))
        cb_sp = k.dma("scalar", c.comb_fm_d[0:16, i * 128:(i + 1) * 128], c.tiny16[0:16, :],
                      c.s_sp, deps=[cb_ev])
    comb_sp_tok = cb_sp
    route_free = k.tok("vector")
    SGT = [c.bf("AR2", j * 2048, 2048) for j in range(8)]
    SUT = [c.bf("AR2", 16384 + j * 2048, 2048) for j in range(8)]
    for j in range(8):
        k.dma("gpsimd", SGT[j][:], c.sgT[j * 128:(j + 1) * 128, :], c.s_w)
        k.dma("gpsimd", SUT[j][:], c.suT[j * 128:(j + 1) * 128, :], c.s_w)
    sw_tok = c.s_w.tok()
    su_tok = sw_tok
    HST = c.bf("AR3", 0, 16 * 512)
    TMP = [c.f32("AR3", 8192 + t * 1024, 512) for t in range(2)]
    sp_prev = None
    ev_hist = {}
    nj = 0
    for g in range(8):
        ev_last = None
        for j in range(16):
            pg, pu = c.ps[4 + 2 * (j % 2)], c.ps[5 + 2 * (j % 2)]
            mm_g = mm_u = None
            pdep = [ev_hist.get(nj - 2)] if ev_hist.get(nj - 2) else []
            for kc in range(8):
                mm_g = k.op("tensor", lambda e, j=j, kc=kc, g=g, pg=pg: e.matmul(
                    pg[:, 0:512], SGT[kc][:, j * 128:(j + 1) * 128],
                    XF[kc][:, g * 512:(g + 1) * 512], start=(kc == 0), stop=(kc == 7)),
                    deps=[sw_tok, xn_tok] + pdep, inc=(kc == 7))
            for kc in range(8):
                mm_u = k.op("tensor", lambda e, j=j, kc=kc, g=g, pu=pu: e.matmul(
                    pu[:, 0:512], SUT[kc][:, j * 128:(j + 1) * 128],
                    XF[kc][:, g * 512:(g + 1) * 512], start=(kc == 0), stop=(kc == 7)),
                    deps=[su_tok] + pdep, inc=(kc == 7))
            tmp = TMP[j % 2]
            s1 = k.op("scalar", lambda e, tmp=tmp, pg=pg: e.activation(tmp[:], pg[:], AF.Silu),
                      deps=[mm_g] + pdep)
            ev_last = k.op("vector", lambda e, j=j, tmp=tmp, pu=pu: e.tensor_tensor(
                HST[:, j * 512:(j + 1) * 512], pu[:], tmp[:], ALU.mult),
                deps=[s1, mm_u] + ([sp_prev] if (j < 1 and sp_prev) else []))
            ev_hist[nj] = ev_last
            nj += 1
        sp_prev = k.dma("scalar",
                        c.h_fm_d.rearrange("(j p) t -> p j t", p=128)[:, 32:48, g * 512:(g + 1) * 512],
                        HST[:].rearrange("p (j w) -> p j w", w=512), c.s_sp, deps=[ev_last])
    GWB = [[c.bf("AR2", b * 8192 + j * 256, 256) for j in range(8)] for b in range(2)]
    UWB = [[c.bf("AR2", 16384 + b * 8192 + j * 256, 256) for j in range(2 * 4)] for b in range(2)]
    CRP = [c.f32("AR3", 10240 + b * 1024, 512) for b in range(2)]
    TMPE = [c.f32("AR3", 12288 + t * 1024, 512) for t in range(2)]
    HS2 = c.bf("AR3", 0, 2 * 512)
    sh_sp_tok = sp_prev
    sh_pe_tok = k.tok("tensor")
    pair_free = [ev_hist.get(nj - 2), ev_hist.get(nj - 1)]
    wtoks = [None, None]
    ctoks = [None, None]
    sp_toks = {}
    pe_ev = None
    for e_i in range(16):
        b = e_i % 2
        wdep = [c.s_sp.tok()] if e_i >= 2 else [sh_pe_tok]
        k.dma("gpsimd", c.AR2[:, b * 8192:b * 8192 + 2048].rearrange(
            "p (kc j) -> p kc j", j=256),
            c.gate_w[e_i].rearrange("(kc p) j -> p kc j", p=128), c.s_we[b], deps=wdep)
        wtoks[b] = k.dma("gpsimd", c.AR2[:, 16384 + b * 8192:16384 + b * 8192 + 2048].rearrange(
            "p (kc j) -> p kc j", j=256),
            c.up_w[e_i].rearrange("(kc p) j -> p kc j", p=128), c.s_we[b])
        for g in range(8):
            bb = g % 2
            cdep = [c.s_sp.tok()]
            ctoks[bb] = k.dma("sync", CRP[bb][:],
                              c.comb_fm_d[e_i:e_i + 1, g * 512:(g + 1) * 512].broadcast_to((128, 512)),
                              c.s_cr[bb], deps=cdep + [comb_sp_tok, route_free])
            v2 = None
            for jj in range(2):
                pg, pu = c.ps[4 + 2 * jj], c.ps[5 + 2 * jj]
                mm_g = mm_u = None
                pbank = [t for t in (pair_free[jj],) if t]
                for kc in range(8):
                    mm_g = k.op("tensor", lambda e, jj=jj, kc=kc, g=g, b=b, pg=pg: e.matmul(
                        pg[:, 0:512],
                        c.AR2[:, b * 8192 + kc * 256 + jj * 128: b * 8192 + kc * 256 + jj * 128 + 128],
                        XF[kc][:, g * 512:(g + 1) * 512], start=(kc == 0), stop=(kc == 7)),
                        deps=[wtoks[b], xn_tok] + pbank, inc=(kc == 7))
                for kc in range(8):
                    mm_u = k.op("tensor", lambda e, jj=jj, kc=kc, g=g, b=b, pu=pu: e.matmul(
                        pu[:, 0:512],
                        c.AR2[:, 16384 + b * 8192 + kc * 256 + jj * 128: 16384 + b * 8192 + kc * 256 + jj * 128 + 128],
                        XF[kc][:, g * 512:(g + 1) * 512], start=(kc == 0), stop=(kc == 7)),
                        deps=pbank, inc=(kc == 7))
                tmp = TMPE[jj]
                s1 = k.op("scalar", lambda e, tmp=tmp, pg=pg: e.activation(tmp[:], pg[:], AF.Silu),
                          deps=[mm_g] +
                          ([sh_sp_tok] if (e_i == 0 and g == 0 and jj == 0) else []))
                v1 = k.op("vector", lambda e, tmp=tmp, pu=pu: e.tensor_tensor(
                    tmp[:], pu[:], tmp[:], ALU.mult), deps=[s1, mm_u])
                v2 = k.op("vector", lambda e, jj=jj, bb=bb, tmp=tmp: e.tensor_tensor(
                    HS2[:, jj * 512:(jj + 1) * 512], tmp[:], CRP[bb][:], ALU.mult),
                    deps=[v1, ctoks[bb]] +
                    ([sh_sp_tok] if (e_i == 0 and g == 0 and jj == 0) else []))
                pair_free[jj] = v2
            pe_ev = k.tok("vector")
            sp_toks[(e_i, g)] = k.dma(
                "scalar",
                c.h_fm_d.rearrange("(j p) t -> p j t", p=128)[:, 2 * e_i:2 * e_i + 2,
                                                              g * 512:(g + 1) * 512],
                HS2[:].rearrange("p (j w) -> p j w", w=512), c.s_sp,
                deps=[pe_ev])
    dwt = {}
    wqs = [c.s_w, c.s_w2, c.s_w3]
    qi = 0
    for e_i in range(16):
        if e_i % 4 == 0:
            continue
        dwt[e_i] = k.dma("gpsimd",
                         c.AR2[:, e_i * 2048:(e_i + 1) * 2048].rearrange("p (kc d) -> p kc d", d=1024),
                         c.down_w[e_i].rearrange("(kc p) d -> p kc d", p=128),
                         wqs[qi % 3], deps=[sh_pe_tok])
        qi += 1
    k.fence([c.s_sp, c.s_g2])
    DW = [c.bf("AR2", i * 1024, 1024) for i in range(32)]
    SD = [c.bf("AR1", i * 1024, 1024) for i in range(16)]
    for e_i in (0, 4, 8, 12):
        dwt[e_i] = k.dma("gpsimd",
                         c.AR2[:, e_i * 2048:(e_i + 1) * 2048].rearrange("p (kc d) -> p kc d", d=1024),
                         c.down_w[e_i].rearrange("(kc p) d -> p kc d", p=128), wqs[e_i // 4 % 3])
    sdt = {}
    for j in range(16):
        sdt[j] = k.dma("gpsimd", SD[j][:], c.sdT[j * 128:(j + 1) * 128, :], wqs[j % 3])
    HLD = [c.bf("AR3", b * 6144, 6144) for b in range(2)]
    GT = c.f32("AR3", 12288, 1024)
    htoks = [None, None]
    gsp = None
    gev = [None] * NCH
    pe_hist = [None] * NCH
    for g in range(NCH):
        b = g % 2
        hdep = [pe_hist[g - 2]] if g >= 2 and pe_hist[g - 2] else []
        htoks[b] = k.dma("sync", HLD[b][:].rearrange("p (j w) -> p j w", w=128),
                         c.h_fm_d.rearrange("(j p) t -> p j t", p=128)[:, :, g * Q:(g + 1) * Q],
                         c.s_hl[b], deps=hdep)
        pd = [c.ps[0 + 2 * (g % 2)], c.ps[1 + 2 * (g % 2)]]
        mm = None
        for half in range(2):
            first = True
            for j in range(48):
                src = (c.AR2[:, (j // 2) * 2048 + (j % 2) * 1024 + half * 512:
                             (j // 2) * 2048 + (j % 2) * 1024 + half * 512 + 512]
                       if j < 32 else SD[j - 32][:, half * 512:half * 512 + 512])
                bankdep = [gev[g - 2]] if (first and g >= 2 and gev[g - 2]) else []
                wtok = dwt[j // 2] if j < 32 else sdt[j - 32]
                mm = k.op("tensor", lambda e, j=j, half=half, b=b, src=src, pd=pd, first=first:
                          e.matmul(pd[half][:, 0:512],
                                   HLD[b][:, j * 128:(j + 1) * 128], src,
                                   start=first, stop=(j == 47)),
                          deps=[htoks[b], wtok] + bankdep, inc=(j == 47))
                first = False
        pe_hist[g] = k.tok("tensor")
        gdep = [gsp] if gsp else []
        e1 = k.op("scalar", lambda e, pd=pd: e.activation(GT[:, 0:512], pd[0][:], AF.Silu),
                  deps=[mm] + gdep)
        e2 = k.op("scalar", lambda e, pd=pd: e.activation(GT[:, 512:1024], pd[1][:], AF.Silu))
        gev[g] = e2
        gsp = k.dma("scalar", c.g_tm_d[g * Q:(g + 1) * Q, :], GT[:], c.s_sp, deps=[e2])


def _routing_elementwise(c, lg_tok):
    k = c.k
    SC = c.f32("AR3", 0, 512)
    SBv = c.f32("AR3", 1024, 512)
    GS = c.f32("AR3", 2048, 128)
    GM = c.f32("AR3", 2304, 128)
    E1 = c.f32("AR3", 2560, 128)
    E2 = c.f32("AR3", 2816, 128)
    MK = c.f32("AR3", 3072, 512)
    SEL = c.f32("AR3", 4096, 512)
    EQ = c.f32("AR3", 5120, 512)
    CU = c.f32("AR3", 6144, 512)
    CV = c.f32("AR3", 7168, 512)
    MX = c.f32("AR3", 8192, 32)
    CB = c.f32("AR3", 8256, 16)
    WS = c.f32("AR3", 8288, 32)
    t0 = k.op("vector", lambda e: e.tensor_scalar(SC[:], c.comb[:], 0.5, None, ALU.mult),
              deps=[lg_tok])
    t1 = k.op("scalar", lambda e: e.activation(SC[:], SC[:], AF.Tanh), deps=[t0])
    t2 = k.op("vector", lambda e: e.tensor_scalar(SC[:], SC[:], 0.5, 0.5, ALU.mult, ALU.add),
              deps=[t1])
    m0 = k.op("tensor", lambda e: e.matmul(c.ps[3][:, 0:16], c.ones1[:], c.corrb_t[:],
                                           start=True, stop=True), deps=[c.consts])
    c0 = k.op("scalar", lambda e: e.activation(CB[:, 0:16], c.ps[3][:, 0:16], AF.Identity),
              deps=[m0])
    t3 = k.op("vector", lambda e: e.tensor_tensor(
        SBv[:].rearrange("p (a b) -> p a b", b=16), SC[:].rearrange("p (a b) -> p a b", b=16),
        CB[:, 0:16].unsqueeze(1).broadcast_to((128, 32, 16)), ALU.add), deps=[t2, c0])
    V = SBv.rearrange("p (a b) -> p a b", b=4)
    first = True
    for i in range(4):
        for j in range(i + 1, 4):
            if first:
                k.op("vector", lambda e, i=i, j=j: e.tensor_tensor(GS[:], V[:, :, i], V[:, :, j],
                                                                   ALU.add), deps=[t3])
                first = False
            else:
                k.op("vector", lambda e, i=i, j=j: e.tensor_tensor(CU[:, 0:128], V[:, :, i],
                                                                   V[:, :, j], ALU.add))
                k.op("vector", lambda e: e.tensor_tensor(GS[:], GS[:], CU[:, 0:128], ALU.max))
    k.op("vector", lambda e: e.memset(GM[:], 0.0))
    GSv = GS.rearrange("p (a b) -> p a b", b=4)
    for it in range(2):
        k.op("vector", lambda e: e.tensor_reduce(MX[:].rearrange("p (a b) -> p a b", b=1),
                                                 GSv, AX.X, ALU.max))
        k.op("vector", lambda e: e.tensor_tensor(
            E1[:].rearrange("p (a b) -> p a b", b=4), GSv,
            MX[:].unsqueeze(2).broadcast_to((128, 32, 4)), ALU.is_equal))
        k.op("vector", lambda e: e.tensor_copy(E2[:], E1[:]))
        k.op("vector", lambda e: e.tensor_tensor(
            E2[:].rearrange("p (a b) -> p a b", b=4)[:, :, 1:4],
            E1[:].rearrange("p (a b) -> p a b", b=4)[:, :, 1:4],
            E1[:].rearrange("p (a b) -> p a b", b=4)[:, :, 0:3], ALU.add))
        k.op("vector", lambda e: e.tensor_copy(CU[:, 0:128], E2[:]))
        k.op("vector", lambda e: e.tensor_tensor(
            CU[:, 0:128].rearrange("p (a b) -> p a b", b=4)[:, :, 2:4],
            E2[:].rearrange("p (a b) -> p a b", b=4)[:, :, 2:4],
            E2[:].rearrange("p (a b) -> p a b", b=4)[:, :, 0:2], ALU.add))
        k.op("vector", lambda e: e.tensor_tensor(CU[:, 0:128], CU[:, 0:128], E1[:],
                                                 ALU.subtract))
        k.op("vector", lambda e: e.tensor_scalar(CU[:, 0:128], CU[:, 0:128], 0.0, None,
                                                 ALU.is_equal))
        k.op("vector", lambda e: e.tensor_tensor(E1[:], E1[:], CU[:, 0:128], ALU.mult))
        k.op("vector", lambda e: e.tensor_tensor(GM[:], GM[:], E1[:], ALU.add))
        if it == 0:
            k.op("vector", lambda e: e.scalar_tensor_tensor(GS[:], E1[:], NEG, GS[:],
                                                            ALU.mult, ALU.add))
    k.op("vector", lambda e: e.tensor_scalar(GM[:], GM[:], -1.0, NEG, ALU.add, ALU.mult))
    k.op("vector", lambda e: e.tensor_scalar(GM[:], GM[:], -1.0, None, ALU.mult))
    k.op("vector", lambda e: e.tensor_tensor(
        MK[:].rearrange("p (a b) -> p a b", b=4), SBv.rearrange("p (a b) -> p a b", b=4),
        GM[:].unsqueeze(2).broadcast_to((128, 128, 4)), ALU.add))
    k.op("vector", lambda e: e.memset(SEL[:], 0.0))
    MKv = MK.rearrange("p (a b) -> p a b", b=16)
    for it in range(4):
        k.op("vector", lambda e: e.tensor_reduce(MX[:].rearrange("p (a b) -> p a b", b=1),
                                                 MKv, AX.X, ALU.max))
        k.op("vector", lambda e: e.tensor_tensor(
            EQ[:].rearrange("p (a b) -> p a b", b=16), MKv,
            MX[:].unsqueeze(2).broadcast_to((128, 32, 16)), ALU.is_equal))
        k.op("vector", lambda e: e.tensor_copy(CU[:], EQ[:]))
        src, dst = CU, CV
        for sh in (1, 2, 4, 8):
            k.op("vector", lambda e, sh=sh, src=src, dst=dst: e.tensor_copy(
                dst[:].rearrange("p (a b) -> p a b", b=16)[:, :, 0:sh],
                src[:].rearrange("p (a b) -> p a b", b=16)[:, :, 0:sh]))
            k.op("vector", lambda e, sh=sh, src=src, dst=dst: e.tensor_tensor(
                dst[:].rearrange("p (a b) -> p a b", b=16)[:, :, sh:16],
                src[:].rearrange("p (a b) -> p a b", b=16)[:, :, sh:16],
                src[:].rearrange("p (a b) -> p a b", b=16)[:, :, 0:16 - sh], ALU.add))
            src, dst = dst, src
        k.op("vector", lambda e, src=src: e.tensor_tensor(CV[:] if src is CU else CU[:],
                                                          src[:], EQ[:], ALU.subtract))
        exv = CV if src is CU else CU
        k.op("vector", lambda e, exv=exv: e.tensor_scalar(exv[:], exv[:], 0.0, None,
                                                          ALU.is_equal))
        k.op("vector", lambda e, exv=exv: e.tensor_tensor(EQ[:], EQ[:], exv[:], ALU.mult))
        k.op("vector", lambda e: e.tensor_tensor(SEL[:], SEL[:], EQ[:], ALU.add))
        if it < 3:
            k.op("vector", lambda e: e.scalar_tensor_tensor(MK[:], EQ[:], NEG, MK[:],
                                                            ALU.mult, ALU.add))
    k.op("vector", lambda e: e.tensor_tensor(SEL[:], SEL[:], SC[:], ALU.mult))
    k.op("vector", lambda e: e.tensor_reduce(WS[:].rearrange("p (a b) -> p a b", b=1),
                                             SEL[:].rearrange("p (a b) -> p a b", b=16),
                                             AX.X, ALU.add))
    k.op("vector", lambda e: e.tensor_scalar(WS[:], WS[:], 1e-20, None, ALU.add))
    k.op("vector", lambda e: e.reciprocal(WS[:], WS[:]))
    k.op("vector", lambda e: e.tensor_tensor(
        c.comb[:].rearrange("p (a b) -> p a b", b=16),
        SEL[:].rearrange("p (a b) -> p a b", b=16),
        WS[:].unsqueeze(2).broadcast_to((128, 32, 16)), ALU.mult))


import os as _os
DBG_NODEFER = bool(int(_os.environ.get("DBG_NODEFER", "0")))


def phase_p2a(c):
    k = c.k
    if c.mode != "twoB":
        SG2 = [c.f32("AR3", 14336 + b_ * 2048, 1024) for b_ in range(2)]
        init_tok = k.op("vector", lambda e: e.memset(c.S_t[:], 0.0))
        sg_toks = {}
        for idx in range(16):
            r, half = idx // 2, idx % 2
            b2 = idx % 2
            ld = k.dma("sync", SG2[b2][:].rearrange("n (h p) -> n h p", p=128),
                       c.coll_out_d[r * DI + half * 1024:r * DI + (half + 1) * 1024]
                       .rearrange("(h n) p -> n h p", n=128),
                       c.s_in, deps=[(c.s_coll.sem, c.s_coll.count), init_tok] +
                       ([sg_toks[idx - 2]] if idx >= 2 else []))
            sg_toks[idx] = k.op("vector", lambda e, r=r, half=half, b2=b2:
                                e.scalar_tensor_tensor(
                c.S_t[:, half * 1024:(half + 1) * 1024], SG2[b2][:],
                c.smask_t[:, r:r + 1], c.S_t[:, half * 1024:(half + 1) * 1024],
                ALU.mult, ALU.add), deps=[ld, c.consts])
        init_tok = sg_toks[15]
    k.fence([c.s_sp, c.s_tr, c.s_w, c.s_g2, c.s_bnc])
    XHT = [c.bf("AR1", b * 2048, 2048) for b in range(2)]
    SZT = [c.bf("AR1", 4096 + b * 2048, 2048) for b in range(2)]
    LL = [c.bf("AR1", 8192 + b * 2048, 2048) for b in range(2)]
    MT = [c.bf("AR1", 12288 + b * 2048, 2048) for b in range(2)]
    XTI = [c.bf("AR1", 16384 + b * 2048, 2048) for b in range(2)]
    XDC = [c.bf("AR1", 20480 + b * 2048, 2048) for b in range(2)]
    SBF = c.bf("AR1", 24576, 2048)
    YTM = c.f32("AR1", 26624, 2048)
    YBF = [c.bf("AR2", b * 2048, 2048) for b in range(2)]
    BTM = [c.bf("AR2", 4096 + b * 128, 128) for b in range(2)]
    BFM = [c.bf("AR2", 4352 + b * 128, 128) for b in range(2)]
    CFM = [c.bf("AR2", 4608 + b * 128, 128) for b in range(2)]
    GSB = [c.f32("AR2", 4864 + b * 256, 128) for b in range(2)]
    S17 = [c.f32("AR2", 5376 + b * 256, 128, 17) for b in range(2)]
    ECOL = [c.f32("AR2", 5888 + b * 32, 16) for b in range(2)]
    EQR = [c.f32("AR2", 5952 + b * 32, 16) for b in range(2)]
    EQROW = [c.f32("AR2", 6016 + b * 32, 16, 1) for b in range(2)]
    T2E = [c.bf("AR2", 6080, 512), c.bf("AR2", 6592, 512)]
    DMS = [c.f32("AR2", 7104 + b * 4096, 2048) for b in range(2)]
    Y2 = c.f32("AR2", 15296, 2048)
    STMP = c.f32("AR2", 19392, 2048)
    DRHS = [c.f32("AR2", 23488 + b * 4096, 2048, 17) for b in range(2)]
    msqz = k.op("vector", lambda e: e.memset(c.msq_all[:], 0.0))
    nd_tok = [k.dma("sync", DRHS[b_][0:16, :], c.negd, c.s_const) for b_ in range(2)][-1]
    if c.mode == "twoB":
        si = k.dma("sync", c.S_t[:].rearrange("n (h p) -> n h p", p=128),
                   c.s_init_in.rearrange("(h n) p -> n h p", n=128), c.s_in)
        init_tok = k.op("vector", lambda e: e.engine_nop(), deps=[si])
    st = dict(upd=init_tok, upd0=init_tok, yb=None, m5=[None, None], dbuild=[None, None],
              lexp=[None, None], dv_e=[None, None], ge=[None, None], q2=[None, None],
              scst=None, prev_yt=None, prev_yt2=None, dvo_tok=None,
              xput=[None, None], bput=[None, None], szput=[None, None],
              bfput=[None, None], cfput=[None, None], s17put=[None, None],
              mt={}, xt={}, xd={}, ectok={}, eq={}, lastmm1={}, gt={}, ds={},
              gm={}, mm2l={}, mul={}, e2l={}, ms={}, getok={}, q1=[None, None],
              dl01=[None, None], dv3=None)

    def d23(cc, ytk3=None):
        b = cc % 2
        dl = st["dl01"][b]
        for gi in (2, 3):
            pD = c.ps[1] if gi % 2 == 0 else c.ps[6]
            ddep = [dl[gi - 2]] + ([ytk3] if (gi == 3 and ytk3) else [])
            dm = k.op("tensor", lambda e, gi=gi, b=b, pD=pD: e.matmul(
                pD[:, 0:512], S17[b][0:17, :], DRHS[b][0:17, gi * 512:(gi + 1) * 512],
                start=True, stop=True), deps=ddep)
            st["dbuild"][b] = dm
            dv = k.op("vector", lambda e, gi=gi, b=b, pD=pD: e.tensor_tensor(
                DMS[b][:, gi * 512:(gi + 1) * 512], pD[:, 0:512], c.tricap_t[:],
                ALU.min), deps=[dm, c.consts])
            if gi == 2:
                st["dv_e"][b] = dv
            else:
                st["dv3"] = dv
            k.op("scalar", lambda e, gi=gi, b=b: e.activation(
                LL[b][:, gi * 512:(gi + 1) * 512], DMS[b][:, gi * 512:(gi + 1) * 512],
                AF.Exp), deps=[dv])
        st["lexp"][b] = k.tok("scalar")

    def eqg(cc):
        b = cc % 2
        m5 = k.op("tensor", lambda e, b=b: e.matmul(c.ps[0][:, 128 + b * 16:144 + b * 16],
                                                    c.ones1[:], EQROW[b][0:1, :],
                                                    start=True, stop=True),
                  deps=[st["q1"][b]] + [t for t in (st["q2"][b],) if t])
        q2 = k.op("scalar", lambda e, b=b: e.activation(EQR[b][:, 0:16],
                                                        c.ps[0][:, 128 + b * 16:144 + b * 16],
                                                        AF.Identity),
                  deps=[m5] + [t for t in (st["mul"].get(cc - 2),) if t])
        st["m5"][b] = m5
        st["q2"][b] = q2
        st["eq"][cc] = q2
        gm = k.op("tensor", lambda e, b=b: e.matmul(c.ps[0][:, 0:128], BFM[b][:], CFM[b][:],
                                                    start=True, stop=True),
                  deps=[st["bfput"][b], q2] + [t for t in (st["ge"][1 - b],) if t])
        st["gm"][cc] = gm
        ge = k.op("scalar", lambda e, b=b: e.activation(GSB[b][:], c.ps[0][:, 0:128],
                                                        AF.Identity),
                  deps=[gm] + [t for t in (st["mt"].get(cc - 2),) if t])
        st["ge"][b] = ge
        st["getok"][cc] = ge

    def mtxt(cc):
        b = cc % 2
        mt = k.op("gpsimd", lambda e, b=b: e.tensor_tensor(
            MT[b][:].rearrange("p (h w) -> p h w", w=128),
            LL[b][:].rearrange("p (h w) -> p h w", w=128),
            GSB[b][:].unsqueeze(1).broadcast_to((128, 16, 128)), ALU.mult),
            deps=[st["lexp"][b], st["getok"][cc]] +
            [t for t in (st["lastmm1"].get(cc - 2),) if t])
        xt = k.op("gpsimd", lambda e, cc=cc, b=b: e.tensor_tensor(
            XTI[b][:].rearrange("p (h w) -> p h w", w=128),
            XHT[b][:].rearrange("p (h w) -> p h w", w=128),
            c.dttm_all[:, cc * 16:(cc + 1) * 16].unsqueeze(2).broadcast_to((128, 16, 128)),
            ALU.mult), deps=[st["xput"][b]] + [t for t in (st["lastmm1"].get(cc - 2),) if t])
        st["mt"][cc] = mt
        st["xt"][cc] = xt

    def xdo(cc):
        b = cc % 2
        st["xd"][cc] = k.op("vector", lambda e, cc=cc, b=b: e.tensor_tensor(
            XDC[b][:].rearrange("p (h w) -> p h w", w=128),
            XHT[b][:].rearrange("p (h w) -> p h w", w=128),
            c.dectm_all[:, cc * 16:(cc + 1) * 16].unsqueeze(2).broadcast_to((128, 16, 128)),
            ALU.mult), deps=[st["xput"][b]] + [t for t in (st["ds"].get(cc - 2),) if t])

    def s1(cc):
        b = cc % 2
        eqput = k.dma("sync", EQROW[b][0:1, :], c.sq_d[cc:cc + 1, :], c.s_pb[b],
              deps=[t for t in (st["m5"][b],) if t])
        s17dep = [t for t in (st["dbuild"][b],) if t]
        st["s17put"][b] = k.dma("sync", S17[b][:], c.sfm_d[cc], c.s_pb[b], deps=s17dep)
        drput = k.dma("sync", DRHS[b][16:17, :], c.sflat_d[cc:cc + 1, :], c.s_pb[b])
        gpb = [t for t in (st["gm"].get(cc - 2), st["mm2l"].get(cc - 2)) if t]
        k.dma("sync", BFM[b][:], c.xbc_fm_d[2048:2176, cc * Q:(cc + 1) * Q],
              c.s_pb[b], deps=gpb)
        st["bfput"][b] = st["cfput"][b] = k.dma(
            "sync", CFM[b][:], c.xbc_fm_d[2176:2304, cc * Q:(cc + 1) * Q], c.s_pb[b])
        gpa = [t for t in (st["lastmm1"].get(cc - 2), st["gt"].get(cc - 2),
                           st["ds"].get(cc - 2)) if t]
        k.dma("sync", XHT[b][:], c.xbc_fm_d[0:2048, cc * Q:(cc + 1) * Q], c.s_pa[b],
              transpose=True, deps=gpa)
        k.dma("sync", BTM[b][:], c.xbc_fm_d[2048:2176, cc * Q:(cc + 1) * Q], c.s_pa[b],
              transpose=True)
        st["xput"][b] = st["bput"][b] = st["szput"][b] = k.dma(
            "sync", SZT[b][:], c.z_fm_d[0:2048, cc * Q:(cc + 1) * Q], c.s_pa[b],
            transpose=True)
        ec = k.op("scalar", lambda e, cc=cc, b=b: e.activation(
            ECOL[b][:, 0:16], c.stm_all[:, cc * 16:(cc + 1) * 16], AF.Exp),
            deps=[t for t in (st["e2l"].get(cc - 2),) if t])
        st["ectok"][cc] = ec
        dl = [None, None]
        for gi in range(2):
            pD = c.ps[1] if gi % 2 == 0 else c.ps[6]
            ddep = [st["s17put"][b], drput, nd_tok]
            ddep += ([t for t in (st["dv_e"][1 - b],) if t] if gi == 0 else
                     [t for t in (st["prev_yt"], st["dv3"]) if t])
            dm = k.op("tensor", lambda e, gi=gi, b=b, pD=pD: e.matmul(
                pD[:, 0:512], S17[b][0:17, :], DRHS[b][0:17, gi * 512:(gi + 1) * 512],
                start=True, stop=True), deps=ddep)
            dv = k.op("vector", lambda e, gi=gi, b=b, pD=pD: e.tensor_tensor(
                DMS[b][:, gi * 512:(gi + 1) * 512], pD[:, 0:512], c.tricap_t[:],
                ALU.min), deps=[dm, c.consts] +
                [t for t in (st["lexp"][b],) if t and gi == 0])
            dl[gi] = dv
            k.op("scalar", lambda e, gi=gi, b=b: e.activation(
                LL[b][:, gi * 512:(gi + 1) * 512], DMS[b][:, gi * 512:(gi + 1) * 512],
                AF.Exp), deps=[dv] + [t for t in (st["mt"].get(cc - 2),) if t and gi == 0])
        st["dl01"][b] = dl
        st["dvo_tok"] = dl[1]
        st["q1"][b] = k.op("scalar", lambda e, b=b: e.activation(
            EQROW[b][0:1, :], EQROW[b][0:1, :], AF.Exp), deps=[eqput])
        if cc == 0 or DBG_NODEFER:
            d23(0)
            eqg(0)
            mtxt(0)
            xdo(0)

    def s23(cc):
        b = cc % 2
        sca = k.op("scalar", lambda e: e.activation(SBF[:, 0:1024], c.S_t[:, 0:1024],
                                                    AF.Identity),
                   deps=[st["upd0"]] + [t for t in (st["mm2l"].get(cc - 1),) if t])
        scb = k.op("scalar", lambda e: e.activation(SBF[:, 1024:2048], c.S_t[:, 1024:2048],
                                                    AF.Identity),
                   deps=[st["upd"]] + [t for t in (st["mm2l"].get(cc - 1),) if t])
        ytoks = []
        e2_prev = {0: st["prev_yt2"], 1: st["prev_yt"]}
        mm2 = None
        e2 = None
        for hg in range(4):
            p1 = c.ps[4 + 2 * (hg % 2)]
            p2 = c.ps[5 + 2 * (hg % 2)]
            if hg >= 2:
                bank_free = [ytoks[hg - 2]]
            else:
                bank_free = [t for t in ((st["prev_yt2"],) if hg == 0 else
                                         (st["prev_yt"],)) if t]
            if hg % 2 == 1:
                bank_free = bank_free + [t for t in (st["dvo_tok"], st["dv3"]) if t]
            mm1 = None
            for hh in range(4):
                h = hg * 4 + hh
                mm1 = k.op("tensor", lambda e, h=h, hh=hh, p1=p1, b=b: e.matmul(
                    p1[:, hh * 128:(hh + 1) * 128], MT[b][:, h * 128:(h + 1) * 128],
                    XTI[b][:, h * 128:(h + 1) * 128], start=True, stop=False),
                    deps=[st["mt"][cc], st["xt"][cc]] + bank_free, inc=False)
                mm1 = k.op("tensor", lambda e, h=h, hh=hh, p1=p1, b=b: e.matmul(
                    p1[:, hh * 128:(hh + 1) * 128], c.dhi[:, h * 128:(h + 1) * 128],
                    XHT[b][:, h * 128:(h + 1) * 128], start=False, stop=True),
                    deps=[c.dhi_tok], inc=(hh == 3))
            scd = sca if hg < 2 else scb
            for hh in range(4):
                h = hg * 4 + hh
                mm2 = k.op("tensor", lambda e, h=h, hh=hh, p2=p2, b=b: e.matmul(
                    p2[:, hh * 128:(hh + 1) * 128], CFM[b][:], SBF[:, h * 128:(h + 1) * 128],
                    start=True, stop=True), deps=[scd, st["cfput"][b]] + bank_free,
                    inc=(hh == 3))
            for hh in range(4):
                h = hg * 4 + hh
                e2 = k.op("scalar", lambda e, h=h, hh=hh, p2=p2, hg=hg, b=b: e.activation(
                    T2E[hg % 2][:, hh * 128:(hh + 1) * 128], p2[:, hh * 128:(hh + 1) * 128],
                    AF.Identity, scale=ECOL[b][:, h:h + 1]),
                    deps=[mm2, st["ectok"][cc]] +
                    ([e2_prev[hg % 2]] if (hg < 2 and e2_prev[hg % 2]) else
                     [ytoks[hg - 2]] if hg >= 2 else []))
            yt = k.op("vector", lambda e, hg=hg, p1=p1: e.tensor_tensor(
                YTM[:, hg * 512:(hg + 1) * 512], p1[:, 0:512], T2E[hg % 2][:], ALU.add),
                deps=[e2, mm1] + [t for t in (st["ms"].get(cc - 1),) if t])
            ytoks.append(yt)
        st["mm2l"][cc] = mm2
        st["e2l"][cc] = e2
        st["prev_yt"] = ytoks[3]
        st["prev_yt2"] = ytoks[2]
        st["lastmm1"][cc] = ytoks[3]
        mm_ = None
        for half in range(2):
            pa, pb2 = c.ps[2], c.ps[3]
            for hh in range(8):
                h = half * 8 + hh
                pt = pa if hh < 4 else pb2
                mm_ = k.op("tensor", lambda e, h=h, hh=hh, pt=pt, b=b: e.matmul(
                    pt[:, (hh % 4) * 128:(hh % 4 + 1) * 128], BTM[b][:],
                    XDC[b][:, h * 128:(h + 1) * 128], start=True, stop=True),
                    deps=[st["xd"][cc], st["bput"][b]] +
                    ([st["upd"], sca, scb] if hh == 0 else []), inc=(hh == 7))
            mu = k.op("gpsimd", lambda e, half=half, b=b: e.tensor_tensor(
                STMP[:, half * 1024:(half + 1) * 1024].rearrange("p (h w) -> p h w", w=128),
                c.S_t[:, half * 1024:(half + 1) * 1024].rearrange("p (h w) -> p h w", w=128),
                EQR[b][:, half * 8:half * 8 + 8].unsqueeze(2).broadcast_to((128, 8, 128)),
                ALU.mult), deps=[st["eq"][cc], st["upd0" if half == 0 else "upd"],
                                 sca if half == 0 else scb])
            st["mul"][cc] = mu
            a1 = k.op("vector", lambda e, half=half, pa=pa: e.tensor_tensor(
                c.S_t[:, half * 1024:half * 1024 + 512], STMP[:, half * 1024:half * 1024 + 512],
                pa[:], ALU.add), deps=[mu, mm_])
            st["upd"] = k.op("vector", lambda e, half=half, pb2=pb2: e.tensor_tensor(
                c.S_t[:, half * 1024 + 512:half * 1024 + 1024],
                STMP[:, half * 1024 + 512:half * 1024 + 1024], pb2[:], ALU.add))
            if half == 0:
                st["upd0"] = st["upd"]
        st["ds"][cc] = mm_
        if cc + 1 < NCH and not DBG_NODEFER:
            eqg(cc + 1)
            d23(cc + 1, ytk3=ytoks[3])
        gta = k.op("vector", lambda e, b=b: e.tensor_tensor(
            YTM[:, 0:1024], YTM[:, 0:1024], SZT[b][:, 0:1024], ALU.mult),
            deps=[ytoks[-1], st["szput"][b]])
        gtb = k.op("gpsimd", lambda e, b=b: e.tensor_tensor(
            YTM[:, 1024:2048], YTM[:, 1024:2048], SZT[b][:, 1024:2048], ALU.mult),
            deps=[ytoks[-1], st["szput"][b]] + [t for t in (st["ms"].get(cc - 1),) if t])
        gt = gta
        st["gt"][cc] = gtb
        ms = k.op("scalar", lambda e, cc=cc: e.activation(Y2[:], YTM[:], AF.Square,
                                                          accum_out=c.msq_all[:, cc:cc + 1]),
                  deps=[gta, gtb, msqz])
        st["ms"][cc] = ms
        r1 = k.op("vector", lambda e, cc=cc: e.tensor_scalar(
            c.tiny[:, 32:33], c.msq_all[:, cc:cc + 1], 1.0 / DI, EPS, ALU.mult, ALU.add),
            deps=[ms])
        r2 = k.op("scalar", lambda e: e.activation(c.tiny[:, 33:34], c.tiny[:, 32:33], AF.Ln),
                  deps=[r1])
        r2b = k.op("vector", lambda e: e.tensor_scalar(c.tiny[:, 33:34], c.tiny[:, 33:34],
                                                       -0.5, None, ALU.mult), deps=[r2])
        r3 = k.op("scalar", lambda e, cc=cc: e.activation(c.rstd_all[:, cc:cc + 1],
                                                          c.tiny[:, 33:34], AF.Exp), deps=[r2b])
        yb = k.op("vector", lambda e, cc=cc, b=b: e.tensor_scalar(
            YBF[b][:], YTM[:], c.rstd_all[:, cc:cc + 1], None, ALU.mult),
            deps=[r3] + ([c.s_sp.tok()] if cc >= 2 else []))
        st["yb"] = yb
        k.dma("scalar", c.y_tm_d[cc * Q:(cc + 1) * Q, :], YBF[b][:], c.s_sp, deps=[yb])
        if cc + 1 < NCH and not DBG_NODEFER:
            xdo(cc + 1)
            mtxt(cc + 1)

    s1(0)
    for cc in range(1, NCH):
        s1(cc)
        s23(cc - 1)
    s23(NCH - 1)


def phase_p2b(c):
    k = c.k
    k.fence([c.s_sp, c.s_tr, c.s_w, c.s_g2, c.s_in])
    WO = [c.bf("AR2", j * 1024, 1024) for j in range(16)]
    YF2 = [[c.bf("AR1", p_ * 16384 + j * 1024, 1024) for j in range(16)] for p_ in range(2)]
    XT = [c.f32("AR3", b * 2048, DM) for b in range(2)]
    GTL = [c.f32("AR3", 4096 + b * 2048, DM) for b in range(2)]
    XN = c.f32("AR3", 8192, DM)
    SS = c.f32("AR3", 10240, DM)
    OT = [c.f32("AR3", 12288 + b * 2048, DM) for b in range(2)]
    for j in range(16):
        k.dma("gpsimd", WO[j][:], c.w_out[j * 128:(j + 1) * 128, :], c.s_w2)
    wld = c.s_w2.tok()
    for j in range(16):
        k.op("scalar", lambda e, j=j: e.activation(WO[j][:], WO[j][:], AF.Identity,
                                                   scale=c.rmsw_t[:, j:j + 1]),
             deps=[wld, c.consts])
    wo_tok = k.tok("scalar")
    otok = [None, None]
    vfree = [None, None]
    vprev = None
    ptoks = {}
    yput = {}
    for sg in range(4):
        p_ = sg % 2
        for j in range(16):
            dep = [ptoks[sg - 2]] if sg >= 2 else []
            yput[(sg, j)] = k.dma("sync", YF2[p_][j][:],
                                  c.y_tm_d[sg * 1024:(sg + 1) * 1024, j * 128:(j + 1) * 128],
                                  c.s_tr, transpose=True, deps=dep)
        for tc in range(8):
            g = sg * 8 + tc
            b = tc % 2
            pd = [c.ps[0 + 2 * b], c.ps[1 + 2 * b]]
            xl = k.dma("sync", XT[b][:], c.x_main[g * Q:(g + 1) * Q, :], c.s_xl[b],
                       deps=[otok[b]] if otok[b] else [])
            gl = k.dma("gpsimd", GTL[b][:], c.g_tm_d[g * Q:(g + 1) * Q, :], c.s_gl[b],
                       deps=[otok[b]] if otok[b] else [])
            mm = [None, None]
            for half in range(2):
                first = True
                for j in range(16):
                    mm[half] = k.op("tensor", lambda e, j=j, half=half, tc=tc, pd=pd,
                                    first=first, p_=p_:
                                    e.matmul(pd[half][:, 0:512],
                                             YF2[p_][j][:, tc * 128:(tc + 1) * 128],
                                             WO[j][:, half * 512:half * 512 + 512],
                                             start=first, stop=(j == 15)),
                                    deps=[yput[(sg, j)], wo_tok] +
                                    ([vfree[b]] if (first and vfree[b]) else []),
                                    inc=(j == 15))
                    first = False
            ptoks[sg] = k.tok("tensor")
            xn = k.op("scalar", lambda e, g=g, b=b: e.activation(
                XN[:], XT[b][:], AF.Identity, scale=c.rho[:, g:g + 1], bias=c.nmr[:, g:g + 1]),
                deps=[xl] + ([vprev] if vprev else []))
            v1 = k.op("vector", lambda e, pd=pd: e.tensor_tensor(SS[:, 0:512], pd[0][:], XN[:, 0:512],
                                                                 ALU.add), deps=[mm[0], xn])
            v2 = k.op("vector", lambda e, pd=pd: e.tensor_tensor(SS[:, 512:1024], pd[1][:],
                                                                 XN[:, 512:1024], ALU.add),
                      deps=[mm[1]])
            v3 = k.op("vector", lambda e, b=b: e.tensor_tensor(SS[:], SS[:], GTL[b][:], ALU.mult),
                      deps=[v2, gl])
            v4 = k.op("vector", lambda e, b=b: e.tensor_tensor(OT[b][:], SS[:], XT[b][:], ALU.add),
                      deps=[v3])
            vprev = v4
            vfree[b] = v2
            otok[b] = k.dma("sync", c.out_t[g * Q:(g + 1) * Q, :], OT[b][:], c.s_out, deps=[v4])


def finish(c):
    k = c.k
    if c.stage == "full" or c.mode == "twoA":
        k.op("sync", lambda e: e.nop(), deps=[c.s_out.tok()])
    k.fence([c.s_sp, c.s_out])


def _layout_inputs(inputs, n_cores=8):
    f = np.float32
    asf = lambda a: np.ascontiguousarray(np.asarray(a, dtype=f))
    x = asf(inputs["L0_attn_x"])
    w_in = asf(inputs["L0_attn_W_in"])
    conv_w = asf(inputs["L0_attn_conv_w"])
    conv_diag = np.zeros((4, 18, 128, 128), f)
    for t in range(4):
        for j in range(18):
            np.fill_diagonal(conv_diag[t, j], conv_w[j * 128:(j + 1) * 128, t])
    tri = np.triu(np.ones((128, 128), f))
    cap = np.where(tri > 0, 0.0, -1e9).astype(f)
    tricap = np.tile(cap, (1, 4))
    negd = np.zeros((16, 2048), f)
    for h in range(16):
        negd[h, h * 128:(h + 1) * 128] = -1.0
    com = dict(
        w_in=w_in,
        w_out=asf(inputs["L0_attn_W_out"]),
        lnw_col=asf(inputs["L0_attn_ln_w"]).reshape(8, 128).T.copy(),
        lnb_col=asf(inputs["L0_attn_ln_b"]).reshape(8, 128).T.copy(),
        conv_diag=conv_diag,
        convb_col=asf(inputs["L0_attn_conv_b"]).reshape(18, 128).T.copy(),
        dtb_col=asf(inputs["L0_attn_dt_bias"]).reshape(16, 1),
        alog_col=asf(inputs["L0_attn_A_log"]).reshape(16, 1),
        dhead_row=asf(inputs["L0_attn_D"]).reshape(1, 16),
        rmsw_col=asf(inputs["L0_attn_rms_w"]).reshape(16, 128).T.copy(),
        router_wT=asf(inputs["L0_ffn_router_w"]).T.copy(),
        corrb_row=asf(inputs["L0_ffn_correction_bias"]).reshape(1, 16),
        gate_w=asf(inputs["L0_ffn_gate_w"]),
        up_w=asf(inputs["L0_ffn_up_w"]),
        down_w=asf(inputs["L0_ffn_down_w"]),
        sgT=asf(inputs["L0_ffn_shared_gate_w"]).T.copy(),
        suT=asf(inputs["L0_ffn_shared_up_w"]).T.copy(),
        sdT=asf(inputs["L0_ffn_shared_down_w"]).T.copy(),
        i128=np.eye(128, dtype=f),
        fconst=np.tile(np.array([[1e-5, 1.0 + 1e-5, 1e-10, -0.5, -1.0, 1.0, 0.5, 2.0]], f),
                       (128, 1)),
        ones_row=np.ones((1, 512), f),
        triu=tri, tricap=tricap, negd=negd,
    )
    maps = []
    for core in range(n_cores):
        b, h = core // 2, core % 2
        m = dict(com)
        m["x_main"] = x[b, h * T:(h + 1) * T]
        halo = np.zeros((16, DM), f)
        if h == 1:
            halo[0:3] = x[b, T - 3:T]
        m["x_halo"] = halo
        m["mask_col"] = np.full((128, 1), 1.0 if h == 1 else 0.0, f)
        sm = np.zeros((128, 8), f)
        if h == 1:
            sm[:, core - 1] = 1.0
        m["smask"] = sm
        maps.append(m)
    return maps


def kernel(**inputs):
    maps = _layout_inputs(inputs)
    import os
    fused = os.environ.get("KMODE", "fused") == "fused"
    if fused:
        nc, es, c = build_nc(mode="fused", stage="full")
        res = run_bass_kernel_spmd(nc, maps, core_ids=list(range(8)))
        results = res.results
    else:
        ncA, esA, cA = build_nc(mode="twoA", stage="p1b")
        resA = run_bass_kernel_spmd(ncA, maps, core_ids=list(range(8)))
        mapsB = []
        for core in range(8):
            m = dict(maps[core])
            if core % 2 == 1:
                m["s_init_in"] = np.ascontiguousarray(resA.results[core - 1]["s_half_out"])
            else:
                m["s_init_in"] = np.zeros((DI, NST), np.float32)
            mapsB.append(m)
        ncB, esB, cB = build_nc(mode="twoB", stage="full")
        resB = run_bass_kernel_spmd(ncB, mapsB, core_ids=list(range(8)))
        results = resB.results
    out = np.zeros((4, 8192, 1024), np.float32)
    for core in range(8):
        b, h = core // 2, core % 2
        out[b, h * T:(h + 1) * T] = results[core]["out_t"]
    return out
